# version 02
# baseline (speedup 1.0000x reference)
import contextlib
import numpy as np

import concourse.bass as bass
import concourse.mybir as mybir
from concourse.bass_utils import run_bass_kernel_spmd

F32 = mybir.dt.float32
BF16 = mybir.dt.bfloat16
AF = mybir.ActivationFunctionType
ALU = mybir.AluOpType
AX = mybir.AxisListType

NCORES = 8
B = 16
S = 4096
DM = 1024
BPC = B // NCORES
CC = DM // 128
EPS = 1e-5

_cache = {}



class _MockIns:
    def then_inc(self, *a, **k):
        return self


_MOCK = _MockIns()


class Ticker:

    def __init__(self, name):
        self.name = name
        self.sem = None
        self.n = 0

    def reset(self):
        self.n = 0


class DmaPool:

    def __init__(self, name, nsem):
        self.name = name
        self.nsem = nsem
        self.sems = None
        self.uses = [0] * nsem
        self.i = 0

    def reset(self):
        self.uses = [0] * self.nsem
        self.i = 0

    def load(self, eng, out_ap, in_ap):
        j = self.i % self.nsem
        if eng is not None:
            if self.uses[j] > 0:
                eng.wait_ge(self.sems[j], 16 * self.uses[j])
            eng.dma_start(out=out_ap, in_=in_ap).then_inc(self.sems[j], 16)
        self.uses[j] += 1
        self.i += 1
        return ("dma", self, j, 16 * self.uses[j])

    def snapshot(self):
        return [("dma", self, j, 16 * self.uses[j]) for j in range(self.nsem)
                if self.uses[j] > 0]


class Eng:

    def __init__(self, bld, ticker, serialize):
        self.bld = bld
        self.ticker = ticker
        self.serialize = serialize
        self.eng = None

    def wait(self, *tokens):
        if self.eng is None:
            return
        need = {}
        for t in tokens:
            if t is None:
                continue
            if t[0] == "dma":
                _, pool, j, cnt = t
                key = ("dma", id(pool), j)
                sem = pool.sems[j]
            else:
                _, tk, cnt = t
                key = ("tk", id(tk))
                sem = tk.sem
            if key not in need or need[key][1] < cnt:
                need[key] = (sem, cnt)
        for sem, cnt in need.values():
            self.eng.wait_ge(sem, cnt)

    def tok(self, mark):
        v = self.bld.marks.get(mark)
        if v is None:
            assert self.bld.pass0, f"mark {mark} unresolved in pass 1"
            return None
        tk, n = v
        return ("tick", tk, n)

    def tok_opt(self, mark):
        v = self.bld.marks.get(mark)
        if v is None:
            return None
        tk, n = v
        return ("tick", tk, n)

    def op(self, name, *a, mark=None, tick=True, **k):
        if self.eng is not None:
            if self.serialize and self.ticker.n > 0:
                self.eng.wait_ge(self.ticker.sem, self.ticker.n)
            ins = getattr(self.eng, name)(*a, **k)
        else:
            ins = _MOCK
        if tick:
            if self.eng is not None:
                ins.then_inc(self.ticker.sem, 1)
            self.ticker.n += 1
            if mark is not None:
                self.bld.marks[mark] = (self.ticker, self.ticker.n)
        return ins

    def __getattr__(self, name):
        if name.startswith("_"):
            raise AttributeError(name)

        def f(*a, **k):
            return self.op(name, *a, **k)
        return f


class Build:

    def __init__(self):
        self.marks = {}
        self.tickers = {n: Ticker(n) for n in ("t", "v", "s", "g", "y")}
        self.pools = {}
        self.pass0 = True

    def pool(self, name, nsem):
        if name not in self.pools:
            self.pools[name] = DmaPool(name, nsem)
        return self.pools[name]

    def run(self, nc, es, bodies):
        return self.run_stage(nc, es, es, bodies)

    def run_stage(self, nc, es, ges, bodies):
        serialize = {"t": False, "v": True, "s": True, "g": False, "y": False}
        self.marks = {}
        bar = [("tick", tk, tk.n) for tk in self.tickers.values() if tk.n > 0]
        for p in self.pools.values():
            bar.extend(p.snapshot())
        snap_t = {k: t.n for k, t in self.tickers.items()}
        snap_p = {p.name: (list(p.uses), p.i) for p in self.pools.values()}
        self.pass0 = True
        engs = {k: Eng(self, self.tickers[k], serialize[k]) for k in self.tickers}
        for k, fn in bodies:
            fn(engs[k])
        counts0 = {k: t.n for k, t in self.tickers.items()}
        pools0 = {p.name: p.i for p in self.pools.values()}
        self.pass0 = False
        for k, t in self.tickers.items():
            t.n = snap_t[k]
            if t.sem is None:
                t.sem = ges.enter_context(nc.semaphore(f"tk_{t.name}"))
        for p in self.pools.values():
            p.uses = list(snap_p[p.name][0]) if p.name in snap_p else [0] * p.nsem
            p.i = snap_p[p.name][1] if p.name in snap_p else 0
            if p.sems is None:
                p.sems = [ges.enter_context(nc.semaphore(f"{p.name}{j}"))
                          for j in range(p.nsem)]
        blk = es.enter_context(nc.Block())
        attach = {"t": blk.tensor, "v": blk.vector, "s": blk.scalar,
                  "g": blk.gpsimd, "y": blk.sync}

        def mk(k, fn):
            def body(raw):
                engs[k].eng = raw
                engs[k].wait(*bar)
                fn(engs[k])
                engs[k].eng = None
            return body
        for k, fn in bodies:
            attach[k](mk(k, fn))
        counts1 = {k: t.n for k, t in self.tickers.items()}
        assert counts0 == counts1, (counts0, counts1)
        for p in self.pools.values():
            assert pools0[p.name] == p.i


class _NcAlias:

    def __init__(self, nc):
        self._nc = nc

    def sbuf_tensor(self, *a, **k):
        k.setdefault("allow_name_mangling", True)
        return self._nc.sbuf_tensor(*a, **k)

    def psum_tensor(self, *a, **k):
        k.setdefault("allow_name_mangling", True)
        return self._nc.psum_tensor(*a, **k)

    def __getattr__(self, name):
        return getattr(self._nc, name)



class LnExit:
    def __init__(self, nc, es, tag, dm=DM, nbuf=2):
        self.dm = dm
        self.nbuf = nbuf
        self.sum_sb = [es.enter_context(nc.sbuf_tensor(f"{tag}sum{i}", [128, dm], F32)) for i in range(nbuf)]
        self.sq_sb = es.enter_context(nc.sbuf_tensor(f"{tag}sq", [128, dm], BF16))
        self.st = es.enter_context(nc.sbuf_tensor(f"{tag}st", [128, 16], F32))
        self.tag = tag

    def v_stats(self, v, i):
        tag = self.tag
        i = i % self.nbuf if False else i
        o = (i % 2) * 8
        st = self.st
        v.reduce_sum(st[:, o + 0:o + 1], self.sum_sb[i % self.nbuf][:], axis=AX.X,
                     mark=(tag, "sum", i))
        v.tensor_scalar_mul(st[:, o + 2:o + 3], st[:, o + 0:o + 1], 1.0 / self.dm)
        v.tensor_mul(st[:, o + 3:o + 4], st[:, o + 2:o + 3], st[:, o + 2:o + 3])
        v.tensor_scalar_sub(st[:, o + 3:o + 4], st[:, o + 3:o + 4], EPS)
        v.wait(v.tok((tag, "sq", i)))
        v.scalar_tensor_tensor(st[:, o + 4:o + 5], st[:, o + 1:o + 2], 1.0 / self.dm,
                               st[:, o + 3:o + 4], ALU.mult, ALU.subtract,
                               mark=(tag, "ve", i))
        v.wait(v.tok((tag, "sqrt", i)))
        v.reciprocal(st[:, o + 6:o + 7], st[:, o + 5:o + 6])
        v.tensor_scalar(self.sum_sb[i % self.nbuf][:], self.sum_sb[i % self.nbuf][:],
                        st[:, o + 2:o + 3], st[:, o + 6:o + 7],
                        ALU.subtract, ALU.mult, mark=(tag, "nrm", i))

    def s_stats(self, s, i):
        tag = self.tag
        o = (i % 2) * 8
        s.wait(s.tok((tag, "sum", i)))
        s.activation(self.sq_sb[:], self.sum_sb[i % self.nbuf][:], AF.Square,
                     accum_out=self.st[:, o + 1:o + 2], mark=(tag, "sq", i))
        s.wait(s.tok((tag, "ve", i)))
        s.activation(self.st[:, o + 5:o + 6], self.st[:, o + 4:o + 5],
                     AF.Sqrt, mark=(tag, "sqrt", i))

    def store(self, y, pool, i, out_ap):
        y.wait(y.tok((self.tag, "nrm", i)))
        return pool.load(y.eng, out_ap, self.sum_sb[i % self.nbuf][:])



def _s2_drams(nc):
    return dict(
        x_in=nc.dram_tensor("x_in", [BPC, S, DM], F32, kind="ExternalInput").ap(),
        wrT=nc.dram_tensor("wrT", [DM, DM], F32, kind="ExternalInput").ap(),
        wkT=nc.dram_tensor("wkT", [DM, DM], F32, kind="ExternalInput").ap(),
        wvT=nc.dram_tensor("wvT", [DM, DM], F32, kind="ExternalInput").ap(),
        mur=nc.dram_tensor("mur", [128, CC], F32, kind="ExternalInput").ap(),
        muk=nc.dram_tensor("muk", [128, CC], F32, kind="ExternalInput").ap(),
        ident=nc.dram_tensor("ident", [128, 128], F32, kind="ExternalInput").ap(),
        x_out=nc.dram_tensor("x_out", [BPC, S, DM], F32, kind="ExternalOutput").ap())


def _emit_stage2(nc, ges, bld, A):
    x_in = A["x_in"]; wrT = A["wrT"]; wkT = A["wkT"]; wvT = A["wvT"]
    mur = A["mur"]; muk = A["muk"]; ident = A["ident"]; x_out = A["x_out"]

    SW = 4097
    NT = S // 128
    TGS = 512
    NTG = S // TGS
    NSUB = TGS // 128
    NSUBT = BPC * NT

    es = contextlib.ExitStack()
    with es:
        xfm = es.enter_context(nc.sbuf_tensor("xfm", [128, CC * SW], BF16))
        xtm = [es.enter_context(nc.sbuf_tensor(f"xtm{i}", [128, DM], BF16)) for i in range(2)]
        idb = es.enter_context(nc.sbuf_tensor("idb", [128, 128], BF16))
        wr_sb = es.enter_context(nc.sbuf_tensor("wr_sb", [128, CC * DM], BF16))
        wk_sb = es.enter_context(nc.sbuf_tensor("wk_sb", [128, CC * DM], BF16))
        wv_sb = es.enter_context(nc.sbuf_tensor("wv_sb", [128, CC * DM], BF16))
        mur_sb = es.enter_context(nc.sbuf_tensor("mur_sb", [128, CC], F32))
        muk_sb = es.enter_context(nc.sbuf_tensor("muk_sb", [128, CC], F32))
        xr_sb = [es.enter_context(nc.sbuf_tensor(f"xr{i}", [128, CC * TGS], BF16)) for i in range(2)]
        xk_sb = [es.enter_context(nc.sbuf_tensor(f"xk{i}", [128, CC * TGS], BF16)) for i in range(2)]
        d_sb = es.enter_context(nc.sbuf_tensor("d_sb", [128, TGS], BF16))
        r_sb = es.enter_context(nc.sbuf_tensor("r_sb", [128, NSUB * DM], BF16))
        kr_sb = es.enter_context(nc.sbuf_tensor("kr_sb", [128, TGS], BF16))
        k2_sb = [es.enter_context(nc.sbuf_tensor(f"k2{i}", [128, CC * TGS], BF16)) for i in range(2)]
        xres = [es.enter_context(nc.sbuf_tensor(f"xres{i}", [128, DM], F32)) for i in range(2)]
        ln = LnExit(nc, es, "L")
        ps_tr = [es.enter_context(nc.psum_tensor(f"ps_tr{i}", [128, 128], BF16)) for i in range(2)]
        pr = [es.enter_context(nc.psum_tensor(f"pr{i}", [128, 512], F32)) for i in range(2)]
        pk = [es.enter_context(nc.psum_tensor(f"pk{i}", [128, 512], F32)) for i in range(2)]
        pv = [es.enter_context(nc.psum_tensor(f"pv{i}", [128, 512], F32)) for i in range(2)]

        gp = bld.pool("gp", 6)
        yp = bld.pool("yp", 6)
        tok_xtile = {}
        tok_w = []
        tok_xres = {}
        tok_store = {}

        def g_body(g):
            tok_w.append(gp.load(g.eng, idb[:], ident[:, :]))
            for (w_sb, w_ap) in ((wr_sb, wrT), (wk_sb, wkT), (wv_sb, wvT)):
                for c in range(CC):
                    tok_w.append(gp.load(g.eng, w_sb[:, c * DM:(c + 1) * DM],
                                         w_ap[c * 128:(c + 1) * 128, :]))
            tok_w.append(gp.load(g.eng, mur_sb[:], mur[:, :]))
            tok_w.append(gp.load(g.eng, muk_sb[:], muk[:, :]))
            for b in range(BPC):
                for t in range(NT):
                    i = b * NT + t
                    if i >= 2:
                        g.wait(g.tok(("tr", (i - 1) * CC - 1)))
                    tok_xtile[(b, t)] = gp.load(
                        g.eng, xtm[i % 2][:], x_in[b, t * 128:(t + 1) * 128, :])

        def y_body(y):
            for b in range(BPC):
                for sg in range(NT):
                    i = b * NT + sg
                    if i >= 2:
                        y.wait(y.tok(("L", "sum", i - 2)))
                    tok_xres[i] = yp.load(
                        y.eng, xres[i % 2][:], x_in[b, sg * 128:(sg + 1) * 128, :])
                    if i >= 2:
                        j = i - 2
                        bb, ss = divmod(j, NT)
                        tok_store[j] = ln.store(y, yp, j,
                                                x_out[bb, ss * 128:(ss + 1) * 128, :])
            for j in (NSUBT - 2, NSUBT - 1):
                bb, ss = divmod(j, NT)
                tok_store[j] = ln.store(y, yp, j, x_out[bb, ss * 128:(ss + 1) * 128, :])
            if y.eng is not None:
                for t in yp.snapshot():
                    y.eng.wait_ge(t[1].sems[t[2]], t[3])

        def t_body(t):
            t.wait(tok_w[0])
            for b in range(BPC):
                for tt in range(NT):
                    i = b * NT + tt
                    t.wait(tok_xtile[(b, tt)])
                    if b > 0 and tt == 0:
                        t.wait(t.tok(("xrk", b - 1, NTG - 1)))
                    for c in range(CC):
                        j = i * CC + c
                        if j >= 2:
                            t.wait(t.tok(("trv", j - 2)))
                        t.op("transpose", ps_tr[j % 2][:],
                             xtm[i % 2][:, c * 128:(c + 1) * 128], idb[:],
                             tick=True, mark=("tr", j))
                if b == 0:
                    t.wait(*tok_w)
                for tg in range(NTG):
                    gi = b * NTG + tg
                    t.wait(t.tok(("xrk", b, tg)))
                    for sub in range(NSUB):
                        for nh in range(2):
                            rgi = gi * 8 + sub * 2 + nh
                            if rgi >= 2:
                                t.wait(t.tok(("sig", rgi - 2)))
                            for c in range(CC):
                                t.op("matmul",
                                     pr[rgi % 2][:],
                                     xr_sb[tg % 2][:, c * TGS + sub * 128: c * TGS + sub * 128 + 128],
                                     wr_sb[:, c * DM + nh * 512: c * DM + (nh + 1) * 512],
                                     start=(c == 0), stop=(c == CC - 1),
                                     tick=(c == CC - 1), mark=(("peR", rgi) if c == CC - 1 else None))
                    for kc in range(CC):
                        kgi = gi * 8 + kc
                        if kgi >= 2:
                            t.wait(t.tok(("k2", kgi - 2)))
                        for c in range(CC):
                            t.op("matmul",
                                 pk[kgi % 2][:],
                                 wk_sb[:, c * DM + kc * 128: c * DM + kc * 128 + 128],
                                 xk_sb[tg % 2][:, c * TGS:(c + 1) * TGS],
                                 start=(c == 0), stop=(c == CC - 1),
                                 tick=(c == CC - 1), mark=(("peK", kgi) if c == CC - 1 else None))
                    t.wait(t.tok(("k2", gi * 8 + 7)))
                    for sub in range(NSUB):
                        for nh in range(2):
                            vgi = gi * 8 + sub * 2 + nh
                            if vgi >= 2:
                                t.wait(t.tok(("mulV", vgi - 2)))
                            for c in range(CC):
                                t.op("matmul",
                                     pv[vgi % 2][:],
                                     k2_sb[tg % 2][:, c * TGS + sub * 128: c * TGS + sub * 128 + 128],
                                     wv_sb[:, c * DM + nh * 512: c * DM + (nh + 1) * 512],
                                     start=(c == 0), stop=(c == CC - 1),
                                     tick=(c == CC - 1), mark=(("peV", vgi) if c == CC - 1 else None))

        def s_body(s):
            for b in range(BPC):
                for tg in range(NTG):
                    gi = b * NTG + tg
                    for sub in range(NSUB):
                        for nh in range(2):
                            rgi = gi * 8 + sub * 2 + nh
                            s.wait(s.tok(("peR", rgi)))
                            if gi >= 1 and sub == 0 and nh == 0:
                                s.wait(s.tok(("mulV", gi * 8 - 1)))
                            s.activation(
                                r_sb[:, sub * DM + nh * 512: sub * DM + (nh + 1) * 512],
                                pr[rgi % 2][:], AF.Sigmoid, mark=("sig", rgi))
                    for kc in range(CC):
                        kgi = gi * 8 + kc
                        s.wait(s.tok(("peK", kgi)))
                        if gi >= 2 and kc == 0:
                            s.wait(s.tok(("peV", (gi - 2) * 8 + 7)))
                        s.activation(kr_sb[:], pk[kgi % 2][:], AF.Relu)
                        s.activation(
                            k2_sb[tg % 2][:, kc * TGS:(kc + 1) * TGS],
                            kr_sb[:], AF.Square, mark=("k2", kgi))
                    for sub in range(NSUB):
                        ln.s_stats(s, gi * NSUB + sub)

        def v_body(v):
            for c in range(CC):
                v.memset(xfm[:, c * SW:c * SW + 1], 0.0)
            for b in range(BPC):
                for tt in range(NT):
                    i = b * NT + tt
                    for c in range(CC):
                        j = i * CC + c
                        v.wait(v.tok(("tr", j)))
                        v.tensor_copy(
                            xfm[:, c * SW + 1 + tt * 128: c * SW + 1 + tt * 128 + 128],
                            ps_tr[j % 2][:], mark=("trv", j))
                for tg in range(NTG):
                    gi = b * NTG + tg
                    if gi >= 2:
                        v.wait(v.tok(("peK", (gi - 2) * 8 + 7)))
                    for c in range(CC):
                        xs = xfm[:, c * SW + 1 + tg * TGS: c * SW + 1 + (tg + 1) * TGS]
                        xp = xfm[:, c * SW + tg * TGS: c * SW + (tg + 1) * TGS]
                        v.tensor_sub(d_sb[:], xs, xp)
                        v.scalar_tensor_tensor(
                            xr_sb[tg % 2][:, c * TGS:(c + 1) * TGS], d_sb[:],
                            mur_sb[:, c:c + 1], xp, ALU.mult, ALU.add)
                        v.scalar_tensor_tensor(
                            xk_sb[tg % 2][:, c * TGS:(c + 1) * TGS], d_sb[:],
                            muk_sb[:, c:c + 1], xp, ALU.mult, ALU.add,
                            mark=(("xrk", b, tg) if c == CC - 1 else None))
                    for sub in range(NSUB):
                        i = gi * NSUB + sub
                        for nh in range(2):
                            vgi = gi * 8 + sub * 2 + nh
                            v.wait(v.tok(("peV", vgi)))
                            if nh == 0 and sub == 0:
                                v.wait(v.tok(("sig", gi * 8 + 7)))
                            if i >= 2 and nh == 0:
                                v.wait(v.tok(("L", "sq", i - 2)))
                                v.wait(tok_store.get(i - 2))
                            v.tensor_mul(
                                ln.sum_sb[i % 2][:, nh * 512:(nh + 1) * 512],
                                pv[vgi % 2][:],
                                r_sb[:, sub * DM + nh * 512: sub * DM + (nh + 1) * 512],
                                mark=("mulV", vgi))
                        v.wait(tok_xres[i])
                        if i >= 2:
                            v.wait(tok_store[i - 2])
                        v.tensor_add(ln.sum_sb[i % 2][:], ln.sum_sb[i % 2][:],
                                     xres[i % 2][:])
                        ln.v_stats(v, i)

        bld.run_stage(nc, es, ges, [("g", g_body), ("y", y_body), ("t", t_body),
                                    ("s", s_body), ("v", v_body)])


def build_stage2():
    if "s2" in _cache:
        return _cache["s2"]
    nc0 = bass.Bass("TRN2", target_bir_lowering=False, debug=False)
    nc = _NcAlias(nc0)
    ges = contextlib.ExitStack()
    with ges:
        _emit_stage2(nc, ges, Build(), _s2_drams(nc))
    _cache["s2"] = nc0
    return nc0



def _s3_drams(nc):
    return dict(
        x_in=nc.dram_tensor("x_in", [BPC, S, DM], F32, kind="ExternalInput").ap(),
        wqkvT=nc.dram_tensor("wqkvT", [DM, 3 * DM], F32, kind="ExternalInput").ap(),
        ident=nc.dram_tensor("ident", [128, 128], F32, kind="ExternalInput").ap(),
        onescol=nc.dram_tensor("onescol", [128, 1], F32, kind="ExternalInput").ap(),
        onesrow=nc.dram_tensor("onesrow", [1, 128], F32, kind="ExternalInput").ap(),
        x_out=nc.dram_tensor("x_out", [BPC, S, DM], F32, kind="ExternalOutput").ap())


def _emit_stage3(nc, ges, bld, A):
    x_in = A["x_in"]; wqkvT = A["wqkvT"]; ident = A["ident"]
    onescol = A["onescol"]; onesrow = A["onesrow"]; x_out = A["x_out"]

    NT = S // 128
    TGS = 512
    NTG = S // TGS
    NSUB = TGS // 128
    NKP = NT
    SCALE = 1.0 / 32.0
    A1G = 17
    A2G = 46

    es = contextlib.ExitStack()
    with es:
        kfm = es.enter_context(nc.sbuf_tensor("kfm", [128, CC * S], BF16))
        vtm = es.enter_context(nc.sbuf_tensor("vtm", [128, NKP * DM], BF16))
        p_sb = es.enter_context(nc.sbuf_tensor("p_sb", [128, NKP * TGS], BF16))
        qg_sb = es.enter_context(nc.sbuf_tensor("qg_sb", [128, CC * TGS], BF16))
        xg_sb = es.enter_context(nc.sbuf_tensor("xg_sb", [128, CC * TGS], BF16))
        vfm_sb = es.enter_context(nc.sbuf_tensor("vfm", [128, TGS], BF16))
        xtm = [es.enter_context(nc.sbuf_tensor(f"xtm{i}", [128, DM], BF16)) for i in range(2)]
        wsl = [es.enter_context(nc.sbuf_tensor(f"wsl{i}", [128, CC * 128], BF16)) for i in range(2)]
        idb = es.enter_context(nc.sbuf_tensor("idb", [128, 128], BF16))
        ones_c = es.enter_context(nc.sbuf_tensor("ones_c", [128, 1], BF16))
        ones_r = es.enter_context(nc.sbuf_tensor("ones_r", [1, 128], F32))
        stat = es.enter_context(nc.sbuf_tensor("stat", [1, 8], F32))
        negb_sb = es.enter_context(nc.sbuf_tensor("negb_sb", [128, 1], F32))
        recd_sb = es.enter_context(nc.sbuf_tensor("recd_sb", [128, NSUB], F32))
        xres = [es.enter_context(nc.sbuf_tensor(f"xres{i}", [128, DM], F32)) for i in range(2)]
        ln = LnExit(nc, es, "L")
        sqg = ln.sq_sb
        A = [es.enter_context(nc.psum_tensor(f"A{i}", [128, 512], F32)) for i in range(2)]
        BIG = [es.enter_context(nc.psum_tensor(f"BIG{i}", [128, DM], F32)) for i in range(2)]
        ps_tr = [es.enter_context(nc.psum_tensor(f"pstr{i}", [128, 128], BF16)) for i in range(2)]

        gp = bld.pool("gp", 6)
        wp = bld.pool("wp", 4)
        yp = bld.pool("yp", 6)
        tok_const = []
        tok_xt = {}
        tok_w = {}
        tok_xres = {}
        tok_store = {}
        TRS = []
        xlast = {}
        vlast = {}

        PERB = NTG * (A1G + A2G)

        def a1(b, tg):
            return b * PERB + tg * A1G

        def a2(b, qg):
            return b * PERB + NTG * A1G + qg * A2G

        def big2(b, qg):
            return (b * NTG + qg) * NSUB

        def wl_k(b, tg, dc):
            return (b, tg * 16 + dc)

        def wl_v(b, tg, dc):
            return (b, tg * 16 + 8 + dc)

        def wl_q(b, qg, dc):
            return (b, 128 + qg * CC + dc)

        WMARK = {}

        def t_body(t):
            t.wait(*tok_const)
            trj = [0]

            def transpose(kind, key, src_sb, src_col, dst_col, waitfor=None):
                j = trj[0]
                if t.bld.pass0:
                    TRS.append((kind, dst_col))
                if j >= 2:
                    t.wait(t.tok(("trv3", j - 2)))
                if waitfor is not None:
                    t.wait(waitfor)
                t.op("transpose", ps_tr[j % 2][:],
                     src_sb[:, src_col:src_col + 128], idb[:],
                     tick=True, mark=("tr3", j))
                if t.bld.pass0:
                    key[0][key[1]] = j
                trj[0] += 1

            if t.bld.pass0:
                TRS.clear()
            for b in range(BPC):
                for tg in range(NTG):
                    for sub in range(NSUB):
                        tokx = tok_xt.get((0, b, tg, sub))
                        for c in range(CC):
                            transpose("x", (xlast, (0, b, tg, sub)),
                                      xtm[(b * 2 * NT + tg * NSUB + sub) % 2],
                                      c * 128, c * TGS + sub * 128,
                                      waitfor=(tokx if c == 0 else None))
                    for dc in range(CC):
                        ai = a1(b, tg) + dc
                        t.wait(tok_w.get(wl_k(b, tg, dc)))
                        if ai >= 2:
                            t.wait(t.tok(("Aev", ai - 2)))
                        t.wait(t.tok(("trv3", xlast.get((0, b, tg, NSUB - 1), 0))))
                        for c in range(CC):
                            t.op("matmul", A[ai % 2][:],
                                 wsl[(tg * 16 + dc) % 2][:, c * 128:(c + 1) * 128],
                                 xg_sb[:, c * TGS:(c + 1) * TGS],
                                 start=(c == 0), stop=(c == CC - 1),
                                 tick=(c == CC - 1), mark=(("Agrp", ai) if c == CC - 1 else None))
                    for dc in range(CC):
                        ai = a1(b, tg) + 8 + dc
                        t.wait(tok_w.get(wl_v(b, tg, dc)))
                        if ai >= 2:
                            t.wait(t.tok(("Aev", ai - 2)))
                        for c in range(CC):
                            t.op("matmul", A[ai % 2][:],
                                 wsl[(tg * 16 + 8 + dc) % 2][:, c * 128:(c + 1) * 128],
                                 xg_sb[:, c * TGS:(c + 1) * TGS],
                                 start=(c == 0), stop=(c == CC - 1),
                                 tick=(c == CC - 1), mark=(("Agrp", ai) if c == CC - 1 else None))
                        for sub in range(NSUB):
                            transpose("v", (vlast, (b, tg, dc)),
                                      vfm_sb, sub * 128,
                                      (tg * NSUB + sub) * DM + dc * 128,
                                      waitfor=(t.tok(("vev", b, tg, dc)) if sub == 0 else None))
                    ai = a1(b, tg) + 16
                    if ai >= 2:
                        t.wait(t.tok(("Aev", ai - 2)))
                    for dc in range(CC):
                        t.wait(t.tok(("ksq", b, tg, dc)))
                        t.op("matmul", A[ai % 2][0:1, :], ones_c[:], sqg[:, 0:TGS],
                             start=(dc == 0), stop=(dc == CC - 1),
                             tick=True, mark=("nmm1", b, tg, dc))
                    bld.marks[("Agrp", ai)] = bld.marks[("nmm1", b, tg, CC - 1)]
                for qg in range(NTG):
                    if qg == 0:
                        t.wait(t.tok(("Aev", a1(b, NTG - 1) + 7)))
                        t.wait(t.tok(("trv3", vlast.get((b, NTG - 1, CC - 1), 0))))
                    for sub in range(NSUB):
                        tokx = tok_xt.get((1, b, qg, sub))
                        for c in range(CC):
                            transpose("x", (xlast, (1, b, qg, sub)),
                                      xtm[(b * 2 * NT + NT + qg * NSUB + sub) % 2],
                                      c * 128, c * TGS + sub * 128,
                                      waitfor=(tokx if c == 0 else None))
                    for dc in range(CC):
                        ai = a2(b, qg) + dc
                        t.wait(tok_w.get(wl_q(b, qg, dc)))
                        t.wait(t.tok(("Aev", ai - 2)))
                        t.wait(t.tok(("trv3", xlast.get((1, b, qg, NSUB - 1), 0))))
                        for c in range(CC):
                            t.op("matmul", A[ai % 2][:],
                                 wsl[(128 + qg * CC + dc) % 2][:, c * 128:(c + 1) * 128],
                                 xg_sb[:, c * TGS:(c + 1) * TGS],
                                 start=(c == 0), stop=(c == CC - 1),
                                 tick=(c == CC - 1), mark=(("Agrp", ai) if c == CC - 1 else None))
                    ai = a2(b, qg) + 8
                    t.wait(t.tok(("Aev", ai - 2)))
                    for dc in range(CC):
                        t.wait(t.tok(("qsq", b, qg, dc)))
                        t.op("matmul", A[ai % 2][0:1, :], ones_c[:], sqg[:, 0:TGS],
                             start=(dc == 0), stop=(dc == CC - 1),
                             tick=True, mark=("nmm2", b, qg, dc))
                    bld.marks[("Agrp", ai)] = bld.marks[("nmm2", b, qg, CC - 1)]
                    ai = a2(b, qg) + 9
                    t.wait(t.tok(("Aev", ai - 2)))
                    t.wait(t.tok(("negb", b, qg)))
                    t.op("matmul", A[ai % 2][:, 0:1], ones_r[:], stat[0:1, 3:4],
                         start=True, stop=True, tick=True, mark=("Agrp", ai))
                    for kc in range(NKP):
                        ai = a2(b, qg) + 10 + kc
                        t.wait(t.tok(("Aev", ai - 2)))
                        for dc in range(CC):
                            t.op("matmul", A[ai % 2][:],
                                 kfm[:, dc * S + kc * 128: dc * S + kc * 128 + 128],
                                 qg_sb[:, dc * TGS:(dc + 1) * TGS],
                                 start=(dc == 0), stop=(dc == CC - 1),
                                 tick=(dc == CC - 1), mark=(("Agrp", ai) if dc == CC - 1 else None))
                    t.wait(t.tok(("Aev", a2(b, qg) + 41)))
                    for sub in range(NSUB):
                        ai = a2(b, qg) + 42 + sub
                        t.wait(t.tok(("Aev", ai - 2)))
                        for kc in range(NKP):
                            t.op("matmul", A[ai % 2][:, 0:1],
                                 p_sb[:, kc * TGS + sub * 128: kc * TGS + sub * 128 + 128],
                                 ones_c[:],
                                 start=(kc == 0), stop=(kc == NKP - 1),
                                 tick=(kc == NKP - 1), mark=(("Agrp", ai) if kc == NKP - 1 else None))
                    for sub in range(NSUB):
                        bi = big2(b, qg) + sub
                        if bi >= 2:
                            t.wait(t.tok(("Bev", bi - 2)))
                        for kc in range(NKP):
                            for nh in range(2):
                                t.op("matmul", BIG[bi % 2][:, nh * 512:(nh + 1) * 512],
                                     p_sb[:, kc * TGS + sub * 128: kc * TGS + sub * 128 + 128],
                                     vtm[:, kc * DM + nh * 512: kc * DM + (nh + 1) * 512],
                                     start=(kc == 0), stop=(kc == NKP - 1),
                                     tick=(kc == NKP - 1 and nh == 1),
                                     mark=(("Bgrp", bi) if (kc == NKP - 1 and nh == 1) else None))

        def s_body(s):
            for b in range(BPC):
                for tg in range(NTG):
                    for dc in range(CC):
                        ai = a1(b, tg) + dc
                        s.wait(s.tok(("Agrp", ai)))
                        if (b, tg) != (0, 0) or dc >= 1:
                            pass
                        s.copy(kfm[:, dc * S + tg * TGS: dc * S + (tg + 1) * TGS],
                               A[ai % 2][:], mark=("Aev", ai))
                    for dc in range(CC):
                        ai = a1(b, tg) + 8 + dc
                        s.wait(s.tok(("Agrp", ai)))
                        if dc >= 1:
                            m = vlast.get((b, tg, dc - 1))
                            s.wait(s.tok(("tr3", m)) if m is not None else None)
                        s.copy(vfm_sb[:], A[ai % 2][:], mark=("Aev", ai))
                        bld.marks[("vev", b, tg, dc)] = bld.marks[("Aev", ai)]
                    for dc in range(CC):
                        if dc > 0:
                            s.wait(s.tok(("nmm1", b, tg, dc - 1)))
                        s.activation(sqg[:, 0:TGS],
                                     kfm[:, dc * S + tg * TGS: dc * S + (tg + 1) * TGS],
                                     AF.Square, mark=("ksq", b, tg, dc))
                for qg in range(NTG):
                    for dc in range(CC):
                        ai = a2(b, qg) + dc
                        s.wait(s.tok(("Agrp", ai)))
                        s.copy(qg_sb[:, dc * TGS:(dc + 1) * TGS],
                               A[ai % 2][:], mark=("Aev", ai))
                    for dc in range(CC):
                        if dc > 0:
                            s.wait(s.tok(("nmm2", b, qg, dc - 1)))
                        s.activation(sqg[:, 0:TGS],
                                     qg_sb[:, dc * TGS:(dc + 1) * TGS],
                                     AF.Square, mark=("qsq", b, qg, dc))
                    s.wait(s.tok(("b2", b, qg)))
                    s.activation(stat[0:1, 2:3], stat[0:1, 2:3], AF.Sqrt,
                                 mark=("sqrtb", b, qg))
                    for kc in range(NKP):
                        ai = a2(b, qg) + 10 + kc
                        s.wait(s.tok(("Agrp", ai)))
                        s.wait(s.tok(("negbev", b, qg)))
                        if (b, qg) != (0, 0) and kc == 0:
                            s.wait(s.tok(("Bgrp", big2(b, qg) - 1)))
                        s.activation(p_sb[:, kc * TGS:(kc + 1) * TGS],
                                     A[ai % 2][:], AF.Exp, bias=negb_sb[:, 0:1],
                                     scale=SCALE, mark=("Aev", ai))
                    for sub in range(NSUB):
                        ln.s_stats(s, b * NT + qg * NSUB + sub)

        def v_body2(v):
            trv = [0]

            def evacs(n):
                for _ in range(n):
                    j = trv[0]
                    kind, dst_col = TRS[j] if TRS else ("x", 0)
                    v.wait(v.tok(("tr3", j)))
                    dst = xg_sb if kind == "x" else vtm
                    v.tensor_copy(dst[:, dst_col:dst_col + 128], ps_tr[j % 2][:],
                                  mark=("trv3", j))
                    trv[0] += 1
            for b in range(BPC):
                for tg in range(NTG):
                    evacs(32)
                    ai = a1(b, tg) + 16
                    for dc in range(CC):
                        evacs(4)
                    v.wait(v.tok(("nmm1", b, tg, CC - 1)))
                    v.reduce_max(stat[0:1, 4:5], A[ai % 2][0:1, :], axis=AX.X,
                                 mark=("Aev", ai))
                    if tg == 0 and b == 0:
                        v.tensor_copy(stat[0:1, 0:1], stat[0:1, 4:5])
                    elif tg == 0:
                        v.tensor_copy(stat[0:1, 0:1], stat[0:1, 4:5])
                    else:
                        v.tensor_max(stat[0:1, 0:1], stat[0:1, 0:1], stat[0:1, 4:5])
                for qg in range(NTG):
                    evacs(32)
                    ai = a2(b, qg) + 8
                    v.wait(v.tok(("nmm2", b, qg, CC - 1)))
                    v.reduce_max(stat[0:1, 1:2], A[ai % 2][0:1, :], axis=AX.X,
                                 mark=("Aev", ai))
                    v.tensor_mul(stat[0:1, 2:3], stat[0:1, 0:1], stat[0:1, 1:2],
                                 mark=("b2", b, qg))
                    v.wait(v.tok(("sqrtb", b, qg)))
                    v.tensor_scalar_mul(stat[0:1, 3:4], stat[0:1, 2:3], -SCALE,
                                        mark=("negb", b, qg))
                    nbai = a2(b, qg) + 9
                    v.wait(v.tok(("Agrp", nbai)))
                    v.tensor_copy(negb_sb[:], A[nbai % 2][:, 0:1],
                                  mark=("negbev", b, qg))
                    bld.marks[("Aev", nbai)] = bld.marks[("negbev", b, qg)]
                    for sub in range(NSUB):
                        ai = a2(b, qg) + 42 + sub
                        v.wait(v.tok(("Agrp", ai)))
                        v.reciprocal(recd_sb[:, sub:sub + 1], A[ai % 2][:, 0:1],
                                     mark=("Aev", ai))
                    for sub in range(NSUB):
                        i = b * NT + qg * NSUB + sub
                        bi = big2(b, qg) + sub
                        v.wait(v.tok(("Bgrp", bi)))
                        v.wait(tok_xres.get(i))
                        if i >= 2:
                            v.wait(v.tok(("L", "sq", i - 2)))
                            v.wait(tok_store.get(i - 2))
                        v.scalar_tensor_tensor(
                            ln.sum_sb[i % 2][:], BIG[bi % 2][:],
                            recd_sb[:, sub:sub + 1], xres[i % 2][:],
                            ALU.mult, ALU.add, mark=("Bev", bi))
                        ln.v_stats(v, i)

        def g_body(g):
            tok_const.append(gp.load(g.eng, idb[:], ident[:, :]))
            tok_const.append(gp.load(g.eng, ones_c[:], onescol[:, :]))
            tok_const.append(gp.load(g.eng, ones_r[:], onesrow[:, :]))
            nx = [0]

            def xload(ph, b, gg, sub):
                if nx[0] >= 2:
                    k = None
                    pb, pph, pgg, psub = xorder[nx[0] - 2]
                    m = xlast.get((pph, pb, pgg, psub))
                    if m is not None:
                        g.wait(g.tok(("tr3", m)))
                tok_xt[(ph, b, gg, sub)] = gp.load(
                    g.eng, xtm[nx[0] % 2][:],
                    x_in[b, (gg * NSUB + sub) * 128:(gg * NSUB + sub + 1) * 128, :])
                nx[0] += 1

            def wload(key, li_b, src_ap):
                b, li = key
                if li >= 2:
                    m = WMARK.get((b, li - 2))
                    if m is not None:
                        g.wait(g.tok(m))
                tok_w[key] = wp.load(
                    g.eng, wsl[li % 2].ap().rearrange("p (c f) -> p c f", c=CC),
                    src_ap)

            xorder = []
            for b in range(BPC):
                for ph in range(2):
                    for gg in range(NTG):
                        for sub in range(NSUB):
                            xorder.append((b, ph, gg, sub))
            xi = [0]

            def xnext():
                b, ph, gg, sub = xorder[xi[0]]
                xi[0] += 1
                xload(ph, b, gg, sub)

            for b in range(BPC):
                for tg in range(NTG):
                    for _ in range(NSUB):
                        xnext()
                    for dc in range(CC):
                        key = wl_k(b, tg, dc)
                        WMARK[key] = ("Agrp", a1(b, tg) + dc)
                        wload(key, None,
                              wqkvT[:, DM + dc * 128: DM + (dc + 1) * 128]
                              .rearrange("(c p) f -> p c f", p=128))
                    for dc in range(CC):
                        key = wl_v(b, tg, dc)
                        WMARK[key] = ("Agrp", a1(b, tg) + 8 + dc)
                        wload(key, None,
                              wqkvT[:, 2 * DM + dc * 128: 2 * DM + (dc + 1) * 128]
                              .rearrange("(c p) f -> p c f", p=128))
                for qg in range(NTG):
                    for _ in range(NSUB):
                        xnext()
                    for dc in range(CC):
                        key = wl_q(b, qg, dc)
                        WMARK[key] = ("Agrp", a2(b, qg) + dc)
                        wload(key, None,
                              wqkvT[:, dc * 128:(dc + 1) * 128]
                              .rearrange("(c p) f -> p c f", p=128))

        def y_body(y):
            NST = BPC * NT
            for b in range(BPC):
                for i_loc in range(NT):
                    i = b * NT + i_loc
                    if i >= 2:
                        y.wait(y.tok(("L", "sum", i - 2)))
                    tok_xres[i] = yp.load(
                        y.eng, xres[i % 2][:],
                        x_in[b, i_loc * 128:(i_loc + 1) * 128, :])
                    if i >= 2:
                        j = i - 2
                        bb, ss = divmod(j, NT)
                        tok_store[j] = ln.store(y, yp, j,
                                                x_out[bb, ss * 128:(ss + 1) * 128, :])
            for j in (NST - 2, NST - 1):
                bb, ss = divmod(j, NT)
                tok_store[j] = ln.store(y, yp, j, x_out[bb, ss * 128:(ss + 1) * 128, :])
            if y.eng is not None:
                for tkn in yp.snapshot():
                    y.eng.wait_ge(tkn[1].sems[tkn[2]], tkn[3])

        bld.run_stage(nc, es, ges, [("t", t_body), ("s", s_body),
                                    ("v", v_body2), ("g", g_body),
                                    ("y", y_body)])


def build_stage3():
    if "s3" in _cache:
        return _cache["s3"]
    nc0 = bass.Bass("TRN2", target_bir_lowering=False, debug=False)
    nc = _NcAlias(nc0)
    ges = contextlib.ExitStack()
    with ges:
        _emit_stage3(nc, ges, Build(), _s3_drams(nc))
    _cache["s3"] = nc0
    return nc0



def _s4_drams(nc):
    return dict(
        x_in=nc.dram_tensor("x_in", [S, DM], F32, kind="ExternalInput").ap(),
        wcat1=nc.dram_tensor("wcat1", [17, DM, 2048], F32, kind="ExternalInput").ap(),
        wcat2=nc.dram_tensor("wcat2", [17, 2048, DM], F32, kind="ExternalInput").ap(),
        b1cat=nc.dram_tensor("b1cat", [128, 17 * 16], F32, kind="ExternalInput").ap(),
        sb2b=nc.dram_tensor("sb2b", [128, DM], F32, kind="ExternalInput").ap(),
        rb2=nc.dram_tensor("rb2", [15, DM], F32, kind="ExternalInput").ap(),
        gw=nc.dram_tensor("gw", [DM, 15], F32, kind="ExternalInput").ap(),
        gbb=nc.dram_tensor("gbb", [128, 15], F32, kind="ExternalInput").ap(),
        ident=nc.dram_tensor("ident", [128, 128], F32, kind="ExternalInput").ap(),
        x_out=nc.dram_tensor("x_out", [S, DM], F32, kind="ExternalOutput").ap())


def _emit_stage4(nc, ges, bld, A):
    x_in = A["x_in"]; wcat1 = A["wcat1"]; wcat2 = A["wcat2"]
    b1cat = A["b1cat"]; sb2b = A["sb2b"]; rb2 = A["rb2"]; gw = A["gw"]
    gbb = A["gbb"]; ident = A["ident"]; x_out = A["x_out"]

    BLKT = 1024
    NBLK = S // BLKT
    NSUB = BLKT // 128
    NE = 17
    FC = 2048 // 128
    NTSUB = S // 128

    es = contextlib.ExitStack()
    with es:
        xfm = es.enter_context(nc.sbuf_tensor("xfm", [128, CC * BLKT], BF16))
        xtm = [es.enter_context(nc.sbuf_tensor(f"xtm{i}", [128, DM], BF16)) for i in range(2)]
        acc = es.enter_context(nc.sbuf_tensor("acc", [128, NSUB * DM], F32))
        h_sb = es.enter_context(nc.sbuf_tensor("h_sb", [128, FC * BLKT], BF16))
        w2_sb = es.enter_context(nc.sbuf_tensor("w2_sb", [128, FC * DM], BF16))
        wsl = [es.enter_context(nc.sbuf_tensor(f"wsl{i}", [128, CC * 128], BF16)) for i in range(2)]
        gated = es.enter_context(nc.sbuf_tensor("gated", [128, NSUB * 15], F32))
        gtb = es.enter_context(nc.sbuf_tensor("gtb", [15, BLKT], BF16))
        wso = es.enter_context(nc.sbuf_tensor("wso", [128, 15], F32))
        wwk = es.enter_context(nc.sbuf_tensor("wwk", [128, 15], F32))
        wmk = es.enter_context(nc.sbuf_tensor("wmk", [128, 15], F32))
        gst = es.enter_context(nc.sbuf_tensor("gst", [128, 4], F32))
        gw_sb = es.enter_context(nc.sbuf_tensor("gw_sb", [128, CC * 15], BF16))
        rb2_sb = es.enter_context(nc.sbuf_tensor("rb2_sb", [15, DM], BF16))
        sb2_sb = es.enter_context(nc.sbuf_tensor("sb2_sb", [128, DM], BF16))
        b1_sb = es.enter_context(nc.sbuf_tensor("b1_sb", [128, 17 * 16], F32))
        gb_sb = es.enter_context(nc.sbuf_tensor("gb_sb", [128, 15], F32))
        idb = es.enter_context(nc.sbuf_tensor("idb", [128, 128], BF16))
        idf = es.enter_context(nc.sbuf_tensor("idf", [128, 128], F32))
        xres = [es.enter_context(nc.sbuf_tensor(f"xres{i}", [128, DM], F32)) for i in range(2)]
        ln = LnExit(nc, es, "L")
        A = [es.enter_context(nc.psum_tensor(f"A{i}", [128, 512], F32)) for i in range(2)]
        BIG = [es.enter_context(nc.psum_tensor(f"BIG{i}", [128, DM], F32)) for i in range(2)]
        ps_tr = [es.enter_context(nc.psum_tensor(f"pstr{i}", [128, 128], BF16)) for i in range(2)]

        gp = bld.pool("gp", 6)
        wp = bld.pool("wp", 6)
        yp = bld.pool("yp", 6)
        tok_const = []
        tok_xt = {}
        tok_w1 = {}
        tok_w2 = {}
        tok_xres = {}
        tok_store = {}

        APB = 16 + 32 * NE

        def ag_log(blk, sub):
            return blk * APB + sub

        def ag_gt(blk, sub):
            return blk * APB + 8 + sub

        def ag_h(blk, e, g):
            return blk * APB + 16 + e * 32 + g

        BPB = (NE + 1) * NSUB

        def bg(blk, e, sub):
            return blk * BPB + e * NSUB + sub

        def t_body(t):
            t.wait(*tok_const)
            trj = [0]
            for blk in range(NBLK):
                for sub in range(NSUB):
                    nx = blk * NSUB + sub
                    t.wait(tok_xt.get((blk, sub)))
                    for c in range(CC):
                        j = trj[0]
                        if j >= 2:
                            t.wait(t.tok(("trv4", j - 2)))
                        t.op("transpose", ps_tr[j % 2][:],
                             xtm[nx % 2][:, c * 128:(c + 1) * 128], idb[:],
                             tick=True, mark=("tr4", j))
                        trj[0] += 1
                for sub in range(NSUB):
                    ai = ag_log(blk, sub)
                    if ai >= 2:
                        t.wait(t.tok(("Aev", ai - 2)))
                    t.wait(t.tok(("trv4", blk * NSUB * CC + sub * CC + CC - 1)))
                    for c in range(CC):
                        t.op("matmul", A[ai % 2][0:128, 0:15],
                             xfm[:, c * BLKT + sub * 128: c * BLKT + sub * 128 + 128],
                             gw_sb[:, c * 15:(c + 1) * 15],
                             start=(c == 0), stop=(c == CC - 1),
                             tick=(c == CC - 1), mark=(("Agrp", ai) if c == CC - 1 else None))
                for sub in range(NSUB):
                    ai = ag_gt(blk, sub)
                    t.wait(t.tok(("Aev", ai - 2)))
                    t.wait(t.tok(("gtm", blk, sub)))
                    t.op("transpose", A[ai % 2][0:15, 0:128],
                         gated[:, sub * 15:(sub + 1) * 15], idf[:],
                         tick=True, mark=("Agrp", ai))
                for e in range(NE):
                    for fc in range(FC):
                        li = e * FC + fc
                        t.wait(tok_w1.get((blk, li)))
                        for th in range(2):
                            ai = ag_h(blk, e, fc * 2 + th)
                            t.wait(t.tok(("Aev", ai - 2)))
                            for c in range(CC):
                                t.op("matmul", A[ai % 2][:],
                                     wsl[li % 2][:, c * 128:(c + 1) * 128],
                                     xfm[:, c * BLKT + th * 512: c * BLKT + (th + 1) * 512],
                                     start=(c == 0), stop=(c == CC - 1),
                                     tick=(c == CC - 1), mark=(("Agrp", ai) if c == CC - 1 else None))
                    for sub in range(NSUB):
                        bi = bg(blk, e, sub)
                        if bi >= 2:
                            t.wait(t.tok(("Bev", bi - 2)))
                        if sub == 0:
                            for fc in range(FC):
                                t.wait(tok_w2.get((blk, e, fc)))
                            t.wait(t.tok(("hev", blk, e, FC * 2 - 1)))
                        for fc in range(FC):
                            for nh in range(2):
                                t.op("matmul", BIG[bi % 2][:, nh * 512:(nh + 1) * 512],
                                     h_sb[:, fc * BLKT + sub * 128: fc * BLKT + sub * 128 + 128],
                                     w2_sb[:, fc * DM + nh * 512: fc * DM + (nh + 1) * 512],
                                     start=(fc == 0), stop=(fc == FC - 1),
                                     tick=(fc == FC - 1 and nh == 1),
                                     mark=(("Bgrp", bi) if (fc == FC - 1 and nh == 1) else None))
                for sub in range(NSUB):
                    bi = bg(blk, NE, sub)
                    t.wait(t.tok(("Bev", bi - 2)))
                    if sub == 0:
                        t.wait(t.tok(("gtev", blk, NSUB - 1)))
                    for nh in range(2):
                        t.op("matmul", BIG[bi % 2][:, nh * 512:(nh + 1) * 512],
                             gtb[:, sub * 128:(sub + 1) * 128],
                             rb2_sb[:, nh * 512:(nh + 1) * 512],
                             start=True, stop=True,
                             tick=(nh == 1), mark=(("Bgrp", bi) if nh == 1 else None))

        def s_body(s):
            for blk in range(NBLK):
                for sub in range(NSUB):
                    s.wait(s.tok(("sfte", blk, sub)))
                    s.activation(wmk[:], wwk[:], AF.Exp,
                                 accum_out=gst[:, 1:2], mark=("sftx", blk, sub))
                for e in range(NE):
                    for fc in range(FC):
                        for th in range(2):
                            g = fc * 2 + th
                            ai = ag_h(blk, e, g)
                            s.wait(s.tok(("Agrp", ai)))
                            if g == 0 and (blk > 0 or e > 0):
                                pe = (blk, e - 1) if e > 0 else (blk - 1, NE - 1)
                                s.wait(s.tok(("Bgrp", bg(pe[0], pe[1], NSUB - 1))))
                            s.activation(
                                h_sb[:, fc * BLKT + th * 512: fc * BLKT + (th + 1) * 512],
                                A[ai % 2][:], AF.Gelu,
                                bias=b1_sb[:, e * 16 + fc: e * 16 + fc + 1],
                                mark=("Aev", ai))
                            bld.marks[("hev", blk, e, g)] = bld.marks[("Aev", ai)]
                for sub in range(NSUB):
                    ln.s_stats(s, blk * NSUB + sub)

        def v_body(v):
            TRV = []
            for blk in range(NBLK):
                for sub in range(NSUB):
                    for c in range(CC):
                        j = blk * NSUB * CC + sub * CC + c
                        v.wait(v.tok(("tr4", j)))
                        v.tensor_copy(
                            xfm[:, c * BLKT + sub * 128: c * BLKT + sub * 128 + 128],
                            ps_tr[j % 2][:], mark=("trv4", j))
                for sub in range(NSUB):
                    ai = ag_log(blk, sub)
                    v.wait(v.tok(("Agrp", ai)))
                    v.tensor_add(wso[:], A[ai % 2][0:128, 0:15], gb_sb[:],
                                 mark=("Aev", ai))
                    v.reduce_max(gst[:, 0:1], wso[:], axis=AX.X)
                    v.tensor_scalar(wwk[:], wso[:], gst[:, 0:1], None, ALU.subtract,
                                    mark=("sfte", blk, sub))
                    v.wait(v.tok(("sftx", blk, sub)))
                    v.reciprocal(gst[:, 2:3], gst[:, 1:2])
                    v.tensor_scalar(wso[:], wmk[:], gst[:, 2:3], None, ALU.mult)
                    v.tensor_copy(wwk[:], wso[:])
                    for r in range(3):
                        v.reduce_max(gst[:, 3:4], wwk[:], axis=AX.X)
                        v.tensor_scalar(wmk[:], wwk[:], gst[:, 3:4], None, ALU.is_ge)
                        if r < 2:
                            v.scalar_tensor_tensor(wwk[:], wmk[:], -1e9, wwk[:],
                                                   ALU.mult, ALU.add)
                    v.tensor_scalar(wmk[:], wso[:], gst[:, 3:4], None, ALU.is_ge)
                    v.tensor_mul(gated[:, sub * 15:(sub + 1) * 15], wmk[:], wso[:],
                                 mark=("gtm", blk, sub))
                for sub in range(NSUB):
                    ai = ag_gt(blk, sub)
                    v.wait(v.tok(("Agrp", ai)))
                    v.tensor_copy(gtb[:, sub * 128:(sub + 1) * 128],
                                  A[ai % 2][0:15, 0:128], mark=("Aev", ai))
                    bld.marks[("gtev", blk, sub)] = bld.marks[("Aev", ai)]
                for e in range(NE):
                    for sub in range(NSUB):
                        bi = bg(blk, e, sub)
                        v.wait(v.tok(("Bgrp", bi)))
                        a_sl = acc[:, sub * DM:(sub + 1) * DM]
                        if e == 0:
                            v.tensor_add(a_sl, BIG[bi % 2][:], sb2_sb[:],
                                         mark=("Bev", bi))
                        elif e == 1:
                            v.scalar_tensor_tensor(a_sl, BIG[bi % 2][:], 1.0,
                                                   a_sl, ALU.mult, ALU.add,
                                                   mark=("Bev", bi))
                        else:
                            v.scalar_tensor_tensor(
                                a_sl, BIG[bi % 2][:],
                                gated[:, sub * 15 + (e - 2): sub * 15 + (e - 1)],
                                a_sl, ALU.mult, ALU.add, mark=("Bev", bi))
                for sub in range(NSUB):
                    bi = bg(blk, NE, sub)
                    i = blk * NSUB + sub
                    v.wait(v.tok(("Bgrp", bi)))
                    a_sl = acc[:, sub * DM:(sub + 1) * DM]
                    v.scalar_tensor_tensor(a_sl, BIG[bi % 2][:], 1.0, a_sl,
                                           ALU.mult, ALU.add, mark=("Bev", bi))
                    v.wait(tok_xres.get(i))
                    if i >= 2:
                        v.wait(v.tok(("L", "sq", i - 2)))
                        v.wait(tok_store.get(i - 2))
                    v.scalar_tensor_tensor(ln.sum_sb[i % 2][:], xres[i % 2][:],
                                           2.0, a_sl, ALU.mult, ALU.add)
                    ln.v_stats(v, i)

        def g_body(g):
            tok_const.append(gp.load(g.eng, idb[:], ident[:, :]))
            for c in range(CC):
                tok_const.append(gp.load(g.eng, gw_sb[:, c * 15:(c + 1) * 15],
                                         gw[c * 128:(c + 1) * 128, :]))
            tok_const.append(gp.load(g.eng, rb2_sb[:], rb2[:, :]))
            tok_const.append(gp.load(g.eng, sb2_sb[:], sb2b[:, :]))
            for blk in range(NBLK):
                for sub in range(NSUB):
                    nx = blk * NSUB + sub
                    if nx >= 2:
                        g.wait(g.tok(("tr4", (nx - 1) * CC - 1)))
                    tok_xt[(blk, sub)] = gp.load(
                        g.eng, xtm[nx % 2][:],
                        x_in[(blk * NSUB + sub) * 128:(blk * NSUB + sub + 1) * 128, :])
                for e in range(NE):
                    for fc in range(FC):
                        li = e * FC + fc
                        gli = blk * NE * FC + li
                        if gli >= 2:
                            pblk, pli = divmod(gli - 2, NE * FC)
                            pe_, pfc = divmod(pli, FC)
                            m = ("Agrp", ag_h(pblk, pe_, pfc * 2 + 1))
                            g.wait(g.tok(m))
                        tok_w1[(blk, li)] = wp.load(
                            g.eng, wsl[li % 2].ap().rearrange("p (c f) -> p c f", c=CC),
                            wcat1[e, :, fc * 128:(fc + 1) * 128]
                            .rearrange("(c p) f -> p c f", p=128))
                    for fc in range(FC):
                        if blk > 0 or e > 0:
                            pe = (blk, e - 1) if e > 0 else (blk - 1, NE - 1)
                            g.wait(g.tok(("Bgrp", bg(pe[0], pe[1], NSUB - 1))))
                        tok_w2[(blk, e, fc)] = wp.load(
                            g.eng, w2_sb[:, fc * DM:(fc + 1) * DM],
                            wcat2[e, fc * 128:(fc + 1) * 128, :])

        def y_body(y):
            for i in range(NTSUB):
                if i >= 2:
                    y.wait(y.tok(("L", "sum", i - 2)))
                tok_xres[i] = yp.load(
                    y.eng, xres[i % 2][:], x_in[i * 128:(i + 1) * 128, :])
                if i >= 2:
                    j = i - 2
                    tok_store[j] = ln.store(y, yp, j,
                                            x_out[j * 128:(j + 1) * 128, :])
            for j in (NTSUB - 2, NTSUB - 1):
                tok_store[j] = ln.store(y, yp, j, x_out[j * 128:(j + 1) * 128, :])
            if y.eng is not None:
                for tkn in yp.snapshot():
                    y.eng.wait_ge(tkn[1].sems[tkn[2]], tkn[3])
        def y0_body(y):
            tok_const.append(yp.load(y.eng, idf[:], ident[:, :]))
            tok_const.append(yp.load(y.eng, b1_sb[:], b1cat[:, :]))
            tok_const.append(yp.load(y.eng, gb_sb[:], gbb[:, :]))

        bld.run_stage(nc, es, ges, [("t", t_body), ("s", s_body),
                                    ("v", v_body), ("g", g_body),
                                    ("y", y0_body), ("y", y_body)])


def build_stage4():
    if "s4" in _cache:
        return _cache["s4"]
    nc0 = bass.Bass("TRN2", target_bir_lowering=False, debug=False)
    nc = _NcAlias(nc0)
    ges = contextlib.ExitStack()
    with ges:
        _emit_stage4(nc, ges, Build(), _s4_drams(nc))
    _cache["s4"] = nc0
    return nc0



def _s1_drams(nc):
    return dict(
        x_in=nc.dram_tensor("x_in", [S, DM], F32, kind="ExternalInput").ap(),
        wqT=nc.dram_tensor("wqT", [DM, DM], F32, kind="ExternalInput").ap(),
        wkT=nc.dram_tensor("wkT", [DM, DM], F32, kind="ExternalInput").ap(),
        wvT=nc.dram_tensor("wvT", [DM, DM], F32, kind="ExternalInput").ap(),
        wo=nc.dram_tensor("wo", [DM, DM], F32, kind="ExternalInput").ap(),
        wck=nc.dram_tensor("wck", [128, 32 * 64], F32, kind="ExternalInput").ap(),
        wcv=nc.dram_tensor("wcv", [128, 32 * 64], F32, kind="ExternalInput").ap(),
        bck=nc.dram_tensor("bck", [64, 1], F32, kind="ExternalInput").ap(),
        bcv=nc.dram_tensor("bcv", [64, 1], F32, kind="ExternalInput").ap(),
        wgm=nc.dram_tensor("wgm", [128, 3], F32, kind="ExternalInput").ap(),
        bgm=nc.dram_tensor("bgm", [1, 3], F32, kind="ExternalInput").ap(),
        amat=nc.dram_tensor("amat", [2, 128, S], F32, kind="ExternalInput").ap(),
        ident=nc.dram_tensor("ident", [128, 128], F32, kind="ExternalInput").ap(),
        onescol=nc.dram_tensor("onescol", [128, 1], F32, kind="ExternalInput").ap(),
        onesrow=nc.dram_tensor("onesrow", [1, 128], F32, kind="ExternalInput").ap(),
        x_out=nc.dram_tensor("x_out", [S, DM], F32, kind="ExternalOutput").ap())


def _emit_stage1(nc, ges, bld, A):
    x_in = A["x_in"]; wqT = A["wqT"]; wkT = A["wkT"]; wvT = A["wvT"]
    wo = A["wo"]; wck = A["wck"]; wcv = A["wcv"]; bck = A["bck"]
    bcv = A["bcv"]; wgm = A["wgm"]; bgm = A["bgm"]; amat = A["amat"]
    ident = A["ident"]; onescol = A["onescol"]; onesrow = A["onesrow"]
    x_out = A["x_out"]

    NT = S // 128
    TGS = 512
    NTG = S // TGS
    NSUB = TGS // 128
    NH = 16
    HP = 8
    NB = 255
    NKC = 32
    SC = 0.125

    es = contextlib.ExitStack()
    with es:
        kfm = es.enter_context(nc.sbuf_tensor("kfm", [128, CC * S], BF16))
        vtm = es.enter_context(nc.sbuf_tensor("vtm", [128, NKC * DM], BF16))
        mbc = es.enter_context(nc.sbuf_tensor("mbc", [128, S], BF16))
        qg_sb = es.enter_context(nc.sbuf_tensor("qg_sb", [128, CC * TGS], BF16))
        xg_sb = es.enter_context(nc.sbuf_tensor("xg_sb", [128, CC * TGS], BF16))
        praw = es.enter_context(nc.sbuf_tensor("praw", [128, S], BF16))
        pm_tail = es.enter_context(nc.sbuf_tensor("pm_tail", [128, 512], BF16))
        ptr_sb = [es.enter_context(nc.sbuf_tensor(f"ptr{i}", [128, 128], BF16)) for i in range(2)]
        oq_sb = es.enter_context(nc.sbuf_tensor("oq_sb", [128, NH * 64], BF16))
        ofm = oq_sb
        xtm = [es.enter_context(nc.sbuf_tensor(f"xtm{i}", [128, DM], BF16)) for i in range(2)]
        wsl = [es.enter_context(nc.sbuf_tensor(f"wsl{i}", [128, CC * 128], BF16)) for i in range(2)]
        wosl = [es.enter_context(nc.sbuf_tensor(f"wosl{i}", [128, 512], BF16)) for i in range(2)]
        wckb = es.enter_context(nc.sbuf_tensor("wckb", [128, 32 * 64], BF16))
        wcvb = es.enter_context(nc.sbuf_tensor("wcvb", [128, 32 * 64], BF16))
        bck_sb = es.enter_context(nc.sbuf_tensor("bck_sb", [64, 1], F32))
        bcv_sb = es.enter_context(nc.sbuf_tensor("bcv_sb", [64, 1], F32))
        wg_sb = es.enter_context(nc.sbuf_tensor("wg_sb", [128, 3], BF16))
        bg_sb = es.enter_context(nc.sbuf_tensor("bg_sb", [1, 3], F32))
        vwork = es.enter_context(nc.sbuf_tensor("vwork", [128, TGS + 16], BF16))
        vtail = es.enter_context(nc.sbuf_tensor("vtail", [128, HP * 16], BF16))
        ck_sb = es.enter_context(nc.sbuf_tensor("ck_sb", [128, HP * NB], BF16))
        cv_sb = es.enter_context(nc.sbuf_tensor("cv_sb", [128, HP * NB], BF16))
        pc_sb = None
        crow = None
        cvbm = es.enter_context(nc.sbuf_tensor("cvbm", [128, NH * 128], BF16))
        idb = es.enter_context(nc.sbuf_tensor("idb", [128, 128], BF16))
        ones_c = es.enter_context(nc.sbuf_tensor("ones_c", [128, 1], BF16))
        ones_rb = es.enter_context(nc.sbuf_tensor("ones_rb", [1, 128], BF16))
        imp_sb = es.enter_context(nc.sbuf_tensor("imp_sb", [128, 2], F32))
        impb = es.enter_context(nc.sbuf_tensor("impb", [128, 2], BF16))
        impc = es.enter_context(nc.sbuf_tensor("impc", [128, 2], F32))
        impc2 = es.enter_context(nc.sbuf_tensor("impc2", [128, 2], F32))
        implo = es.enter_context(nc.sbuf_tensor("implo", [128, 2], BF16))
        ind_sb = es.enter_context(nc.sbuf_tensor("ind_sb", [128, 2], BF16))
        rank_sb = es.enter_context(nc.sbuf_tensor("rank_sb", [128, 2], F32))
        qm_sb = es.enter_context(nc.sbuf_tensor("qm_sb", [128, HP], F32))
        qmb = es.enter_context(nc.sbuf_tensor("qmb", [128, HP], BF16))
        qmt = es.enter_context(nc.sbuf_tensor("qmt", [128, 1], F32))
        gates = es.enter_context(nc.sbuf_tensor("gates", [1, 64], BF16))
        gcol = es.enter_context(nc.sbuf_tensor("gcol", [128, 64], F32))
        dst = es.enter_context(nc.sbuf_tensor("dst", [128, 16], F32))
        brwb = es.enter_context(nc.sbuf_tensor("brwb", [1, TGS], BF16))
        imp_row = brwb
        stat1 = es.enter_context(nc.sbuf_tensor("stat1", [1, 8], F32))
        maxk = es.enter_context(nc.sbuf_tensor("maxk", [1, 16], F32))
        nbcol = es.enter_context(nc.sbuf_tensor("nbcol", [128, 64], BF16))
        xres = es.enter_context(nc.sbuf_tensor("xres", [128, DM], F32))
        ln = LnExit(nc, es, "L", nbuf=1)
        sq_s = ln.sq_sb
        pc_sb = cv_sb
        crow = cv_sb.ap()[:, 256:512]
        psS = [es.enter_context(nc.psum_tensor(f"psS{i}", [128, 512], F32)) for i in range(2)]
        psG = [es.enter_context(nc.psum_tensor(f"psG{i}", [128, 512], F32)) for i in range(2)]
        psO = [es.enter_context(nc.psum_tensor(f"psO{i}", [128, 192], F32)) for i in range(2)]
        ps_tr = [es.enter_context(nc.psum_tensor(f"pstr{i}", [128, 256], BF16)) for i in range(2)]

        gp = bld.pool("gp", 6)
        wp = bld.pool("wp", 6)
        yp = bld.pool("yp", 6)
        tok_const = []
        tok_xt = {}
        tok_w = {}
        tok_woall = {}
        tok_xres = {}
        tok_store = {}

        def blocks_of_tg(tg):
            n0 = 0 if tg == 0 else 32 * tg - 1
            n1 = min(32 * tg + 30, NB - 1)
            return n0, n1 - n0 + 1

        xord = []
        for tg in range(NTG):
            for sub in range(NSUB):
                xord.append(("a0", tg, sub))
        for ph in ("a1", "b"):
            for qg in range(NTG):
                for sub in range(NSUB):
                    xord.append((ph, qg, sub))

        def t_body(t):
            t.wait(*tok_const)
            cnt = {"S": 0, "G": 0, "O": 0, "T": 0}

            def sgrp(emit, mark=None):
                i = cnt["S"]; cnt["S"] += 1
                if i >= 2:
                    t.wait(t.tok(("Sev", i - 2)))
                emit(psS[i % 2], i)
                bld.marks[("Sgrp", i)] = (bld.tickers["t"], bld.tickers["t"].n)
                if mark:
                    bld.marks[mark] = bld.marks[("Sgrp", i)]
                return i

            def ggrp(emit, mark=None, wait_list=()):
                i = cnt["G"]; cnt["G"] += 1
                if i >= 2:
                    t.wait(t.tok(("Gev", i - 2)))
                for wtk in wait_list:
                    t.wait(wtk)
                emit(psG[i % 2], i)
                bld.marks[("Ggrp", i)] = (bld.tickers["t"], bld.tickers["t"].n)
                if mark:
                    bld.marks[mark] = bld.marks[("Ggrp", i)]
                return i

            def transp(src_ap, width, mark=None, wait_list=(), rows=128, base=0):
                j = cnt["T"]; cnt["T"] += 1
                if j >= 2:
                    t.wait(t.tok(("Tev", j - 2)))
                for wtk in wait_list:
                    t.wait(wtk)
                t.op("transpose", ps_tr[j % 2][0:width, 0:rows],
                     src_ap, idb[base:base + rows, base:base + rows],
                     tick=True, mark=("Ttr", j))
                if mark:
                    bld.marks[mark] = bld.marks[("Ttr", j)]
                return j

            def xgroup(ph, gg, nxc):
                for sub in range(NSUB):
                    t.wait(tok_xt.get((ph, gg, sub)))
                    for c in range(CC):
                        transp(xtm[nxc[0] % 2][:, c * 128:(c + 1) * 128], 128,
                               ("xtr", ph, gg, sub, c))
                    nxc[0] += 1

            def proj_group(widx, mark):
                def em(ps, i):
                    for c in range(CC):
                        t.op("matmul", ps[:],
                             wsl[widx % 2][:, c * 128:(c + 1) * 128],
                             xg_sb[:, c * TGS:(c + 1) * TGS],
                             start=(c == 0), stop=(c == CC - 1),
                             tick=(c == CC - 1))
                sgrp(em, mark)

            nxc = [0]
            for tg in range(NTG):
                xgroup("a0", tg, nxc)
                t.wait(t.tok(("xev", "a0", tg)))
                for dc in range(CC):
                    t.wait(tok_w.get(("k", tg, dc)))
                    proj_group(dc, ("kg", tg, dc))
                for hp in range(HP):
                    t.wait(tok_w.get(("v", tg, hp)))
                    proj_group(hp, ("vg", tg, hp))
                    for sub in range(NSUB):
                        transp(vwork[:, 16 + sub * 128:16 + sub * 128 + 128], 128,
                               ("vtr", tg, hp, sub),
                               wait_list=([t.tok(("vev", tg, hp))] if sub == 0 else ()))
                    n0, nn = blocks_of_tg(tg)
                    off = 16 * n0 - 512 * tg + 16
                    for hh in range(2):
                        def em(ps, i, hh=hh, nn=nn, off=off):
                            for tt in range(32):
                                t.op("matmul", ps[0:64, 0:nn],
                                     wcvb[hh * 64:hh * 64 + 64,
                                          tt * 64:(tt + 1) * 64],
                                     vwork[hh * 64:hh * 64 + 64,
                                           off + tt: off + tt + 16 * (nn - 1) + 1:16],
                                     start=(tt == 0), stop=(tt == 31),
                                     tick=(tt == 31))
                        ggrp(em, ("cvg", tg, hp, hh),
                             wait_list=[t.tok(("vwork", tg, hp))] if hh == 0 else ())
            for h in range(NH):
                hb = (h % 2) * 64
                hc = h // 2

                def em(ps, i, hb=hb, hc=hc):
                    for tt in range(32):
                        t.op("matmul", ps[0:64, 0:NB],
                             wckb[hb:hb + 64, tt * 64:(tt + 1) * 64],
                             kfm[hb:hb + 64, hc * S + tt: hc * S + tt + 16 * (NB - 1) + 1:16],
                             start=(tt == 0), stop=(tt == 31),
                             tick=(tt == 31))
                ggrp(em, ("ckg", h),
                     wait_list=[t.tok(("kg", NTG - 1, CC - 1))] if h == 0 else ())
                for ch in range(2):
                    wdt = 128 if ch == 0 else 127
                    transp(cv_sb[hb:hb + 64, hc * NB + ch * 128: hc * NB + ch * 128 + wdt],
                           wdt, ("cvtr", h, ch), rows=64, base=hb,
                           wait_list=[t.tok(("cvev", NTG - 1, hc, h % 2))] if ch == 0 else ())
                for tg in range(NTG):
                    def em(ps, i, hb=hb):
                        t.op("matmul", ps[0:1, 0:TGS], ones_c[hb:hb + 64, :],
                             sq_s[hb:hb + 64, 0:TGS],
                             start=True, stop=True, tick=True)
                    ggrp(em, ("kng", h, tg),
                         wait_list=[t.tok(("ksq", h, tg))])
            for qg in range(NTG):
                xgroup("a1", qg, nxc)
                t.wait(t.tok(("xev", "a1", qg)))
                for dc in range(CC):
                    t.wait(tok_w.get(("qa", qg, dc)))
                    proj_group(dc, ("qga", qg, dc))
                for h in range(NH):
                    hb = (h % 2) * 64
                    hc = h // 2
                    for qt in range(NSUB):
                        def em(ps, i, hb=hb, hc=hc, qt=qt):
                            t.op("matmul", ps[:, 0:NB],
                                 qg_sb[hb:hb + 64, hc * TGS + qt * 128: hc * TGS + qt * 128 + 128],
                                 ck_sb[hb:hb + 64, hc * NB:(hc + 1) * NB],
                                 start=True, stop=True, tick=True)
                        wl = []
                        if h == 0 and qt == 0:
                            wl.append(t.tok(("qga_ev", qg, CC - 1)))
                        if qg == 0 and qt == 0:
                            wl.append(t.tok(("ckv", h)))
                        ggrp(em, ("csca", qg, h, qt), wait_list=wl)

                        def em(ps, i):
                            t.op("matmul", ps[0:128, 0:1], pc_sb[:, 0:128],
                                 ones_c[:], start=True, stop=True, tick=False)
                            t.op("matmul", ps[0:127, 1:2], pc_sb[:, 128:255],
                                 ones_c[:], start=True, stop=True, tick=True)
                        ggrp(em, ("impg", qg, h, qt),
                             wait_list=[t.tok(("pnca", qg, h, qt))])
            for h in range(NH):
                hb = (h % 2) * 64

                def em(ps, i, h=h, hb=hb):
                    t.op("matmul", ps[0:1, 0:3],
                         qmb[hb:hb + 64, (h // 2):(h // 2) + 1],
                         wg_sb[hb:hb + 64, :], start=True, stop=True, tick=True)
                ggrp(em, ("gmm", h),
                     wait_list=[t.tok(("qmrdy",))] if h == 0 else ())

            def em(ps, i):
                t.op("matmul", ps[0:128, 0:48], ones_rb[:, :], gates[0:1, 0:48],
                     start=True, stop=True, tick=True)
            ggrp(em, ("gbc",), wait_list=[t.tok(("grow",))])

            transp(impb[:, 0:1], 1, wait_list=[t.tok(("impbv",))])
            transp(impb[0:127, 1:2], 1, rows=127)
            transp(implo[:, 0:1], 1)
            transp(implo[0:127, 1:2], 1, rows=127)

            def em(ps, i):
                t.op("matmul", ps[0:128, 0:NB], ones_rb[:, :], imp_row[0:1, 0:NB],
                     start=True, stop=True, tick=True)
            ggrp(em, ("impbc",), wait_list=[t.tok(("improw",))])

            def em(ps, i):
                t.op("matmul", ps[0:128, 0:NB], ones_rb[:, :],
                     imp_row[0:1, 256:256 + NB],
                     start=True, stop=True, tick=True)
            ggrp(em, ("impbc2",))
            for kg in range(CC):
                for ch in range(2):
                    wdt = 128 if ch == 0 else 127

                    def em(ps, i, ch=ch, kg=kg, wdt=wdt):
                        t.op("matmul", ps[0:1, 0:TGS],
                             ind_sb[0:wdt, ch:ch + 1],
                             wsl[(kg * 2 + ch) % 2][0:wdt, 0:TGS],
                             start=True, stop=True, tick=True)
                    ggrp(em, ("mg", kg, ch),
                         wait_list=[t.tok(("indrdy",)), tok_w.get(("amat", ch, kg))]
                         if (kg == 0 and ch == 0) else [tok_w.get(("amat", ch, kg))])
            for kg in range(CC):
                def em(ps, i, kg=kg):
                    t.op("matmul", ps[0:128, 0:TGS], ones_rb[:, :],
                         mbc[0:1, kg * TGS:(kg + 1) * TGS],
                         start=True, stop=True, tick=True)
                ggrp(em, ("mbcg", kg),
                     wait_list=[t.tok(("mrowv",))] if kg == 0 else ())
            for qg in range(NTG):
                xgroup("b", qg, nxc)
                t.wait(t.tok(("xev", "b", qg)))
                for dc in range(CC):
                    t.wait(tok_w.get(("qb", qg, dc)))
                    proj_group(dc, ("qgb", qg, dc))
                t.wait(t.tok(("qgb_ev", qg, CC - 1)))
                for h in range(NH):
                    hb = (h % 2) * 64

                    def em(ps, i, hb=hb):
                        t.op("matmul", ps[0:1, 0:TGS], ones_c[hb:hb + 64, :],
                             sq_s[hb:hb + 64, 0:TGS],
                             start=True, stop=True, tick=True)
                    ggrp(em, ("qng", qg, h), wait_list=[t.tok(("qsq", qg, h))])
                    for q4 in range(NSUB):
                        transp(brwb[0:1, q4 * 128:(q4 + 1) * 128], 128,
                               ("nbtr", qg, h, q4), rows=1,
                               wait_list=([t.tok(("browv", qg, h))] if q4 == 0 else ()))
                for qt in range(NSUB):
                    for h in range(NH):
                        hb = (h % 2) * 64
                        hc = h // 2
                        for kch in range(CC):
                            def em(ps, i, hb=hb, hc=hc, qt=qt, kch=kch):
                                t.op("matmul", ps[:],
                                     qg_sb[hb:hb + 64, hc * TGS + qt * 128: hc * TGS + qt * 128 + 128],
                                     kfm[hb:hb + 64, hc * S + kch * TGS: hc * S + (kch + 1) * TGS],
                                     start=True, stop=True, tick=True)
                            sgrp(em, ("ssc", qg, qt, h, kch))

                        def em(ps, i, hb=hb, hc=hc, qt=qt):
                            t.op("matmul", ps[:, 0:NB],
                                 qg_sb[hb:hb + 64, hc * TGS + qt * 128: hc * TGS + qt * 128 + 128],
                                 ck_sb[hb:hb + 64, hc * NB:(hc + 1) * NB],
                                 start=True, stop=True, tick=True)
                        ggrp(em, ("cscb", qg, qt, h))
                        oi = cnt["O"]; cnt["O"] += 1
                        if oi >= 2:
                            t.wait(t.tok(("Oev", oi - 2)))
                        plan = []
                        for kc in range(NKC):
                            srcp = (praw[:, kc * 128:(kc + 1) * 128] if kc < 28
                                    else pm_tail[:, (kc - 28) * 128:(kc - 27) * 128])
                            plan.append((srcp, 128, 128,
                                         vtm[:, kc * DM + h * 64: kc * DM + h * 64 + 64],
                                         0, kc == 0, kc == NKC - 1))
                        for kw in range(4):
                            kc = 28 + kw
                            plan.append((praw[:, kc * 128:(kc + 1) * 128], 128, 128,
                                         vtm[:, kc * DM + h * 64: kc * DM + h * 64 + 64],
                                         64, kw == 0, kw == 3))
                        for ch in range(2):
                            wdt = 128 if ch == 0 else 127
                            plan.append((pc_sb[:, ch * 128: ch * 128 + wdt], wdt, 128,
                                         cvbm[0:wdt, h * 128 + ch * 64: h * 128 + ch * 64 + 64],
                                         128, ch == 0, ch == 1))
                        tjs = []

                        def emit_tr(k):
                            src_ap, wdt, rws, rhs, col, st, sp = plan[k]
                            wl = []
                            if k == 0:
                                wl.append(t.tok(("pmv", qg, qt, h)))
                                wl.append(t.tok(("pncb", qg, qt, h)))
                            tjs.append(transp(src_ap, wdt, rows=rws, wait_list=wl))
                        emit_tr(0)
                        if len(plan) > 1:
                            emit_tr(1)
                        for k in range(len(plan)):
                            src_ap, wdt, rws, rhs, col, st, sp = plan[k]
                            j = tjs[k]
                            t.wait(t.tok(("Tevd", j)))
                            mm = t.op("matmul", psO[oi % 2][:, col:col + 64],
                                      ptr_sb[j % 2][0:wdt, :], rhs,
                                      start=st, stop=sp, tick=True,
                                      mark=("pvmm", j))
                            if k + 2 < len(plan):
                                emit_tr(k + 2)
                        bld.marks[("Ogrp", oi)] = bld.marks[("pvmm", tjs[-1])]
                        bld.marks[("opv", qg, qt, h)] = bld.marks[("Ogrp", oi)]
                    for c in range(CC):
                        transp(oq_sb[:, c * 128:(c + 1) * 128], 128,
                               ("otr", qg, qt, c),
                               wait_list=([t.tok(("oqv", qg, qt))] if c == 0 else ()))
                    for nh in range(2):
                        def em(ps, i, qt=qt, nh=nh, qg=qg):
                            for c in range(CC):
                                wli = ((qg * NSUB + qt) * 2 + nh) * CC + c
                                t.wait(tok_w.get(("wo", qg, qt, nh, c)))
                                if c == 0:
                                    t.wait(t.tok(("ofmv", qg, qt)))
                                t.op("matmul", ps[:],
                                     ofm[:, c * 128:(c + 1) * 128],
                                     wosl[wli % 2][:],
                                     start=(c == 0), stop=(c == CC - 1),
                                     tick=True, mark=("womm", wli))
                        si = sgrp(em, ("wog", qg, qt, nh))
                        bld.marks[("wogS", qg, qt, nh)] = ("idx", si)
            bld.marks[("CNT",)] = ("cnt", dict(cnt))

        def s_body(s):
            cS = [0]; cG = [0]

            def snS():
                i = cS[0]; cS[0] += 1; return i

            def snG():
                i = cG[0]; cG[0] += 1; return i

            for tg in range(NTG):
                for dc in range(CC):
                    si = snS()
                    s.wait(s.tok(("Sgrp", si)))
                    s.copy(kfm[:, dc * S + tg * TGS: dc * S + (tg + 1) * TGS],
                           psS[si % 2][:], mark=("Sev", si))
                for hp in range(HP):
                    si = snS()
                    s.wait(s.tok(("Sgrp", si)))
                    if tg > 0 or hp > 0:
                        ptg, php = (tg, hp - 1) if hp > 0 else (tg - 1, HP - 1)
                        s.wait(s.tok(("vwork", ptg, php)))
                        s.wait(s.tok(("vtr", ptg, php, NSUB - 1)))
                        s.wait(s.tok(("cvg", ptg, php, 1)))
                    s.copy(vwork[:, 16:16 + TGS], psS[si % 2][:], mark=("Sev", si))
                    bld.marks[("vev", tg, hp)] = bld.marks[("Sev", si)]
                    n0, nn = blocks_of_tg(tg)
                    h2 = hp * 2
                    for hh in range(2):
                        gi = snG()
                        s.wait(s.tok(("Ggrp", gi)))
                        h = h2 + hh
                        s.add(cv_sb[(h % 2) * 64:(h % 2) * 64 + 64,
                                    (h // 2) * NB + n0:(h // 2) * NB + n0 + nn],
                              psG[gi % 2][0:64, 0:nn],
                              bcv_sb[:, 0:1], mark=("Gev", gi))
                        bld.marks[("cvev", tg, h // 2, h % 2)] = bld.marks[("Gev", gi)]
            for h in range(NH):
                hb = (h % 2) * 64
                hc = h // 2
                gi = snG()
                s.wait(s.tok(("Ggrp", gi)))
                s.add(ck_sb[hb:hb + 64, hc * NB: hc * NB + NB],
                      psG[gi % 2][0:64, 0:NB],
                      bck_sb[:, 0:1], mark=("Gev", gi))
                bld.marks[("ckv", h)] = bld.marks[("Gev", gi)]
                for tg in range(NTG):
                    if tg > 0:
                        s.wait(s.tok(("kng", h, tg - 1)))
                    elif h > 0:
                        s.wait(s.tok(("kng", h - 1, NTG - 1)))
                    s.activation(sq_s[hb:hb + 64, 0:TGS],
                                 kfm[hb:hb + 64, hc * S + tg * TGS: hc * S + (tg + 1) * TGS],
                                 AF.Square, mark=("ksq", h, tg))
                    gi = snG()
            for qg in range(NTG):
                for dc in range(CC):
                    si = snS()
                    s.wait(s.tok(("Sgrp", si)))
                    s.copy(qg_sb[:, dc * TGS:(dc + 1) * TGS], psS[si % 2][:],
                           mark=("Sev", si))
                    bld.marks[("qga_ev", qg, dc)] = bld.marks[("Sev", si)]
                for h in range(NH):
                    for qt in range(NSUB):
                        gi = snG()
                        s.wait(s.tok(("Ggrp", gi)))
                        s.wait(s.tok(("dmax", 0, qg, h, qt)))
                        if (h, qt) != (0, 0) or qg > 0:
                            ph, pqt = (h, qt - 1) if qt > 0 else ((h - 1, NSUB - 1) if h > 0 else (NH - 1, NSUB - 1))
                            pqg = qg if (h, qt) != (0, 0) else qg - 1
                            s.wait(s.tok(("impg", pqg, ph, pqt)))
                        s.activation(pc_sb[:, 0:NB], psG[gi % 2][:, 0:NB], AF.Exp,
                                     bias=dst[:, 0:1], scale=SC,
                                     accum_out=dst[:, 1:2], mark=("cexa", qg, h, qt))
                        bld.marks[("Gev", gi)] = bld.marks[("cexa", qg, h, qt)]
                        gi = snG()
            for h in range(NH):
                gi = snG()
            gi = snG()
            s.wait(s.tok(("graw",)))
            s.activation(gates[0:1, 0:48], gates[0:1, 0:48], AF.Sigmoid,
                         mark=("grow",))
            gi = snG()
            gi = snG()
            for kg in range(CC):
                for ch in range(2):
                    gi = snG()
            for kg in range(CC):
                gi = snG()
            for qg in range(NTG):
                for dc in range(CC):
                    si = snS()
                    s.wait(s.tok(("Sgrp", si)))
                    s.copy(qg_sb[:, dc * TGS:(dc + 1) * TGS], psS[si % 2][:],
                           mark=("Sev", si))
                    bld.marks[("qgb_ev", qg, dc)] = bld.marks[("Sev", si)]
                for h in range(NH):
                    hb = (h % 2) * 64
                    hc = h // 2
                    if h > 0:
                        s.wait(s.tok(("qng", qg, h - 1)))
                    s.activation(sq_s[hb:hb + 64, 0:TGS],
                                 qg_sb[hb:hb + 64, hc * TGS:(hc + 1) * TGS],
                                 AF.Square, mark=("qsq", qg, h))
                    s.wait(s.tok(("brw1", qg, h)))
                    s.activation(brwb[0:1, :], brwb[0:1, :], AF.Sqrt,
                                 mark=("brws", qg, h))
                for h in range(NH):
                    snG()
                for qt in range(NSUB):
                    for h in range(NH):
                        for kch in range(CC):
                            si = snS()
                            s.wait(s.tok(("Sgrp", si)))
                            if kch == 0:
                                s.wait(s.tok(("nbev", qg, h)))
                                prev = ((qg, qt, h - 1) if h > 0 else
                                        ((qg, qt - 1, NH - 1) if qt > 0 else
                                         ((qg - 1, NSUB - 1, NH - 1) if qg > 0 else None)))
                                if prev is not None:
                                    s.wait(s.tok(("opv", *prev)))
                                    s.wait(s.tok(("opvd", *prev)))
                            s.activation(praw[:, kch * TGS:(kch + 1) * TGS],
                                         psS[si % 2][:], AF.Exp,
                                         bias=nbcol[:, h * 4 + qt: h * 4 + qt + 1],
                                         scale=SC, mark=("Sev", si))
                            if kch == CC - 1:
                                bld.marks[("pex8", qg, qt, h)] = bld.marks[("Sev", si)]
                        gi = snG()
                        s.wait(s.tok(("Ggrp", gi)))
                        s.wait(s.tok(("dmax", 1, qg, h, qt)))
                        s.activation(pc_sb[:, 0:NB], psG[gi % 2][:, 0:NB], AF.Exp,
                                     bias=dst[:, 0:1], scale=SC,
                                     accum_out=dst[:, 1:2], mark=("cexb", qg, qt, h))
                        bld.marks[("Gev", gi)] = bld.marks[("cexb", qg, qt, h)]
                        bld.marks[("pncb", qg, qt, h)] = bld.marks[("cexb", qg, qt, h)]
                    for nh in range(2):
                        si = snS()
                    ln.s_stats(s, qg * NSUB + qt)

        def v_body(v):
            cT = [0]; cG = [0]; cO = [0]
            v.memset(stat1[:], 0.0)
            v.memset(imp_sb[:], 0.0)
            v.memset(rank_sb[:], 0.0)

            def tev(dest_ap, mark=None, width=128, rows=128):
                j = cT[0]; cT[0] += 1
                v.wait(v.tok(("Ttr", j)))
                v.tensor_copy(dest_ap, ps_tr[j % 2][0:width, 0:rows],
                              mark=("Tev", j))
                if mark:
                    bld.marks[mark] = bld.marks[("Tev", j)]
                return j

            def tevd(width=128):
                j = cT[0]; cT[0] += 1
                v.wait(v.tok(("Ttr", j)))
                if j >= 2:
                    v.wait(v.tok_opt(("pvmm", j - 2)))
                v.tensor_copy(ptr_sb[j % 2][0:width, :],
                              ps_tr[j % 2][0:width, 0:128], mark=("Tevd", j))
                bld.marks[("Tev", j)] = bld.marks[("Tevd", j)]
                return j

            def xevac(ph, gg):
                for sub in range(NSUB):
                    for c in range(CC):
                        tev(xg_sb[:, c * TGS + sub * 128: c * TGS + sub * 128 + 128])
                v.op("tensor_copy", stat1[0:1, 7:8], stat1[0:1, 7:8], tick=True,
                     mark=("xev", ph, gg))

            def snG():
                i = cG[0]; cG[0] += 1; return i

            for tg in range(NTG):
                xevac("a0", tg)
                for hp in range(HP):
                    v.wait(v.tok(("vev", tg, hp)))
                    if tg > 0:
                        v.tensor_copy(vwork[:, 0:16], vtail[:, hp * 16:(hp + 1) * 16])
                    v.tensor_copy(vtail[:, hp * 16:(hp + 1) * 16],
                                  vwork[:, TGS:TGS + 16], mark=("vwork", tg, hp))
                    for sub in range(NSUB):
                        kc = tg * NSUB + sub
                        tev(vtm.ap()[:, kc * DM + hp * 128: kc * DM + hp * 128 + 128],
                            mark=("vtr", tg, hp, sub))
                    for hh in range(2):
                        snG()
            for h in range(NH):
                snG()
                for ch in range(2):
                    wdt = 128 if ch == 0 else 127
                    tev(cvbm[0:wdt, h * 128 + ch * 64: h * 128 + ch * 64 + 64],
                        mark=("cvbm", h, ch), width=wdt, rows=64)
                for tg in range(NTG):
                    gi = snG()
                    v.wait(v.tok(("kng", h, tg)))
                    v.reduce_max(stat1[0:1, 0:1], psG[gi % 2][0:1, 0:TGS], axis=AX.X,
                                 mark=("Gev", gi))
                    if tg == 0:
                        v.tensor_copy(maxk[0:1, h:h + 1], stat1[0:1, 0:1])
                    else:
                        v.tensor_max(maxk[0:1, h:h + 1], maxk[0:1, h:h + 1],
                                     stat1[0:1, 0:1])
            for qg in range(NTG):
                xevac("a1", qg)
                for h in range(NH):
                    hb = (h % 2) * 64
                    hc = h // 2
                    for qt in range(NSUB):
                        gi = snG()
                        v.wait(v.tok(("Ggrp", gi)))
                        v.reduce_max(dst[:, 0:1], psG[gi % 2][:, 0:NB], axis=AX.X,
                                     negate=True, apply_transpose=False)
                        v.tensor_scalar_mul(dst[:, 0:1], dst[:, 0:1], SC,
                                            mark=("dmax", 0, qg, h, qt))
                        v.wait(v.tok(("cexa", qg, h, qt)))
                        v.reciprocal(dst[:, 2:3], dst[:, 1:2])
                        v.tensor_scalar(pc_sb[:, 0:NB], pc_sb[:, 0:NB],
                                        dst[:, 2:3], None, ALU.mult,
                                        mark=("pnca", qg, h, qt))
                        gi = snG()
                        v.wait(v.tok(("Ggrp", gi)))
                        if qg == 0 and h == 0 and qt == 0:
                            v.tensor_copy(imp_sb[:, 0:1], psG[gi % 2][:, 0:1])
                            v.tensor_copy(imp_sb[0:127, 1:2],
                                          psG[gi % 2][0:127, 1:2],
                                          mark=("Gev", gi))
                        else:
                            v.tensor_add(imp_sb[:, 0:1], imp_sb[:, 0:1],
                                         psG[gi % 2][:, 0:1])
                            v.tensor_add(imp_sb[0:127, 1:2], imp_sb[0:127, 1:2],
                                         psG[gi % 2][0:127, 1:2],
                                         mark=("Gev", gi))
                    v.reduce_sum(qmt[hb:hb + 64, 0:1],
                                 qg_sb[hb:hb + 64, hc * TGS:(hc + 1) * TGS], axis=AX.X)
                    if qg == 0:
                        v.tensor_copy(qm_sb[hb:hb + 64, hc:hc + 1], qmt[hb:hb + 64, 0:1])
                    else:
                        v.tensor_add(qm_sb[hb:hb + 64, hc:hc + 1],
                                     qm_sb[hb:hb + 64, hc:hc + 1], qmt[hb:hb + 64, 0:1])
            v.tensor_scalar_mul(qmb[:], qm_sb[:], 1.0 / S, mark=("qmrdy",))
            for h in range(NH):
                gi = snG()
                v.wait(v.tok(("Ggrp", gi)))
                v.tensor_add(gates[0:1, h * 3:h * 3 + 3], psG[gi % 2][0:1, 0:3],
                             bg_sb[0:1, :], mark=("Gev", gi))
                if h == NH - 1:
                    bld.marks[("graw",)] = bld.marks[("Gev", gi)]
            gi = snG()
            v.wait(v.tok(("Ggrp", gi)))
            v.tensor_copy(gcol[:, 0:48], psG[gi % 2][0:128, 0:48], mark=("Gev", gi))
            v.tensor_copy(impb[:], imp_sb[:])
            v.tensor_copy(impc[:], impb[:])
            v.tensor_tensor(implo[:], imp_sb[:], impb[:], ALU.subtract)
            v.tensor_copy(impc2[:], implo[:], mark=("impbv",))
            tev(imp_row[0:1, 0:128], width=1, rows=128)
            tev(imp_row[0:1, 128:255], width=1, rows=127)
            tev(imp_row[0:1, 256:384], width=1, rows=128)
            tev(imp_row[0:1, 384:511], width=1, rows=127, mark=("improw",))
            gi = snG()
            gi2 = snG()
            v.wait(v.tok(("Ggrp", gi2)))
            pcx = cv_sb
            for ch in range(2):
                wdt = 128 if ch == 0 else 127
                v.tensor_scalar(crow[0:wdt, 0:NB], psG[gi % 2][0:wdt, 0:NB],
                                impc[0:wdt, ch:ch + 1], None, ALU.is_gt)
                v.reduce_sum(rank_sb[0:wdt, ch:ch + 1], crow[0:wdt, 0:NB], axis=AX.X)
                v.tensor_scalar(crow[0:wdt, 0:NB], psG[gi % 2][0:wdt, 0:NB],
                                impc[0:wdt, ch:ch + 1], None, ALU.is_equal)
                v.tensor_scalar(pcx[0:wdt, 0:NB], psG[gi2 % 2][0:wdt, 0:NB],
                                impc2[0:wdt, ch:ch + 1], None, ALU.is_gt)
                v.tensor_tensor(crow[0:wdt, 0:NB], crow[0:wdt, 0:NB],
                                pcx[0:wdt, 0:NB], ALU.mult)
                v.reduce_sum(dst[0:wdt, 3:4], crow[0:wdt, 0:NB], axis=AX.X)
                v.tensor_tensor(rank_sb[0:wdt, ch:ch + 1],
                                rank_sb[0:wdt, ch:ch + 1], dst[0:wdt, 3:4],
                                ALU.add)
            v.tensor_scalar(ind_sb[:], rank_sb[:], 15.5, None, ALU.is_lt,
                            mark=("indrdy",))
            bld.marks[("Gev", gi)] = bld.marks[("indrdy",)]
            bld.marks[("Gev", gi2)] = bld.marks[("indrdy",)]
            for kg in range(CC):
                for ch in range(2):
                    gi = snG()
                    v.wait(v.tok(("Ggrp", gi)))
                    if ch == 0:
                        v.tensor_copy(mbc[0:1, kg * TGS:(kg + 1) * TGS],
                                      psG[gi % 2][0:1, 0:TGS], mark=("Gev", gi))
                    else:
                        v.tensor_add(mbc[0:1, kg * TGS:(kg + 1) * TGS],
                                     mbc[0:1, kg * TGS:(kg + 1) * TGS],
                                     psG[gi % 2][0:1, 0:TGS], mark=("Gev", gi))
            bld.marks[("mrowv",)] = bld.marks[("Gev", gi)]
            for kg in range(CC):
                gi = snG()
                v.wait(v.tok(("Ggrp", gi)))
                v.tensor_copy(mbc[:, kg * TGS:(kg + 1) * TGS],
                              psG[gi % 2][0:128, 0:TGS], mark=("Gev", gi))
            for qg in range(NTG):
                xevac("b", qg)
                for h in range(NH):
                    gi = snG()
                    v.wait(v.tok(("qng", qg, h)))
                    if (qg, h) != (0, 0):
                        pqg, ph = (qg, h - 1) if h > 0 else (qg - 1, NH - 1)
                        v.wait(v.tok(("nbtr", pqg, ph, NSUB - 1)))
                    v.tensor_scalar(brwb[0:1, :], psG[gi % 2][0:1, 0:TGS],
                                    maxk[0:1, h:h + 1], None, ALU.mult,
                                    mark=("brw1", qg, h))
                    bld.marks[("Gev", gi)] = bld.marks[("brw1", qg, h)]
                    v.wait(v.tok(("brws", qg, h)))
                    v.tensor_scalar(brwb[0:1, :], brwb[0:1, :], -SC, None,
                                    ALU.mult, mark=("browv", qg, h))
                    for q4 in range(NSUB):
                        if q4 == 0 and qg > 0:
                            v.wait(v.tok(("pex8", qg - 1, NSUB - 1, h)))
                        tev(nbcol[:, h * 4 + q4:h * 4 + q4 + 1],
                            mark=(("nbev", qg, h) if q4 == NSUB - 1 else None),
                            width=128, rows=1)
                for qt in range(NSUB):
                    for h in range(NH):
                        v.wait(v.tok(("pex8", qg, qt, h)))
                        v.reduce_sum(dst[:, 5:6], praw[:, 3584:4096], axis=AX.X)
                        v.scalar_tensor_tensor(praw[:, 0:3584], praw[:, 0:3584],
                                               1.0, mbc[:, 0:3584],
                                               ALU.mult, ALU.mult,
                                               accum_out=dst[:, 4:5])
                        v.scalar_tensor_tensor(pm_tail[:], praw[:, 3584:4096],
                                               1.0, mbc[:, 3584:4096],
                                               ALU.mult, ALU.mult,
                                               accum_out=dst[:, 9:10])
                        v.tensor_add(dst[:, 4:5], dst[:, 4:5], dst[:, 9:10],
                                     mark=("pmv", qg, qt, h))
                        gi = snG()
                        v.wait(v.tok(("Ggrp", gi)))
                        v.reduce_max(dst[:, 0:1], psG[gi % 2][:, 0:NB], axis=AX.X,
                                     negate=True)
                        v.tensor_scalar_mul(dst[:, 0:1], dst[:, 0:1], SC,
                                            mark=("dmax", 1, qg, h, qt))
                        for k in range(36):
                            tevd(128)
                        tevd(128)
                        tevd(127)
                        v.wait(v.tok(("cexb", qg, qt, h)))
                        v.reciprocal(dst[:, 6:7], dst[:, 4:5])
                        v.tensor_scalar(dst[:, 6:7], dst[:, 6:7],
                                        gcol[:, h * 3 + 1: h * 3 + 2], None, ALU.mult)
                        v.reciprocal(dst[:, 7:8], dst[:, 5:6])
                        v.tensor_scalar(dst[:, 7:8], dst[:, 7:8],
                                        gcol[:, h * 3 + 2: h * 3 + 3], None, ALU.mult)
                        v.reciprocal(dst[:, 8:9], dst[:, 1:2])
                        v.tensor_scalar(dst[:, 8:9], dst[:, 8:9],
                                        gcol[:, h * 3: h * 3 + 1], None, ALU.mult)
                        oi = cO[0]; cO[0] += 1
                        v.wait(v.tok(("opv", qg, qt, h)))
                        v.tensor_scalar(oq_sb[:, h * 64:(h + 1) * 64],
                                        psO[oi % 2][:, 0:64], dst[:, 6:7], None,
                                        ALU.mult)
                        v.scalar_tensor_tensor(oq_sb[:, h * 64:(h + 1) * 64],
                                               psO[oi % 2][:, 64:128], dst[:, 7:8],
                                               oq_sb[:, h * 64:(h + 1) * 64],
                                               ALU.mult, ALU.add)
                        v.scalar_tensor_tensor(oq_sb[:, h * 64:(h + 1) * 64],
                                               psO[oi % 2][:, 128:192], dst[:, 8:9],
                                               oq_sb[:, h * 64:(h + 1) * 64],
                                               ALU.mult, ALU.add,
                                               mark=("Oev", oi))
                        if h == NH - 1:
                            bld.marks[("oqv", qg, qt)] = bld.marks[("Oev", oi)]
                        bld.marks[("opvd", qg, qt, h)] = bld.marks[("Oev", oi)]
                    for c in range(CC):
                        tev(ofm[:, c * 128:(c + 1) * 128],
                            mark=("ofmv", qg, qt) if c == CC - 1 else None)
                    i = qg * NSUB + qt
                    for nh in range(2):
                        v.wait(v.tok(("wog", qg, qt, nh)))
                        v.wait(tok_xres.get(i))
                        if i >= 1 and nh == 0:
                            v.wait(v.tok(("L", "sq", i - 1)))
                            v.wait(tok_store.get(i - 1))
                        si = bld.marks[("wogS", qg, qt, nh)][1]
                        v.scalar_tensor_tensor(
                            ln.sum_sb[0][:, nh * 512:(nh + 1) * 512],
                            psS[si % 2][:], 1.0,
                            xres[:, nh * 512:(nh + 1) * 512],
                            ALU.mult, ALU.add, mark=("Sev", si))
                    ln.v_stats(v, i)

        def g_body(g):
            tok_const.append(gp.load(g.eng, idb[:], ident[:, :]))
            tok_const.append(gp.load(g.eng, ones_c[:], onescol[:, :]))
            tok_const.append(gp.load(g.eng, ones_rb[:], onesrow[:, :]))
            tok_const.append(gp.load(g.eng, wckb[:], wck[:, :]))
            tok_const.append(gp.load(g.eng, wcvb[:], wcv[:, :]))
            tok_const.append(gp.load(g.eng, wg_sb[:], wgm[:, :]))
            nx = [0]

            def xload(ph, gg, sub):
                if nx[0] >= 2:
                    m = ("xtr", *xord[nx[0] - 2][0:1], xord[nx[0] - 2][1], xord[nx[0] - 2][2], CC - 1)
                    g.wait(g.tok(m))
                tok_xt[(ph, gg, sub)] = gp.load(
                    g.eng, xtm[nx[0] % 2][:],
                    x_in[(gg * NSUB + sub) * 128:(gg * NSUB + sub + 1) * 128, :])
                nx[0] += 1

            WMARK = {}
            li = [0]

            def wload(key, cons_mark, src, full=True):
                if li[0] >= 2:
                    m = WMARK.get(li[0] - 2)
                    if m:
                        g.wait(g.tok(m))
                dest = (wsl[li[0] % 2].ap().rearrange("p (c f) -> p c f", c=CC)
                        if full else wsl[li[0] % 2][:, 0:TGS])
                tok_w[key] = wp.load(g.eng, dest, src)
                WMARK[li[0]] = cons_mark
                li[0] += 1

            for tg in range(NTG):
                for sub in range(NSUB):
                    xload("a0", tg, sub)
                for dc in range(CC):
                    wload(("k", tg, dc), ("kg", tg, dc),
                          wkT[:, dc * 128:(dc + 1) * 128].rearrange("(c p) f -> p c f", p=128))
                for hp in range(HP):
                    wload(("v", tg, hp), ("vg", tg, hp),
                          wvT[:, hp * 128:(hp + 1) * 128].rearrange("(c p) f -> p c f", p=128))
            for qg in range(NTG):
                for sub in range(NSUB):
                    xload("a1", qg, sub)
                for dc in range(CC):
                    wload(("qa", qg, dc), ("qga", qg, dc),
                          wqT[:, dc * 128:(dc + 1) * 128].rearrange("(c p) f -> p c f", p=128))
            for kg in range(CC):
                for ch in range(2):
                    wload(("amat", ch, kg), ("mg", kg, ch),
                          amat[ch, :, kg * TGS:(kg + 1) * TGS], full=False)
            for qg in range(NTG):
                for sub in range(NSUB):
                    xload("b", qg, sub)
                for dc in range(CC):
                    wload(("qb", qg, dc), ("qgb", qg, dc),
                          wqT[:, dc * 128:(dc + 1) * 128].rearrange("(c p) f -> p c f", p=128))
                for qt in range(NSUB):
                    for nh in range(2):
                        for c in range(CC):
                            wli = ((qg * NSUB + qt) * 2 + nh) * CC + c
                            if wli >= 2:
                                g.wait(g.tok(("womm", wli - 2)))
                            tok_w[("wo", qg, qt, nh, c)] = wp.load(
                                g.eng, wosl[wli % 2][:],
                                wo[c * 128:(c + 1) * 128, nh * 512:(nh + 1) * 512])

        def y_body(y):
            tok_const.append(yp.load(y.eng, bck_sb[:], bck[:, :]))
            tok_const.append(yp.load(y.eng, bcv_sb[:], bcv[:, :]))
            tok_const.append(yp.load(y.eng, bg_sb[:], bgm[:, :]))
            for i in range(NT):
                if i >= 1:
                    y.wait(y.tok(("L", "sum", i - 1)))
                tok_xres[i] = yp.load(
                    y.eng, xres[:], x_in[i * 128:(i + 1) * 128, :])
                if i >= 1:
                    j = i - 1
                    tok_store[j] = ln.store(y, yp, j, x_out[j * 128:(j + 1) * 128, :])
            tok_store[NT - 1] = ln.store(y, yp, NT - 1, x_out[(NT - 1) * 128:NT * 128, :])
            if y.eng is not None:
                for tkn in yp.snapshot():
                    y.eng.wait_ge(tkn[1].sems[tkn[2]], tkn[3])

        bld.run_stage(nc, es, ges, [("t", t_body), ("s", s_body),
                                    ("v", v_body), ("g", g_body),
                                    ("y", y_body)])


def build_stage1():
    if "s1" in _cache:
        return _cache["s1"]
    nc0 = bass.Bass("TRN2", target_bir_lowering=False, debug=False)
    nc = _NcAlias(nc0)
    ges = contextlib.ExitStack()
    with ges:
        _emit_stage1(nc, ges, Build(), _s1_drams(nc))
    _cache["s1"] = nc0
    return nc0



def _wT(a):
    return np.ascontiguousarray(np.asarray(a).T)


def _fmvec(v):
    return np.ascontiguousarray(np.asarray(v).reshape(CC, 128).T)


def _ident():
    return np.eye(128, dtype=np.float32)


def stage2_inputs(x_core, W_r, W_k, W_v, mu_r, mu_k):
    return dict(x_in=x_core, wrT=_wT(W_r), wkT=_wT(W_k), wvT=_wT(W_v),
                mur=_fmvec(mu_r), muk=_fmvec(mu_k), ident=_ident())


def run_stage2(x, W_r, W_k, W_v, mu_r, mu_k):
    nc = build_stage2()
    com = stage2_inputs(None, W_r, W_k, W_v, mu_r, mu_k)
    del com["x_in"]
    in_maps = [dict(com, x_in=np.ascontiguousarray(x[c * BPC:(c + 1) * BPC]))
               for c in range(NCORES)]
    res = run_bass_kernel_spmd(nc, in_maps, core_ids=list(range(NCORES)))
    return np.concatenate([np.asarray(r["x_out"]) for r in res.results], axis=0)


def _c(a):
    return np.ascontiguousarray(np.asarray(a))


def _s1_dup(W):
    out = np.zeros((128, 32 * 64), np.float32)
    for t in range(32):
        A = np.asarray(W)[t * 64:(t + 1) * 64, :]
        out[0:64, t * 64:(t + 1) * 64] = A
        out[64:128, t * 64:(t + 1) * 64] = A
    return out


def _s1_amat():
    A = np.zeros((255, S), np.float32)
    for t in range(255):
        for r in range(64):
            A[t, min(t * 16 + r, S - 1)] += 1.0
    amat = np.zeros((2, 128, S), np.float32)
    amat[0, :, :] = A[0:128]
    amat[1, 0:127, :] = A[128:255]
    return amat


def stage1_inputs(Wq, Wk, Wv, Wo, Wck, bck, Wcv, bcv, Wg, bg):
    return dict(wqT=_c(Wq), wkT=_c(Wk), wvT=_c(Wv), wo=_c(Wo),
                wck=_s1_dup(Wck), wcv=_s1_dup(Wcv),
                bck=_c(np.asarray(bck).reshape(64, 1)),
                bcv=_c(np.asarray(bcv).reshape(64, 1)),
                wgm=_c(np.concatenate([np.asarray(Wg), np.asarray(Wg)], axis=0)),
                bgm=_c(np.asarray(bg).reshape(1, 3)),
                amat=_s1_amat(), ident=_ident(),
                onescol=np.ones((128, 1), np.float32),
                onesrow=np.ones((1, 128), np.float32))


def stage4_inputs(sW1, sb1, sW2, sb2, rW1, rb1, rW2, rb2, gW, gb):
    sW1 = np.asarray(sW1); sb1 = np.asarray(sb1)
    sW2 = np.asarray(sW2); sb2 = np.asarray(sb2)
    w1s = np.stack([sW1[0, :, :2048], sW1[0, :, 2048:]])
    wcat1 = _c(np.concatenate([w1s, np.asarray(rW1)], axis=0))
    w2s = np.stack([sW2[0, :2048, :], sW2[0, 2048:, :]])
    wcat2 = _c(np.concatenate([w2s, np.asarray(rW2)], axis=0))
    b1cat_parts = [sb1[0][:2048].reshape(16, 128).T,
                   sb1[0][2048:].reshape(16, 128).T]
    for e in range(15):
        b1cat_parts.append(np.asarray(rb1)[e].reshape(16, 128).T)
    b1cat = _c(np.concatenate(b1cat_parts, axis=1))
    sb2b = _c(np.tile(sb2[0][None, :], (128, 1)))
    gbb = _c(np.tile(np.asarray(gb)[None, :], (128, 1)))
    return dict(wcat1=wcat1, wcat2=wcat2, b1cat=b1cat, sb2b=sb2b,
                rb2=_c(rb2), gw=_c(gW), gbb=gbb, ident=_ident())


def build_mega():
    if "mega" in _cache:
        return _cache["mega"]
    nc0 = bass.Bass("TRN2", target_bir_lowering=False, debug=False)
    nc = _NcAlias(nc0)

    def T(name, shape, kind="ExternalInput"):
        return nc.dram_tensor(name, shape, F32, kind=kind).ap()

    x_in = T("x_in", [BPC, S, DM])
    x_out = T("x_out", [BPC, S, DM], "ExternalOutput")
    y1 = T("y1", [BPC, S, DM], "Internal")
    y2 = T("y2", [BPC, S, DM], "Internal")
    y3 = T("y3", [BPC, S, DM], "Internal")
    ident = T("ident", [128, 128])
    onescol = T("onescol", [128, 1])
    onesrow = T("onesrow", [1, 128])
    com = dict(ident=ident, onescol=onescol, onesrow=onesrow)
    s1w = dict(wqT=T("wqT", [DM, DM]), wkT=T("wkT", [DM, DM]),
               wvT=T("wvT", [DM, DM]), wo=T("wo", [DM, DM]),
               wck=T("wck", [128, 32 * 64]), wcv=T("wcv", [128, 32 * 64]),
               bck=T("bck", [64, 1]), bcv=T("bcv", [64, 1]),
               wgm=T("wgm", [128, 3]), bgm=T("bgm", [1, 3]),
               amat=T("amat", [2, 128, S]))
    s2w = dict(wrT=T("fwrT", [DM, DM]), wkT=T("fwkT", [DM, DM]),
               wvT=T("fwvT", [DM, DM]), mur=T("mur", [128, CC]),
               muk=T("muk", [128, CC]))
    s3w = dict(wqkvT=T("wqkvT", [DM, 3 * DM]))
    s4w = dict(wcat1=T("wcat1", [17, DM, 2048]), wcat2=T("wcat2", [17, 2048, DM]),
               b1cat=T("b1cat", [128, 17 * 16]), sb2b=T("sb2b", [128, DM]),
               rb2=T("rb2", [15, DM]), gw=T("gw", [DM, 15]),
               gbb=T("gbb", [128, 15]))

    bld = Build()
    ges = contextlib.ExitStack()
    with ges:
        for b in range(BPC):
            _emit_stage1(nc, ges, bld,
                         dict(s1w, **com, x_in=x_in[b], x_out=y1[b]))
        _emit_stage2(nc, ges, bld, dict(s2w, **com, x_in=y1, x_out=y2))
        _emit_stage3(nc, ges, bld, dict(s3w, **com, x_in=y2, x_out=y3))
        for b in range(BPC):
            _emit_stage4(nc, ges, bld,
                         dict(s4w, **com, x_in=y3[b], x_out=x_out[b]))
    _cache["mega"] = nc0
    return nc0


def mega_inputs(inputs):
    s1c = stage1_inputs(inputs["L0_attn_Wq"], inputs["L0_attn_Wk"],
                        inputs["L0_attn_Wv"], inputs["L0_attn_Wo"],
                        inputs["L0_attn_Wck"], inputs["L0_attn_bck"],
                        inputs["L0_attn_Wcv"], inputs["L0_attn_bcv"],
                        inputs["L0_attn_Wg"], inputs["L0_attn_bg"])
    s4c = stage4_inputs(inputs["L1_ffn_sW1"], inputs["L1_ffn_sb1"],
                        inputs["L1_ffn_sW2"], inputs["L1_ffn_sb2"],
                        inputs["L1_ffn_rW1"], inputs["L1_ffn_rb1"],
                        inputs["L1_ffn_rW2"], inputs["L1_ffn_rb2"],
                        inputs["L1_ffn_gW"], inputs["L1_ffn_gb"])
    com = dict(s1c)
    com.update(s4c)
    com.update(fwrT=_wT(inputs["L0_ffn_W_r"]), fwkT=_wT(inputs["L0_ffn_W_k"]),
               fwvT=_wT(inputs["L0_ffn_W_v"]),
               mur=_fmvec(inputs["L0_ffn_mu_r"]),
               muk=_fmvec(inputs["L0_ffn_mu_k"]),
               wqkvT=_wT(inputs["L1_attn_W_qkv"]))
    return com


def kernel(**inputs):
    x = _c(inputs["L0_attn_x"])
    nc = build_mega()
    com = mega_inputs(inputs)
    in_maps = [dict(com, x_in=_c(x[c * BPC:(c + 1) * BPC]))
               for c in range(NCORES)]
    res = run_bass_kernel_spmd(nc, in_maps, core_ids=list(range(NCORES)))
    return np.concatenate([np.asarray(r["x_out"]) for r in res.results],
                          axis=0)


def kernel_unfused(**inputs):
    x = _c(inputs["L0_attn_x"])
    cores = list(range(NCORES))
    ones_c = np.ones((128, 1), np.float32)
    ones_r = np.ones((1, 128), np.float32)

    nc1 = build_stage1()
    s1c = stage1_inputs(inputs["L0_attn_Wq"], inputs["L0_attn_Wk"],
                        inputs["L0_attn_Wv"], inputs["L0_attn_Wo"],
                        inputs["L0_attn_Wck"], inputs["L0_attn_bck"],
                        inputs["L0_attn_Wcv"], inputs["L0_attn_bcv"],
                        inputs["L0_attn_Wg"], inputs["L0_attn_bg"])
    y = np.empty((B, S, DM), np.float32)
    for lp in range(2):
        in_maps = [dict(s1c, x_in=_c(x[2 * c + lp])) for c in range(NCORES)]
        res = run_bass_kernel_spmd(nc1, in_maps, core_ids=cores)
        for c in range(NCORES):
            y[2 * c + lp] = np.asarray(res.results[c]["x_out"])

    nc2 = build_stage2()
    com2 = stage2_inputs(None, inputs["L0_ffn_W_r"], inputs["L0_ffn_W_k"],
                         inputs["L0_ffn_W_v"], inputs["L0_ffn_mu_r"],
                         inputs["L0_ffn_mu_k"])
    del com2["x_in"]
    in_maps = [dict(com2, x_in=_c(y[c * BPC:(c + 1) * BPC]))
               for c in range(NCORES)]
    res = run_bass_kernel_spmd(nc2, in_maps, core_ids=cores)
    y = np.concatenate([np.asarray(r["x_out"]) for r in res.results], axis=0)

    nc3 = build_stage3()
    com3 = dict(wqkvT=_wT(inputs["L1_attn_W_qkv"]), ident=_ident(),
                onescol=ones_c, onesrow=ones_r)
    in_maps = [dict(com3, x_in=_c(y[c * BPC:(c + 1) * BPC]))
               for c in range(NCORES)]
    res = run_bass_kernel_spmd(nc3, in_maps, core_ids=cores)
    y = np.concatenate([np.asarray(r["x_out"]) for r in res.results], axis=0)

    nc4 = build_stage4()
    s4c = stage4_inputs(inputs["L1_ffn_sW1"], inputs["L1_ffn_sb1"],
                        inputs["L1_ffn_sW2"], inputs["L1_ffn_sb2"],
                        inputs["L1_ffn_rW1"], inputs["L1_ffn_rb1"],
                        inputs["L1_ffn_rW2"], inputs["L1_ffn_rb2"],
                        inputs["L1_ffn_gW"], inputs["L1_ffn_gb"])
    out = np.empty((B, S, DM), np.float32)
    for lp in range(2):
        in_maps = [dict(s4c, x_in=_c(y[2 * c + lp])) for c in range(NCORES)]
        res = run_bass_kernel_spmd(nc4, in_maps, core_ids=cores)
        for c in range(NCORES):
            out[2 * c + lp] = np.asarray(res.results[c]["x_out"])
    return out


# version 03
# speedup vs baseline: 1.2527x; 1.2527x over previous
import contextlib
import numpy as np

import concourse.bass as bass
import concourse.mybir as mybir
from concourse.bass_utils import run_bass_kernel_spmd

F32 = mybir.dt.float32
BF16 = mybir.dt.bfloat16
AF = mybir.ActivationFunctionType
ALU = mybir.AluOpType
AX = mybir.AxisListType

NCORES = 8
B = 16
S = 4096
DM = 1024
BPC = B // NCORES
CC = DM // 128
EPS = 1e-5

_cache = {}



class _MockIns:
    def then_inc(self, *a, **k):
        return self


_MOCK = _MockIns()


class Ticker:

    def __init__(self, name):
        self.name = name
        self.sem = None
        self.n = 0

    def reset(self):
        self.n = 0


class DmaPool:

    def __init__(self, name, nsem):
        self.name = name
        self.nsem = nsem
        self.sems = None
        self.uses = [0] * nsem
        self.i = 0

    def reset(self):
        self.uses = [0] * self.nsem
        self.i = 0

    def load(self, eng, out_ap, in_ap):
        j = self.i % self.nsem
        if eng is not None:
            if self.uses[j] > 0:
                eng.wait_ge(self.sems[j], 16 * self.uses[j])
            eng.dma_start(out=out_ap, in_=in_ap).then_inc(self.sems[j], 16)
        self.uses[j] += 1
        self.i += 1
        return ("dma", self, j, 16 * self.uses[j])

    def snapshot(self):
        return [("dma", self, j, 16 * self.uses[j]) for j in range(self.nsem)
                if self.uses[j] > 0]


class Eng:

    def __init__(self, bld, ticker, serialize):
        self.bld = bld
        self.ticker = ticker
        self.serialize = serialize
        self.eng = None

    def wait(self, *tokens):
        if self.eng is None:
            return
        need = {}
        for t in tokens:
            if t is None:
                continue
            if t[0] == "dma":
                _, pool, j, cnt = t
                key = ("dma", id(pool), j)
                sem = pool.sems[j]
            else:
                _, tk, cnt = t
                key = ("tk", id(tk))
                sem = tk.sem
            if key not in need or need[key][1] < cnt:
                need[key] = (sem, cnt)
        for sem, cnt in need.values():
            self.eng.wait_ge(sem, cnt)

    def tok(self, mark):
        v = self.bld.marks.get(mark)
        if v is None:
            assert self.bld.pass0, f"mark {mark} unresolved in pass 1"
            return None
        tk, n = v
        return ("tick", tk, n)

    def tok_opt(self, mark):
        v = self.bld.marks.get(mark)
        if v is None:
            return None
        tk, n = v
        return ("tick", tk, n)

    def op(self, name, *a, mark=None, tick=True, **k):
        if self.eng is not None:
            if self.serialize and self.ticker.n > 0:
                self.eng.wait_ge(self.ticker.sem, self.ticker.n)
            ins = getattr(self.eng, name)(*a, **k)
        else:
            ins = _MOCK
        if tick:
            if self.eng is not None:
                ins.then_inc(self.ticker.sem, 1)
            self.ticker.n += 1
            if mark is not None:
                self.bld.marks[mark] = (self.ticker, self.ticker.n)
        return ins

    def __getattr__(self, name):
        if name.startswith("_"):
            raise AttributeError(name)

        def f(*a, **k):
            return self.op(name, *a, **k)
        return f


class Build:

    def __init__(self):
        self.marks = {}
        self.tickers = {n: Ticker(n) for n in ("t", "v", "s", "g", "y")}
        self.pools = {}
        self.pass0 = True

    def pool(self, name, nsem):
        if name not in self.pools:
            self.pools[name] = DmaPool(name, nsem)
        return self.pools[name]

    def run(self, nc, es, bodies):
        return self.run_stage(nc, es, es, bodies)

    def run_stage(self, nc, es, ges, bodies):
        serialize = {"t": False, "v": True, "s": True, "g": False, "y": False}
        self.marks = {}
        bar = [("tick", tk, tk.n) for tk in self.tickers.values() if tk.n > 0]
        for p in self.pools.values():
            bar.extend(p.snapshot())
        snap_t = {k: t.n for k, t in self.tickers.items()}
        snap_p = {p.name: (list(p.uses), p.i) for p in self.pools.values()}
        self.pass0 = True
        engs = {k: Eng(self, self.tickers[k], serialize[k]) for k in self.tickers}
        for k, fn in bodies:
            fn(engs[k])
        counts0 = {k: t.n for k, t in self.tickers.items()}
        pools0 = {p.name: p.i for p in self.pools.values()}
        self.pass0 = False
        for k, t in self.tickers.items():
            t.n = snap_t[k]
            if t.sem is None:
                t.sem = ges.enter_context(nc.semaphore(f"tk_{t.name}"))
        for p in self.pools.values():
            p.uses = list(snap_p[p.name][0]) if p.name in snap_p else [0] * p.nsem
            p.i = snap_p[p.name][1] if p.name in snap_p else 0
            if p.sems is None:
                p.sems = [ges.enter_context(nc.semaphore(f"{p.name}{j}"))
                          for j in range(p.nsem)]
        blk = es.enter_context(nc.Block())
        attach = {"t": blk.tensor, "v": blk.vector, "s": blk.scalar,
                  "g": blk.gpsimd, "y": blk.sync}

        def mk(k, fn):
            def body(raw):
                engs[k].eng = raw
                engs[k].wait(*bar)
                fn(engs[k])
                engs[k].eng = None
            return body
        for k, fn in bodies:
            attach[k](mk(k, fn))
        counts1 = {k: t.n for k, t in self.tickers.items()}
        assert counts0 == counts1, (counts0, counts1)
        for p in self.pools.values():
            assert pools0[p.name] == p.i


class _NcAlias:

    def __init__(self, nc):
        self._nc = nc

    def sbuf_tensor(self, *a, **k):
        k.setdefault("allow_name_mangling", True)
        return self._nc.sbuf_tensor(*a, **k)

    def psum_tensor(self, *a, **k):
        k.setdefault("allow_name_mangling", True)
        return self._nc.psum_tensor(*a, **k)

    def __getattr__(self, name):
        return getattr(self._nc, name)



class LnExit:
    def __init__(self, nc, es, tag, dm=DM, nbuf=2):
        self.dm = dm
        self.nbuf = nbuf
        self.sum_sb = [es.enter_context(nc.sbuf_tensor(f"{tag}sum{i}", [128, dm], F32)) for i in range(nbuf)]
        self.sq_sb = es.enter_context(nc.sbuf_tensor(f"{tag}sq", [128, dm], BF16))
        self.st = es.enter_context(nc.sbuf_tensor(f"{tag}st", [128, 16], F32))
        self.tag = tag

    def v_stats(self, v, i):
        tag = self.tag
        i = i % self.nbuf if False else i
        o = (i % 2) * 8
        st = self.st
        v.reduce_sum(st[:, o + 0:o + 1], self.sum_sb[i % self.nbuf][:], axis=AX.X,
                     mark=(tag, "sum", i))
        v.tensor_scalar_mul(st[:, o + 2:o + 3], st[:, o + 0:o + 1], 1.0 / self.dm)
        v.tensor_mul(st[:, o + 3:o + 4], st[:, o + 2:o + 3], st[:, o + 2:o + 3])
        v.tensor_scalar_sub(st[:, o + 3:o + 4], st[:, o + 3:o + 4], EPS)
        v.wait(v.tok((tag, "sq", i)))
        v.scalar_tensor_tensor(st[:, o + 4:o + 5], st[:, o + 1:o + 2], 1.0 / self.dm,
                               st[:, o + 3:o + 4], ALU.mult, ALU.subtract,
                               mark=(tag, "ve", i))
        v.wait(v.tok((tag, "sqrt", i)))
        v.reciprocal(st[:, o + 6:o + 7], st[:, o + 5:o + 6])
        v.tensor_scalar(self.sum_sb[i % self.nbuf][:], self.sum_sb[i % self.nbuf][:],
                        st[:, o + 2:o + 3], st[:, o + 6:o + 7],
                        ALU.subtract, ALU.mult, mark=(tag, "nrm", i))

    def s_stats(self, s, i):
        tag = self.tag
        o = (i % 2) * 8
        s.wait(s.tok((tag, "sum", i)))
        s.activation(self.sq_sb[:], self.sum_sb[i % self.nbuf][:], AF.Square,
                     accum_out=self.st[:, o + 1:o + 2], mark=(tag, "sq", i))
        s.wait(s.tok((tag, "ve", i)))
        s.activation(self.st[:, o + 5:o + 6], self.st[:, o + 4:o + 5],
                     AF.Sqrt, mark=(tag, "sqrt", i))

    def store(self, y, pool, i, out_ap):
        y.wait(y.tok((self.tag, "nrm", i)))
        return pool.load(y.eng, out_ap, self.sum_sb[i % self.nbuf][:])



def _s2_drams(nc):
    return dict(
        x_in=nc.dram_tensor("x_in", [BPC, S, DM], F32, kind="ExternalInput").ap(),
        wrT=nc.dram_tensor("wrT", [DM, DM], F32, kind="ExternalInput").ap(),
        wkT=nc.dram_tensor("wkT", [DM, DM], F32, kind="ExternalInput").ap(),
        wvT=nc.dram_tensor("wvT", [DM, DM], F32, kind="ExternalInput").ap(),
        mur=nc.dram_tensor("mur", [128, CC], F32, kind="ExternalInput").ap(),
        muk=nc.dram_tensor("muk", [128, CC], F32, kind="ExternalInput").ap(),
        ident=nc.dram_tensor("ident", [128, 128], F32, kind="ExternalInput").ap(),
        x_out=nc.dram_tensor("x_out", [BPC, S, DM], F32, kind="ExternalOutput").ap())


def _emit_stage2(nc, ges, bld, A):
    x_in = A["x_in"]; wrT = A["wrT"]; wkT = A["wkT"]; wvT = A["wvT"]
    mur = A["mur"]; muk = A["muk"]; ident = A["ident"]; x_out = A["x_out"]

    SW = 4097
    NT = S // 128
    TGS = 512
    NTG = S // TGS
    NSUB = TGS // 128
    NSUBT = BPC * NT

    es = contextlib.ExitStack()
    with es:
        xfm = es.enter_context(nc.sbuf_tensor("xfm", [128, CC * SW], BF16))
        xtm = [es.enter_context(nc.sbuf_tensor(f"xtm{i}", [128, DM], BF16)) for i in range(2)]
        idb = es.enter_context(nc.sbuf_tensor("idb", [128, 128], BF16))
        wr_sb = es.enter_context(nc.sbuf_tensor("wr_sb", [128, CC * DM], BF16))
        wk_sb = es.enter_context(nc.sbuf_tensor("wk_sb", [128, CC * DM], BF16))
        wv_sb = es.enter_context(nc.sbuf_tensor("wv_sb", [128, CC * DM], BF16))
        mur_sb = es.enter_context(nc.sbuf_tensor("mur_sb", [128, CC], F32))
        muk_sb = es.enter_context(nc.sbuf_tensor("muk_sb", [128, CC], F32))
        xr_sb = [es.enter_context(nc.sbuf_tensor(f"xr{i}", [128, CC * TGS], BF16)) for i in range(2)]
        xk_sb = [es.enter_context(nc.sbuf_tensor(f"xk{i}", [128, CC * TGS], BF16)) for i in range(2)]
        d_sb = es.enter_context(nc.sbuf_tensor("d_sb", [128, TGS], BF16))
        r_sb = es.enter_context(nc.sbuf_tensor("r_sb", [128, NSUB * DM], BF16))
        kr_sb = es.enter_context(nc.sbuf_tensor("kr_sb", [128, TGS], BF16))
        k2_sb = [es.enter_context(nc.sbuf_tensor(f"k2{i}", [128, CC * TGS], BF16)) for i in range(2)]
        xres = [es.enter_context(nc.sbuf_tensor(f"xres{i}", [128, DM], F32)) for i in range(2)]
        ln = LnExit(nc, es, "L")
        ps_tr = [es.enter_context(nc.psum_tensor(f"ps_tr{i}", [128, 128], BF16)) for i in range(2)]
        pr = [es.enter_context(nc.psum_tensor(f"pr{i}", [128, 512], F32)) for i in range(2)]
        pk = [es.enter_context(nc.psum_tensor(f"pk{i}", [128, 512], F32)) for i in range(2)]
        pv = [es.enter_context(nc.psum_tensor(f"pv{i}", [128, 512], F32)) for i in range(2)]

        gp = bld.pool("gp", 6)
        yp = bld.pool("yp", 6)
        tok_xtile = {}
        tok_w = []
        tok_xres = {}
        tok_store = {}

        def g_body(g):
            tok_w.append(gp.load(g.eng, idb[:], ident[:, :]))
            for (w_sb, w_ap) in ((wr_sb, wrT), (wk_sb, wkT), (wv_sb, wvT)):
                for c in range(CC):
                    tok_w.append(gp.load(g.eng, w_sb[:, c * DM:(c + 1) * DM],
                                         w_ap[c * 128:(c + 1) * 128, :]))
            tok_w.append(gp.load(g.eng, mur_sb[:], mur[:, :]))
            tok_w.append(gp.load(g.eng, muk_sb[:], muk[:, :]))
            for b in range(BPC):
                for t in range(NT):
                    i = b * NT + t
                    if i >= 2:
                        g.wait(g.tok(("tr", (i - 1) * CC - 1)))
                    tok_xtile[(b, t)] = gp.load(
                        g.eng, xtm[i % 2][:], x_in[b, t * 128:(t + 1) * 128, :])

        def y_body(y):
            for b in range(BPC):
                for sg in range(NT):
                    i = b * NT + sg
                    if i >= 2:
                        y.wait(y.tok(("L", "sum", i - 2)))
                    tok_xres[i] = yp.load(
                        y.eng, xres[i % 2][:], x_in[b, sg * 128:(sg + 1) * 128, :])
                    if i >= 2:
                        j = i - 2
                        bb, ss = divmod(j, NT)
                        tok_store[j] = ln.store(y, yp, j,
                                                x_out[bb, ss * 128:(ss + 1) * 128, :])
            for j in (NSUBT - 2, NSUBT - 1):
                bb, ss = divmod(j, NT)
                tok_store[j] = ln.store(y, yp, j, x_out[bb, ss * 128:(ss + 1) * 128, :])
            if y.eng is not None:
                for t in yp.snapshot():
                    y.eng.wait_ge(t[1].sems[t[2]], t[3])

        def t_body(t):
            t.wait(tok_w[0])
            for b in range(BPC):
                for tt in range(NT):
                    i = b * NT + tt
                    t.wait(tok_xtile[(b, tt)])
                    if b > 0 and tt == 0:
                        t.wait(t.tok(("xrk", b - 1, NTG - 1)))
                    for c in range(CC):
                        j = i * CC + c
                        if j >= 2:
                            t.wait(t.tok(("trv", j - 2)))
                        t.op("transpose", ps_tr[j % 2][:],
                             xtm[i % 2][:, c * 128:(c + 1) * 128], idb[:],
                             tick=True, mark=("tr", j))
                if b == 0:
                    t.wait(*tok_w)
                for tg in range(NTG):
                    gi = b * NTG + tg
                    t.wait(t.tok(("xrk", b, tg)))
                    for sub in range(NSUB):
                        for nh in range(2):
                            rgi = gi * 8 + sub * 2 + nh
                            if rgi >= 2:
                                t.wait(t.tok(("sig", rgi - 2)))
                            for c in range(CC):
                                t.op("matmul",
                                     pr[rgi % 2][:],
                                     xr_sb[tg % 2][:, c * TGS + sub * 128: c * TGS + sub * 128 + 128],
                                     wr_sb[:, c * DM + nh * 512: c * DM + (nh + 1) * 512],
                                     start=(c == 0), stop=(c == CC - 1),
                                     tick=(c == CC - 1), mark=(("peR", rgi) if c == CC - 1 else None))
                    for kc in range(CC):
                        kgi = gi * 8 + kc
                        if kgi >= 2:
                            t.wait(t.tok(("k2", kgi - 2)))
                        for c in range(CC):
                            t.op("matmul",
                                 pk[kgi % 2][:],
                                 wk_sb[:, c * DM + kc * 128: c * DM + kc * 128 + 128],
                                 xk_sb[tg % 2][:, c * TGS:(c + 1) * TGS],
                                 start=(c == 0), stop=(c == CC - 1),
                                 tick=(c == CC - 1), mark=(("peK", kgi) if c == CC - 1 else None))
                    t.wait(t.tok(("k2", gi * 8 + 7)))
                    for sub in range(NSUB):
                        for nh in range(2):
                            vgi = gi * 8 + sub * 2 + nh
                            if vgi >= 2:
                                t.wait(t.tok(("mulV", vgi - 2)))
                            for c in range(CC):
                                t.op("matmul",
                                     pv[vgi % 2][:],
                                     k2_sb[tg % 2][:, c * TGS + sub * 128: c * TGS + sub * 128 + 128],
                                     wv_sb[:, c * DM + nh * 512: c * DM + (nh + 1) * 512],
                                     start=(c == 0), stop=(c == CC - 1),
                                     tick=(c == CC - 1), mark=(("peV", vgi) if c == CC - 1 else None))

        def s_body(s):
            for b in range(BPC):
                for tg in range(NTG):
                    gi = b * NTG + tg
                    for sub in range(NSUB):
                        for nh in range(2):
                            rgi = gi * 8 + sub * 2 + nh
                            s.wait(s.tok(("peR", rgi)))
                            if gi >= 1 and sub == 0 and nh == 0:
                                s.wait(s.tok(("mulV", gi * 8 - 1)))
                            s.activation(
                                r_sb[:, sub * DM + nh * 512: sub * DM + (nh + 1) * 512],
                                pr[rgi % 2][:], AF.Sigmoid, mark=("sig", rgi))
                    for kc in range(CC):
                        kgi = gi * 8 + kc
                        s.wait(s.tok(("peK", kgi)))
                        if gi >= 2 and kc == 0:
                            s.wait(s.tok(("peV", (gi - 2) * 8 + 7)))
                        s.activation(kr_sb[:], pk[kgi % 2][:], AF.Relu)
                        s.activation(
                            k2_sb[tg % 2][:, kc * TGS:(kc + 1) * TGS],
                            kr_sb[:], AF.Square, mark=("k2", kgi))
                    for sub in range(NSUB):
                        ln.s_stats(s, gi * NSUB + sub)

        def v_body(v):
            for c in range(CC):
                v.memset(xfm[:, c * SW:c * SW + 1], 0.0)
            for b in range(BPC):
                for tt in range(NT):
                    i = b * NT + tt
                    for c in range(CC):
                        j = i * CC + c
                        v.wait(v.tok(("tr", j)))
                        v.tensor_copy(
                            xfm[:, c * SW + 1 + tt * 128: c * SW + 1 + tt * 128 + 128],
                            ps_tr[j % 2][:], mark=("trv", j))
                for tg in range(NTG):
                    gi = b * NTG + tg
                    if gi >= 2:
                        v.wait(v.tok(("peK", (gi - 2) * 8 + 7)))
                    for c in range(CC):
                        xs = xfm[:, c * SW + 1 + tg * TGS: c * SW + 1 + (tg + 1) * TGS]
                        xp = xfm[:, c * SW + tg * TGS: c * SW + (tg + 1) * TGS]
                        v.tensor_sub(d_sb[:], xs, xp)
                        v.scalar_tensor_tensor(
                            xr_sb[tg % 2][:, c * TGS:(c + 1) * TGS], d_sb[:],
                            mur_sb[:, c:c + 1], xp, ALU.mult, ALU.add)
                        v.scalar_tensor_tensor(
                            xk_sb[tg % 2][:, c * TGS:(c + 1) * TGS], d_sb[:],
                            muk_sb[:, c:c + 1], xp, ALU.mult, ALU.add,
                            mark=(("xrk", b, tg) if c == CC - 1 else None))
                    for sub in range(NSUB):
                        i = gi * NSUB + sub
                        for nh in range(2):
                            vgi = gi * 8 + sub * 2 + nh
                            v.wait(v.tok(("peV", vgi)))
                            if nh == 0 and sub == 0:
                                v.wait(v.tok(("sig", gi * 8 + 7)))
                            if i >= 2 and nh == 0:
                                v.wait(v.tok(("L", "sq", i - 2)))
                                v.wait(tok_store.get(i - 2))
                            v.tensor_mul(
                                ln.sum_sb[i % 2][:, nh * 512:(nh + 1) * 512],
                                pv[vgi % 2][:],
                                r_sb[:, sub * DM + nh * 512: sub * DM + (nh + 1) * 512],
                                mark=("mulV", vgi))
                        v.wait(tok_xres[i])
                        if i >= 2:
                            v.wait(tok_store[i - 2])
                        v.tensor_add(ln.sum_sb[i % 2][:], ln.sum_sb[i % 2][:],
                                     xres[i % 2][:])
                        ln.v_stats(v, i)

        bld.run_stage(nc, es, ges, [("g", g_body), ("y", y_body), ("t", t_body),
                                    ("s", s_body), ("v", v_body)])


def build_stage2():
    if "s2" in _cache:
        return _cache["s2"]
    nc0 = bass.Bass("TRN2", target_bir_lowering=False, debug=False)
    nc = _NcAlias(nc0)
    ges = contextlib.ExitStack()
    with ges:
        _emit_stage2(nc, ges, Build(), _s2_drams(nc))
    _cache["s2"] = nc0
    return nc0



def _s3_drams(nc):
    return dict(
        x_in=nc.dram_tensor("x_in", [BPC, S, DM], F32, kind="ExternalInput").ap(),
        wqkvT=nc.dram_tensor("wqkvT", [DM, 3 * DM], F32, kind="ExternalInput").ap(),
        ident=nc.dram_tensor("ident", [128, 128], F32, kind="ExternalInput").ap(),
        onescol=nc.dram_tensor("onescol", [128, 1], F32, kind="ExternalInput").ap(),
        onesrow=nc.dram_tensor("onesrow", [1, 128], F32, kind="ExternalInput").ap(),
        x_out=nc.dram_tensor("x_out", [BPC, S, DM], F32, kind="ExternalOutput").ap())


def _emit_stage3(nc, ges, bld, A):
    x_in = A["x_in"]; wqkvT = A["wqkvT"]; ident = A["ident"]
    onescol = A["onescol"]; onesrow = A["onesrow"]; x_out = A["x_out"]

    NT = S // 128
    TGS = 512
    NTG = S // TGS
    NSUB = TGS // 128
    NKP = NT
    SCALE = 1.0 / 32.0
    A1G = 17
    A2G = 46

    es = contextlib.ExitStack()
    with es:
        kfm = es.enter_context(nc.sbuf_tensor("kfm", [128, CC * S], BF16))
        vtm = es.enter_context(nc.sbuf_tensor("vtm", [128, NKP * DM], BF16))
        p_sb = es.enter_context(nc.sbuf_tensor("p_sb", [128, NKP * TGS], BF16))
        qg_sb = es.enter_context(nc.sbuf_tensor("qg_sb", [128, CC * TGS], BF16))
        xg_sb = es.enter_context(nc.sbuf_tensor("xg_sb", [128, CC * TGS], BF16))
        vfm_sb = es.enter_context(nc.sbuf_tensor("vfm", [128, TGS], BF16))
        xtm = [es.enter_context(nc.sbuf_tensor(f"xtm{i}", [128, DM], BF16)) for i in range(2)]
        wsl = [es.enter_context(nc.sbuf_tensor(f"wsl{i}", [128, CC * 128], BF16)) for i in range(2)]
        idb = es.enter_context(nc.sbuf_tensor("idb", [128, 128], BF16))
        ones_c = es.enter_context(nc.sbuf_tensor("ones_c", [128, 1], BF16))
        ones_r = es.enter_context(nc.sbuf_tensor("ones_r", [1, 128], F32))
        stat = es.enter_context(nc.sbuf_tensor("stat", [1, 8], F32))
        negb_sb = es.enter_context(nc.sbuf_tensor("negb_sb", [128, 1], F32))
        recd_sb = es.enter_context(nc.sbuf_tensor("recd_sb", [128, NSUB], F32))
        xres = [es.enter_context(nc.sbuf_tensor(f"xres{i}", [128, DM], F32)) for i in range(2)]
        ln = LnExit(nc, es, "L")
        sqg = ln.sq_sb
        A = [es.enter_context(nc.psum_tensor(f"A{i}", [128, 512], F32)) for i in range(2)]
        BIG = [es.enter_context(nc.psum_tensor(f"BIG{i}", [128, DM], F32)) for i in range(2)]
        ps_tr = [es.enter_context(nc.psum_tensor(f"pstr{i}", [128, 128], BF16)) for i in range(2)]

        gp = bld.pool("gp", 6)
        wp = bld.pool("wp", 4)
        yp = bld.pool("yp", 6)
        tok_const = []
        tok_xt = {}
        tok_w = {}
        tok_xres = {}
        tok_store = {}
        TRS = []
        xlast = {}
        vlast = {}

        PERB = NTG * (A1G + A2G)

        def a1(b, tg):
            return b * PERB + tg * A1G

        def a2(b, qg):
            return b * PERB + NTG * A1G + qg * A2G

        def big2(b, qg):
            return (b * NTG + qg) * NSUB

        def wl_k(b, tg, dc):
            return (b, tg * 16 + dc)

        def wl_v(b, tg, dc):
            return (b, tg * 16 + 8 + dc)

        def wl_q(b, qg, dc):
            return (b, 128 + qg * CC + dc)

        WMARK = {}

        def t_body(t):
            t.wait(*tok_const)
            trj = [0]

            def transpose(kind, key, src_sb, src_col, dst_col, waitfor=None):
                j = trj[0]
                if t.bld.pass0:
                    TRS.append((kind, dst_col))
                if j >= 2:
                    t.wait(t.tok(("trv3", j - 2)))
                if waitfor is not None:
                    t.wait(waitfor)
                t.op("transpose", ps_tr[j % 2][:],
                     src_sb[:, src_col:src_col + 128], idb[:],
                     tick=True, mark=("tr3", j))
                if t.bld.pass0:
                    key[0][key[1]] = j
                trj[0] += 1

            if t.bld.pass0:
                TRS.clear()
            for b in range(BPC):
                for tg in range(NTG):
                    for sub in range(NSUB):
                        tokx = tok_xt.get((0, b, tg, sub))
                        for c in range(CC):
                            transpose("x", (xlast, (0, b, tg, sub)),
                                      xtm[(b * 2 * NT + tg * NSUB + sub) % 2],
                                      c * 128, c * TGS + sub * 128,
                                      waitfor=(tokx if c == 0 else None))
                    for dc in range(CC):
                        ai = a1(b, tg) + dc
                        t.wait(tok_w.get(wl_k(b, tg, dc)))
                        if ai >= 2:
                            t.wait(t.tok(("Aev", ai - 2)))
                        t.wait(t.tok(("trv3", xlast.get((0, b, tg, NSUB - 1), 0))))
                        for c in range(CC):
                            t.op("matmul", A[ai % 2][:],
                                 wsl[(tg * 16 + dc) % 2][:, c * 128:(c + 1) * 128],
                                 xg_sb[:, c * TGS:(c + 1) * TGS],
                                 start=(c == 0), stop=(c == CC - 1),
                                 tick=(c == CC - 1), mark=(("Agrp", ai) if c == CC - 1 else None))
                    for dc in range(CC):
                        ai = a1(b, tg) + 8 + dc
                        t.wait(tok_w.get(wl_v(b, tg, dc)))
                        if ai >= 2:
                            t.wait(t.tok(("Aev", ai - 2)))
                        for c in range(CC):
                            t.op("matmul", A[ai % 2][:],
                                 wsl[(tg * 16 + 8 + dc) % 2][:, c * 128:(c + 1) * 128],
                                 xg_sb[:, c * TGS:(c + 1) * TGS],
                                 start=(c == 0), stop=(c == CC - 1),
                                 tick=(c == CC - 1), mark=(("Agrp", ai) if c == CC - 1 else None))
                        for sub in range(NSUB):
                            transpose("v", (vlast, (b, tg, dc)),
                                      vfm_sb, sub * 128,
                                      (tg * NSUB + sub) * DM + dc * 128,
                                      waitfor=(t.tok(("vev", b, tg, dc)) if sub == 0 else None))
                    ai = a1(b, tg) + 16
                    if ai >= 2:
                        t.wait(t.tok(("Aev", ai - 2)))
                    for dc in range(CC):
                        t.wait(t.tok(("ksq", b, tg, dc)))
                        t.op("matmul", A[ai % 2][0:1, :], ones_c[:], sqg[:, 0:TGS],
                             start=(dc == 0), stop=(dc == CC - 1),
                             tick=True, mark=("nmm1", b, tg, dc))
                    bld.marks[("Agrp", ai)] = bld.marks[("nmm1", b, tg, CC - 1)]
                for qg in range(NTG):
                    if qg == 0:
                        t.wait(t.tok(("Aev", a1(b, NTG - 1) + 7)))
                        t.wait(t.tok(("trv3", vlast.get((b, NTG - 1, CC - 1), 0))))
                    for sub in range(NSUB):
                        tokx = tok_xt.get((1, b, qg, sub))
                        for c in range(CC):
                            transpose("x", (xlast, (1, b, qg, sub)),
                                      xtm[(b * 2 * NT + NT + qg * NSUB + sub) % 2],
                                      c * 128, c * TGS + sub * 128,
                                      waitfor=(tokx if c == 0 else None))
                    for dc in range(CC):
                        ai = a2(b, qg) + dc
                        t.wait(tok_w.get(wl_q(b, qg, dc)))
                        t.wait(t.tok(("Aev", ai - 2)))
                        t.wait(t.tok(("trv3", xlast.get((1, b, qg, NSUB - 1), 0))))
                        for c in range(CC):
                            t.op("matmul", A[ai % 2][:],
                                 wsl[(128 + qg * CC + dc) % 2][:, c * 128:(c + 1) * 128],
                                 xg_sb[:, c * TGS:(c + 1) * TGS],
                                 start=(c == 0), stop=(c == CC - 1),
                                 tick=(c == CC - 1), mark=(("Agrp", ai) if c == CC - 1 else None))
                    ai = a2(b, qg) + 8
                    t.wait(t.tok(("Aev", ai - 2)))
                    for dc in range(CC):
                        t.wait(t.tok(("qsq", b, qg, dc)))
                        t.op("matmul", A[ai % 2][0:1, :], ones_c[:], sqg[:, 0:TGS],
                             start=(dc == 0), stop=(dc == CC - 1),
                             tick=True, mark=("nmm2", b, qg, dc))
                    bld.marks[("Agrp", ai)] = bld.marks[("nmm2", b, qg, CC - 1)]
                    ai = a2(b, qg) + 9
                    t.wait(t.tok(("Aev", ai - 2)))
                    t.wait(t.tok(("negb", b, qg)))
                    t.op("matmul", A[ai % 2][:, 0:1], ones_r[:], stat[0:1, 3:4],
                         start=True, stop=True, tick=True, mark=("Agrp", ai))
                    for kc in range(NKP):
                        ai = a2(b, qg) + 10 + kc
                        t.wait(t.tok(("Aev", ai - 2)))
                        for dc in range(CC):
                            t.op("matmul", A[ai % 2][:],
                                 kfm[:, dc * S + kc * 128: dc * S + kc * 128 + 128],
                                 qg_sb[:, dc * TGS:(dc + 1) * TGS],
                                 start=(dc == 0), stop=(dc == CC - 1),
                                 tick=(dc == CC - 1), mark=(("Agrp", ai) if dc == CC - 1 else None))
                    t.wait(t.tok(("Aev", a2(b, qg) + 41)))
                    for sub in range(NSUB):
                        ai = a2(b, qg) + 42 + sub
                        t.wait(t.tok(("Aev", ai - 2)))
                        for kc in range(NKP):
                            t.op("matmul", A[ai % 2][:, 0:1],
                                 p_sb[:, kc * TGS + sub * 128: kc * TGS + sub * 128 + 128],
                                 ones_c[:],
                                 start=(kc == 0), stop=(kc == NKP - 1),
                                 tick=(kc == NKP - 1), mark=(("Agrp", ai) if kc == NKP - 1 else None))
                    for sub in range(NSUB):
                        bi = big2(b, qg) + sub
                        if bi >= 2:
                            t.wait(t.tok(("Bev", bi - 2)))
                        for kc in range(NKP):
                            for nh in range(2):
                                t.op("matmul", BIG[bi % 2][:, nh * 512:(nh + 1) * 512],
                                     p_sb[:, kc * TGS + sub * 128: kc * TGS + sub * 128 + 128],
                                     vtm[:, kc * DM + nh * 512: kc * DM + (nh + 1) * 512],
                                     start=(kc == 0), stop=(kc == NKP - 1),
                                     tick=(kc == NKP - 1 and nh == 1),
                                     mark=(("Bgrp", bi) if (kc == NKP - 1 and nh == 1) else None))

        def s_body(s):
            for b in range(BPC):
                for tg in range(NTG):
                    for dc in range(CC):
                        ai = a1(b, tg) + dc
                        s.wait(s.tok(("Agrp", ai)))
                        if (b, tg) != (0, 0) or dc >= 1:
                            pass
                        s.copy(kfm[:, dc * S + tg * TGS: dc * S + (tg + 1) * TGS],
                               A[ai % 2][:], mark=("Aev", ai))
                    for dc in range(CC):
                        ai = a1(b, tg) + 8 + dc
                        s.wait(s.tok(("Agrp", ai)))
                        if dc >= 1:
                            m = vlast.get((b, tg, dc - 1))
                            s.wait(s.tok(("tr3", m)) if m is not None else None)
                        s.copy(vfm_sb[:], A[ai % 2][:], mark=("Aev", ai))
                        bld.marks[("vev", b, tg, dc)] = bld.marks[("Aev", ai)]
                    for dc in range(CC):
                        if dc > 0:
                            s.wait(s.tok(("nmm1", b, tg, dc - 1)))
                        s.activation(sqg[:, 0:TGS],
                                     kfm[:, dc * S + tg * TGS: dc * S + (tg + 1) * TGS],
                                     AF.Square, mark=("ksq", b, tg, dc))
                for qg in range(NTG):
                    for dc in range(CC):
                        ai = a2(b, qg) + dc
                        s.wait(s.tok(("Agrp", ai)))
                        s.copy(qg_sb[:, dc * TGS:(dc + 1) * TGS],
                               A[ai % 2][:], mark=("Aev", ai))
                    for dc in range(CC):
                        if dc > 0:
                            s.wait(s.tok(("nmm2", b, qg, dc - 1)))
                        s.activation(sqg[:, 0:TGS],
                                     qg_sb[:, dc * TGS:(dc + 1) * TGS],
                                     AF.Square, mark=("qsq", b, qg, dc))
                    s.wait(s.tok(("b2", b, qg)))
                    s.activation(stat[0:1, 2:3], stat[0:1, 2:3], AF.Sqrt,
                                 mark=("sqrtb", b, qg))
                    for kc in range(NKP):
                        ai = a2(b, qg) + 10 + kc
                        s.wait(s.tok(("Agrp", ai)))
                        s.wait(s.tok(("negbev", b, qg)))
                        if (b, qg) != (0, 0) and kc == 0:
                            s.wait(s.tok(("Bgrp", big2(b, qg) - 1)))
                        s.activation(p_sb[:, kc * TGS:(kc + 1) * TGS],
                                     A[ai % 2][:], AF.Exp, bias=negb_sb[:, 0:1],
                                     scale=SCALE, mark=("Aev", ai))
                    for sub in range(NSUB):
                        ln.s_stats(s, b * NT + qg * NSUB + sub)

        def v_body2(v):
            trv = [0]

            def evacs(n):
                for _ in range(n):
                    j = trv[0]
                    kind, dst_col = TRS[j] if TRS else ("x", 0)
                    v.wait(v.tok(("tr3", j)))
                    dst = xg_sb if kind == "x" else vtm
                    v.tensor_copy(dst[:, dst_col:dst_col + 128], ps_tr[j % 2][:],
                                  mark=("trv3", j))
                    trv[0] += 1
            for b in range(BPC):
                for tg in range(NTG):
                    evacs(32)
                    ai = a1(b, tg) + 16
                    for dc in range(CC):
                        evacs(4)
                    v.wait(v.tok(("nmm1", b, tg, CC - 1)))
                    v.reduce_max(stat[0:1, 4:5], A[ai % 2][0:1, :], axis=AX.X,
                                 mark=("Aev", ai))
                    if tg == 0 and b == 0:
                        v.tensor_copy(stat[0:1, 0:1], stat[0:1, 4:5])
                    elif tg == 0:
                        v.tensor_copy(stat[0:1, 0:1], stat[0:1, 4:5])
                    else:
                        v.tensor_max(stat[0:1, 0:1], stat[0:1, 0:1], stat[0:1, 4:5])
                for qg in range(NTG):
                    evacs(32)
                    ai = a2(b, qg) + 8
                    v.wait(v.tok(("nmm2", b, qg, CC - 1)))
                    v.reduce_max(stat[0:1, 1:2], A[ai % 2][0:1, :], axis=AX.X,
                                 mark=("Aev", ai))
                    v.tensor_mul(stat[0:1, 2:3], stat[0:1, 0:1], stat[0:1, 1:2],
                                 mark=("b2", b, qg))
                    v.wait(v.tok(("sqrtb", b, qg)))
                    v.tensor_scalar_mul(stat[0:1, 3:4], stat[0:1, 2:3], -SCALE,
                                        mark=("negb", b, qg))
                    nbai = a2(b, qg) + 9
                    v.wait(v.tok(("Agrp", nbai)))
                    v.tensor_copy(negb_sb[:], A[nbai % 2][:, 0:1],
                                  mark=("negbev", b, qg))
                    bld.marks[("Aev", nbai)] = bld.marks[("negbev", b, qg)]
                    for sub in range(NSUB):
                        ai = a2(b, qg) + 42 + sub
                        v.wait(v.tok(("Agrp", ai)))
                        v.reciprocal(recd_sb[:, sub:sub + 1], A[ai % 2][:, 0:1],
                                     mark=("Aev", ai))
                    for sub in range(NSUB):
                        i = b * NT + qg * NSUB + sub
                        bi = big2(b, qg) + sub
                        v.wait(v.tok(("Bgrp", bi)))
                        v.wait(tok_xres.get(i))
                        if i >= 2:
                            v.wait(v.tok(("L", "sq", i - 2)))
                            v.wait(tok_store.get(i - 2))
                        v.scalar_tensor_tensor(
                            ln.sum_sb[i % 2][:], BIG[bi % 2][:],
                            recd_sb[:, sub:sub + 1], xres[i % 2][:],
                            ALU.mult, ALU.add, mark=("Bev", bi))
                        ln.v_stats(v, i)

        def g_body(g):
            tok_const.append(gp.load(g.eng, idb[:], ident[:, :]))
            tok_const.append(gp.load(g.eng, ones_c[:], onescol[:, :]))
            tok_const.append(gp.load(g.eng, ones_r[:], onesrow[:, :]))
            nx = [0]

            def xload(ph, b, gg, sub):
                if nx[0] >= 2:
                    k = None
                    pb, pph, pgg, psub = xorder[nx[0] - 2]
                    m = xlast.get((pph, pb, pgg, psub))
                    if m is not None:
                        g.wait(g.tok(("tr3", m)))
                tok_xt[(ph, b, gg, sub)] = gp.load(
                    g.eng, xtm[nx[0] % 2][:],
                    x_in[b, (gg * NSUB + sub) * 128:(gg * NSUB + sub + 1) * 128, :])
                nx[0] += 1

            def wload(key, li_b, src_ap):
                b, li = key
                if li >= 2:
                    m = WMARK.get((b, li - 2))
                    if m is not None:
                        g.wait(g.tok(m))
                tok_w[key] = wp.load(
                    g.eng, wsl[li % 2].ap().rearrange("p (c f) -> p c f", c=CC),
                    src_ap)

            xorder = []
            for b in range(BPC):
                for ph in range(2):
                    for gg in range(NTG):
                        for sub in range(NSUB):
                            xorder.append((b, ph, gg, sub))
            xi = [0]

            def xnext():
                b, ph, gg, sub = xorder[xi[0]]
                xi[0] += 1
                xload(ph, b, gg, sub)

            for b in range(BPC):
                for tg in range(NTG):
                    for _ in range(NSUB):
                        xnext()
                    for dc in range(CC):
                        key = wl_k(b, tg, dc)
                        WMARK[key] = ("Agrp", a1(b, tg) + dc)
                        wload(key, None,
                              wqkvT[:, DM + dc * 128: DM + (dc + 1) * 128]
                              .rearrange("(c p) f -> p c f", p=128))
                    for dc in range(CC):
                        key = wl_v(b, tg, dc)
                        WMARK[key] = ("Agrp", a1(b, tg) + 8 + dc)
                        wload(key, None,
                              wqkvT[:, 2 * DM + dc * 128: 2 * DM + (dc + 1) * 128]
                              .rearrange("(c p) f -> p c f", p=128))
                for qg in range(NTG):
                    for _ in range(NSUB):
                        xnext()
                    for dc in range(CC):
                        key = wl_q(b, qg, dc)
                        WMARK[key] = ("Agrp", a2(b, qg) + dc)
                        wload(key, None,
                              wqkvT[:, dc * 128:(dc + 1) * 128]
                              .rearrange("(c p) f -> p c f", p=128))

        def y_body(y):
            NST = BPC * NT
            for b in range(BPC):
                for i_loc in range(NT):
                    i = b * NT + i_loc
                    if i >= 2:
                        y.wait(y.tok(("L", "sum", i - 2)))
                    tok_xres[i] = yp.load(
                        y.eng, xres[i % 2][:],
                        x_in[b, i_loc * 128:(i_loc + 1) * 128, :])
                    if i >= 2:
                        j = i - 2
                        bb, ss = divmod(j, NT)
                        tok_store[j] = ln.store(y, yp, j,
                                                x_out[bb, ss * 128:(ss + 1) * 128, :])
            for j in (NST - 2, NST - 1):
                bb, ss = divmod(j, NT)
                tok_store[j] = ln.store(y, yp, j, x_out[bb, ss * 128:(ss + 1) * 128, :])
            if y.eng is not None:
                for tkn in yp.snapshot():
                    y.eng.wait_ge(tkn[1].sems[tkn[2]], tkn[3])

        bld.run_stage(nc, es, ges, [("t", t_body), ("s", s_body),
                                    ("v", v_body2), ("g", g_body),
                                    ("y", y_body)])


def build_stage3():
    if "s3" in _cache:
        return _cache["s3"]
    nc0 = bass.Bass("TRN2", target_bir_lowering=False, debug=False)
    nc = _NcAlias(nc0)
    ges = contextlib.ExitStack()
    with ges:
        _emit_stage3(nc, ges, Build(), _s3_drams(nc))
    _cache["s3"] = nc0
    return nc0



def _s4_drams(nc):
    return dict(
        x_in=nc.dram_tensor("x_in", [S, DM], F32, kind="ExternalInput").ap(),
        wcat1=nc.dram_tensor("wcat1", [17, DM, 2048], F32, kind="ExternalInput").ap(),
        wcat2=nc.dram_tensor("wcat2", [17, 2048, DM], F32, kind="ExternalInput").ap(),
        b1cat=nc.dram_tensor("b1cat", [128, 17 * 16], F32, kind="ExternalInput").ap(),
        sb2b=nc.dram_tensor("sb2b", [128, DM], F32, kind="ExternalInput").ap(),
        rb2=nc.dram_tensor("rb2", [15, DM], F32, kind="ExternalInput").ap(),
        gw=nc.dram_tensor("gw", [DM, 15], F32, kind="ExternalInput").ap(),
        gbb=nc.dram_tensor("gbb", [128, 15], F32, kind="ExternalInput").ap(),
        ident=nc.dram_tensor("ident", [128, 128], F32, kind="ExternalInput").ap(),
        x_out=nc.dram_tensor("x_out", [S, DM], F32, kind="ExternalOutput").ap())


def _emit_stage4(nc, ges, bld, A):
    x_in = A["x_in"]; wcat1 = A["wcat1"]; wcat2 = A["wcat2"]
    b1cat = A["b1cat"]; sb2b = A["sb2b"]; rb2 = A["rb2"]; gw = A["gw"]
    gbb = A["gbb"]; ident = A["ident"]; x_out = A["x_out"]

    BLKT = 1024
    NBLK = S // BLKT
    NSUB = BLKT // 128
    NE = 17
    FC = 2048 // 128
    NTSUB = S // 128

    es = contextlib.ExitStack()
    with es:
        xfm = es.enter_context(nc.sbuf_tensor("xfm", [128, CC * BLKT], BF16))
        xtm = [es.enter_context(nc.sbuf_tensor(f"xtm{i}", [128, DM], BF16)) for i in range(2)]
        acc = es.enter_context(nc.sbuf_tensor("acc", [128, NSUB * DM], F32))
        h_sb = es.enter_context(nc.sbuf_tensor("h_sb", [128, FC * BLKT], BF16))
        w2_sb = es.enter_context(nc.sbuf_tensor("w2_sb", [128, FC * DM], BF16))
        wsl = [es.enter_context(nc.sbuf_tensor(f"wsl{i}", [128, CC * 128], BF16)) for i in range(2)]
        gated = es.enter_context(nc.sbuf_tensor("gated", [128, NSUB * 15], F32))
        gtb = es.enter_context(nc.sbuf_tensor("gtb", [15, BLKT], BF16))
        wso = es.enter_context(nc.sbuf_tensor("wso", [128, 15], F32))
        wwk = es.enter_context(nc.sbuf_tensor("wwk", [128, 15], F32))
        wmk = es.enter_context(nc.sbuf_tensor("wmk", [128, 15], F32))
        gst = es.enter_context(nc.sbuf_tensor("gst", [128, 4], F32))
        gw_sb = es.enter_context(nc.sbuf_tensor("gw_sb", [128, CC * 15], BF16))
        rb2_sb = es.enter_context(nc.sbuf_tensor("rb2_sb", [15, DM], BF16))
        sb2_sb = es.enter_context(nc.sbuf_tensor("sb2_sb", [128, DM], BF16))
        b1_sb = es.enter_context(nc.sbuf_tensor("b1_sb", [128, 17 * 16], F32))
        gb_sb = es.enter_context(nc.sbuf_tensor("gb_sb", [128, 15], F32))
        idb = es.enter_context(nc.sbuf_tensor("idb", [128, 128], BF16))
        idf = es.enter_context(nc.sbuf_tensor("idf", [128, 128], F32))
        xres = [es.enter_context(nc.sbuf_tensor(f"xres{i}", [128, DM], F32)) for i in range(2)]
        ln = LnExit(nc, es, "L")
        A = [es.enter_context(nc.psum_tensor(f"A{i}", [128, 512], F32)) for i in range(2)]
        BIG = [es.enter_context(nc.psum_tensor(f"BIG{i}", [128, DM], F32)) for i in range(2)]
        ps_tr = [es.enter_context(nc.psum_tensor(f"pstr{i}", [128, 128], BF16)) for i in range(2)]

        gp = bld.pool("gp", 6)
        wp = bld.pool("wp", 6)
        yp = bld.pool("yp", 6)
        tok_const = []
        tok_xt = {}
        tok_w1 = {}
        tok_w2 = {}
        tok_xres = {}
        tok_store = {}

        APB = 16 + 32 * NE

        def ag_log(blk, sub):
            return blk * APB + sub

        def ag_gt(blk, sub):
            return blk * APB + 8 + sub

        def ag_h(blk, e, g):
            return blk * APB + 16 + e * 32 + g

        BPB = (NE + 1) * NSUB

        def bg(blk, e, sub):
            return blk * BPB + e * NSUB + sub

        def t_body(t):
            t.wait(*tok_const)
            trj = [0]
            for blk in range(NBLK):
                for sub in range(NSUB):
                    nx = blk * NSUB + sub
                    t.wait(tok_xt.get((blk, sub)))
                    for c in range(CC):
                        j = trj[0]
                        if j >= 2:
                            t.wait(t.tok(("trv4", j - 2)))
                        t.op("transpose", ps_tr[j % 2][:],
                             xtm[nx % 2][:, c * 128:(c + 1) * 128], idb[:],
                             tick=True, mark=("tr4", j))
                        trj[0] += 1
                for sub in range(NSUB):
                    ai = ag_log(blk, sub)
                    if ai >= 2:
                        t.wait(t.tok(("Aev", ai - 2)))
                    t.wait(t.tok(("trv4", blk * NSUB * CC + sub * CC + CC - 1)))
                    for c in range(CC):
                        t.op("matmul", A[ai % 2][0:128, 0:15],
                             xfm[:, c * BLKT + sub * 128: c * BLKT + sub * 128 + 128],
                             gw_sb[:, c * 15:(c + 1) * 15],
                             start=(c == 0), stop=(c == CC - 1),
                             tick=(c == CC - 1), mark=(("Agrp", ai) if c == CC - 1 else None))
                for sub in range(NSUB):
                    ai = ag_gt(blk, sub)
                    t.wait(t.tok(("Aev", ai - 2)))
                    t.wait(t.tok(("gtm", blk, sub)))
                    t.op("transpose", A[ai % 2][0:15, 0:128],
                         gated[:, sub * 15:(sub + 1) * 15], idf[:],
                         tick=True, mark=("Agrp", ai))
                for e in range(NE):
                    for fc in range(FC):
                        li = e * FC + fc
                        t.wait(tok_w1.get((blk, li)))
                        for th in range(2):
                            ai = ag_h(blk, e, fc * 2 + th)
                            t.wait(t.tok(("Aev", ai - 2)))
                            for c in range(CC):
                                t.op("matmul", A[ai % 2][:],
                                     wsl[li % 2][:, c * 128:(c + 1) * 128],
                                     xfm[:, c * BLKT + th * 512: c * BLKT + (th + 1) * 512],
                                     start=(c == 0), stop=(c == CC - 1),
                                     tick=(c == CC - 1), mark=(("Agrp", ai) if c == CC - 1 else None))
                    for sub in range(NSUB):
                        bi = bg(blk, e, sub)
                        if bi >= 2:
                            t.wait(t.tok(("Bev", bi - 2)))
                        if sub == 0:
                            for fc in range(FC):
                                t.wait(tok_w2.get((blk, e, fc)))
                            t.wait(t.tok(("hev", blk, e, FC * 2 - 1)))
                        for fc in range(FC):
                            for nh in range(2):
                                t.op("matmul", BIG[bi % 2][:, nh * 512:(nh + 1) * 512],
                                     h_sb[:, fc * BLKT + sub * 128: fc * BLKT + sub * 128 + 128],
                                     w2_sb[:, fc * DM + nh * 512: fc * DM + (nh + 1) * 512],
                                     start=(fc == 0), stop=(fc == FC - 1),
                                     tick=(fc == FC - 1 and nh == 1),
                                     mark=(("Bgrp", bi) if (fc == FC - 1 and nh == 1) else None))
                for sub in range(NSUB):
                    bi = bg(blk, NE, sub)
                    t.wait(t.tok(("Bev", bi - 2)))
                    if sub == 0:
                        t.wait(t.tok(("gtev", blk, NSUB - 1)))
                    for nh in range(2):
                        t.op("matmul", BIG[bi % 2][:, nh * 512:(nh + 1) * 512],
                             gtb[:, sub * 128:(sub + 1) * 128],
                             rb2_sb[:, nh * 512:(nh + 1) * 512],
                             start=True, stop=True,
                             tick=(nh == 1), mark=(("Bgrp", bi) if nh == 1 else None))

        def s_body(s):
            for blk in range(NBLK):
                for sub in range(NSUB):
                    s.wait(s.tok(("sfte", blk, sub)))
                    s.activation(wmk[:], wwk[:], AF.Exp,
                                 accum_out=gst[:, 1:2], mark=("sftx", blk, sub))
                for e in range(NE):
                    for fc in range(FC):
                        for th in range(2):
                            g = fc * 2 + th
                            ai = ag_h(blk, e, g)
                            s.wait(s.tok(("Agrp", ai)))
                            if g == 0 and (blk > 0 or e > 0):
                                pe = (blk, e - 1) if e > 0 else (blk - 1, NE - 1)
                                s.wait(s.tok(("Bgrp", bg(pe[0], pe[1], NSUB - 1))))
                            s.activation(
                                h_sb[:, fc * BLKT + th * 512: fc * BLKT + (th + 1) * 512],
                                A[ai % 2][:], AF.Gelu,
                                bias=b1_sb[:, e * 16 + fc: e * 16 + fc + 1],
                                mark=("Aev", ai))
                            bld.marks[("hev", blk, e, g)] = bld.marks[("Aev", ai)]
                for sub in range(NSUB):
                    ln.s_stats(s, blk * NSUB + sub)

        def v_body(v):
            TRV = []
            for blk in range(NBLK):
                for sub in range(NSUB):
                    for c in range(CC):
                        j = blk * NSUB * CC + sub * CC + c
                        v.wait(v.tok(("tr4", j)))
                        v.tensor_copy(
                            xfm[:, c * BLKT + sub * 128: c * BLKT + sub * 128 + 128],
                            ps_tr[j % 2][:], mark=("trv4", j))
                for sub in range(NSUB):
                    ai = ag_log(blk, sub)
                    v.wait(v.tok(("Agrp", ai)))
                    v.tensor_add(wso[:], A[ai % 2][0:128, 0:15], gb_sb[:],
                                 mark=("Aev", ai))
                    v.reduce_max(gst[:, 0:1], wso[:], axis=AX.X)
                    v.tensor_scalar(wwk[:], wso[:], gst[:, 0:1], None, ALU.subtract,
                                    mark=("sfte", blk, sub))
                    v.wait(v.tok(("sftx", blk, sub)))
                    v.reciprocal(gst[:, 2:3], gst[:, 1:2])
                    v.tensor_scalar(wso[:], wmk[:], gst[:, 2:3], None, ALU.mult)
                    v.tensor_copy(wwk[:], wso[:])
                    for r in range(3):
                        v.reduce_max(gst[:, 3:4], wwk[:], axis=AX.X)
                        v.tensor_scalar(wmk[:], wwk[:], gst[:, 3:4], None, ALU.is_ge)
                        if r < 2:
                            v.scalar_tensor_tensor(wwk[:], wmk[:], -1e9, wwk[:],
                                                   ALU.mult, ALU.add)
                    v.tensor_scalar(wmk[:], wso[:], gst[:, 3:4], None, ALU.is_ge)
                    v.tensor_mul(gated[:, sub * 15:(sub + 1) * 15], wmk[:], wso[:],
                                 mark=("gtm", blk, sub))
                for sub in range(NSUB):
                    ai = ag_gt(blk, sub)
                    v.wait(v.tok(("Agrp", ai)))
                    v.tensor_copy(gtb[:, sub * 128:(sub + 1) * 128],
                                  A[ai % 2][0:15, 0:128], mark=("Aev", ai))
                    bld.marks[("gtev", blk, sub)] = bld.marks[("Aev", ai)]
                for e in range(NE):
                    for sub in range(NSUB):
                        bi = bg(blk, e, sub)
                        v.wait(v.tok(("Bgrp", bi)))
                        a_sl = acc[:, sub * DM:(sub + 1) * DM]
                        if e == 0:
                            v.tensor_add(a_sl, BIG[bi % 2][:], sb2_sb[:],
                                         mark=("Bev", bi))
                        elif e == 1:
                            v.scalar_tensor_tensor(a_sl, BIG[bi % 2][:], 1.0,
                                                   a_sl, ALU.mult, ALU.add,
                                                   mark=("Bev", bi))
                        else:
                            v.scalar_tensor_tensor(
                                a_sl, BIG[bi % 2][:],
                                gated[:, sub * 15 + (e - 2): sub * 15 + (e - 1)],
                                a_sl, ALU.mult, ALU.add, mark=("Bev", bi))
                for sub in range(NSUB):
                    bi = bg(blk, NE, sub)
                    i = blk * NSUB + sub
                    v.wait(v.tok(("Bgrp", bi)))
                    a_sl = acc[:, sub * DM:(sub + 1) * DM]
                    v.scalar_tensor_tensor(a_sl, BIG[bi % 2][:], 1.0, a_sl,
                                           ALU.mult, ALU.add, mark=("Bev", bi))
                    v.wait(tok_xres.get(i))
                    if i >= 2:
                        v.wait(v.tok(("L", "sq", i - 2)))
                        v.wait(tok_store.get(i - 2))
                    v.scalar_tensor_tensor(ln.sum_sb[i % 2][:], xres[i % 2][:],
                                           2.0, a_sl, ALU.mult, ALU.add)
                    ln.v_stats(v, i)

        def g_body(g):
            tok_const.append(gp.load(g.eng, idb[:], ident[:, :]))
            for c in range(CC):
                tok_const.append(gp.load(g.eng, gw_sb[:, c * 15:(c + 1) * 15],
                                         gw[c * 128:(c + 1) * 128, :]))
            tok_const.append(gp.load(g.eng, rb2_sb[:], rb2[:, :]))
            tok_const.append(gp.load(g.eng, sb2_sb[:], sb2b[:, :]))
            for blk in range(NBLK):
                for sub in range(NSUB):
                    nx = blk * NSUB + sub
                    if nx >= 2:
                        g.wait(g.tok(("tr4", (nx - 1) * CC - 1)))
                    tok_xt[(blk, sub)] = gp.load(
                        g.eng, xtm[nx % 2][:],
                        x_in[(blk * NSUB + sub) * 128:(blk * NSUB + sub + 1) * 128, :])
                for e in range(NE):
                    for fc in range(FC):
                        li = e * FC + fc
                        gli = blk * NE * FC + li
                        if gli >= 2:
                            pblk, pli = divmod(gli - 2, NE * FC)
                            pe_, pfc = divmod(pli, FC)
                            m = ("Agrp", ag_h(pblk, pe_, pfc * 2 + 1))
                            g.wait(g.tok(m))
                        tok_w1[(blk, li)] = wp.load(
                            g.eng, wsl[li % 2].ap().rearrange("p (c f) -> p c f", c=CC),
                            wcat1[e, :, fc * 128:(fc + 1) * 128]
                            .rearrange("(c p) f -> p c f", p=128))
                    for fc in range(FC):
                        if blk > 0 or e > 0:
                            pe = (blk, e - 1) if e > 0 else (blk - 1, NE - 1)
                            g.wait(g.tok(("Bgrp", bg(pe[0], pe[1], NSUB - 1))))
                        tok_w2[(blk, e, fc)] = wp.load(
                            g.eng, w2_sb[:, fc * DM:(fc + 1) * DM],
                            wcat2[e, fc * 128:(fc + 1) * 128, :])

        def y_body(y):
            for i in range(NTSUB):
                if i >= 2:
                    y.wait(y.tok(("L", "sum", i - 2)))
                tok_xres[i] = yp.load(
                    y.eng, xres[i % 2][:], x_in[i * 128:(i + 1) * 128, :])
                if i >= 2:
                    j = i - 2
                    tok_store[j] = ln.store(y, yp, j,
                                            x_out[j * 128:(j + 1) * 128, :])
            for j in (NTSUB - 2, NTSUB - 1):
                tok_store[j] = ln.store(y, yp, j, x_out[j * 128:(j + 1) * 128, :])
            if y.eng is not None:
                for tkn in yp.snapshot():
                    y.eng.wait_ge(tkn[1].sems[tkn[2]], tkn[3])
        def y0_body(y):
            tok_const.append(yp.load(y.eng, idf[:], ident[:, :]))
            tok_const.append(yp.load(y.eng, b1_sb[:], b1cat[:, :]))
            tok_const.append(yp.load(y.eng, gb_sb[:], gbb[:, :]))

        bld.run_stage(nc, es, ges, [("t", t_body), ("s", s_body),
                                    ("v", v_body), ("g", g_body),
                                    ("y", y0_body), ("y", y_body)])


def build_stage4():
    if "s4" in _cache:
        return _cache["s4"]
    nc0 = bass.Bass("TRN2", target_bir_lowering=False, debug=False)
    nc = _NcAlias(nc0)
    ges = contextlib.ExitStack()
    with ges:
        _emit_stage4(nc, ges, Build(), _s4_drams(nc))
    _cache["s4"] = nc0
    return nc0



def _s1_drams(nc):
    return dict(
        x_in=nc.dram_tensor("x_in", [S, DM], F32, kind="ExternalInput").ap(),
        wqT=nc.dram_tensor("wqT", [DM, DM], F32, kind="ExternalInput").ap(),
        wkT=nc.dram_tensor("wkT", [DM, DM], F32, kind="ExternalInput").ap(),
        wvT=nc.dram_tensor("wvT", [DM, DM], F32, kind="ExternalInput").ap(),
        wo=nc.dram_tensor("wo", [DM, DM], F32, kind="ExternalInput").ap(),
        wck=nc.dram_tensor("wck", [128, 32 * 64], F32, kind="ExternalInput").ap(),
        wcv=nc.dram_tensor("wcv", [128, 32 * 64], F32, kind="ExternalInput").ap(),
        bck=nc.dram_tensor("bck", [64, 1], F32, kind="ExternalInput").ap(),
        bcv=nc.dram_tensor("bcv", [64, 1], F32, kind="ExternalInput").ap(),
        wgm=nc.dram_tensor("wgm", [128, 3], F32, kind="ExternalInput").ap(),
        bgm=nc.dram_tensor("bgm", [1, 3], F32, kind="ExternalInput").ap(),
        amat=nc.dram_tensor("amat", [2, 128, S], F32, kind="ExternalInput").ap(),
        ident=nc.dram_tensor("ident", [128, 128], F32, kind="ExternalInput").ap(),
        onescol=nc.dram_tensor("onescol", [128, 1], F32, kind="ExternalInput").ap(),
        onesrow=nc.dram_tensor("onesrow", [1, 128], F32, kind="ExternalInput").ap(),
        x_out=nc.dram_tensor("x_out", [S, DM], F32, kind="ExternalOutput").ap())


def _emit_stage1(nc, ges, bld, A):
    x_in = A["x_in"]; wqT = A["wqT"]; wkT = A["wkT"]; wvT = A["wvT"]
    wo = A["wo"]; wck = A["wck"]; wcv = A["wcv"]; bck = A["bck"]
    bcv = A["bcv"]; wgm = A["wgm"]; bgm = A["bgm"]; amat = A["amat"]
    ident = A["ident"]; onescol = A["onescol"]; onesrow = A["onesrow"]
    x_out = A["x_out"]

    NT = S // 128
    TGS = 512
    NTG = S // TGS
    NSUB = TGS // 128
    NH = 16
    HP = 8
    NB = 255
    NKC = 32
    SC = 0.125

    es = contextlib.ExitStack()
    with es:
        kfm = es.enter_context(nc.sbuf_tensor("kfm", [128, CC * S], BF16))
        vtm = es.enter_context(nc.sbuf_tensor("vtm", [128, NKC * DM], BF16))
        vwin = es.enter_context(nc.sbuf_tensor("vwin", [128, 4 * DM], BF16))
        mcol_sb = es.enter_context(nc.sbuf_tensor("mcol_sb", [128, NKC], F32))
        mw2 = es.enter_context(nc.sbuf_tensor("mw2", [128, 2 * NKC], BF16))
        dcol_sb = es.enter_context(nc.sbuf_tensor("dcol_sb", [128, 8], F32))
        ones16 = es.enter_context(nc.sbuf_tensor("ones16", [128, 16], BF16))
        qg_sb = es.enter_context(nc.sbuf_tensor("qg_sb", [128, CC * TGS], BF16))
        xg_sb = es.enter_context(nc.sbuf_tensor("xg_sb", [128, CC * TGS], BF16))
        prT = [es.enter_context(nc.sbuf_tensor(f"prT{i}", [128, TGS], BF16)) for i in range(2)]
        ptr_sb = [es.enter_context(nc.sbuf_tensor(f"ptr{i}", [128, 128], BF16)) for i in range(2)]
        oq_sb = es.enter_context(nc.sbuf_tensor("oq_sb", [128, NSUB * NH * 64], BF16))
        ofm = oq_sb
        xtm = [es.enter_context(nc.sbuf_tensor(f"xtm{i}", [128, DM], BF16)) for i in range(2)]
        wsl = [es.enter_context(nc.sbuf_tensor(f"wsl{i}", [128, CC * 128], BF16)) for i in range(2)]
        wosl = [es.enter_context(nc.sbuf_tensor(f"wosl{i}", [128, 512], BF16)) for i in range(2)]
        wckb = es.enter_context(nc.sbuf_tensor("wckb", [128, 32 * 64], BF16))
        wcvb = es.enter_context(nc.sbuf_tensor("wcvb", [128, 32 * 64], BF16))
        bck_sb = es.enter_context(nc.sbuf_tensor("bck_sb", [64, 1], F32))
        bcv_sb = es.enter_context(nc.sbuf_tensor("bcv_sb", [64, 1], F32))
        wg_sb = es.enter_context(nc.sbuf_tensor("wg_sb", [128, 3], BF16))
        bg_sb = es.enter_context(nc.sbuf_tensor("bg_sb", [1, 3], F32))
        vwork = es.enter_context(nc.sbuf_tensor("vwork", [128, TGS + 16], BF16))
        vtail = es.enter_context(nc.sbuf_tensor("vtail", [128, HP * 16], BF16))
        ck_sb = es.enter_context(nc.sbuf_tensor("ck_sb", [128, HP * NB], BF16))
        cv_sb = es.enter_context(nc.sbuf_tensor("cv_sb", [128, HP * NB], BF16))
        pc_sb = None
        crow = None
        cvbm = es.enter_context(nc.sbuf_tensor("cvbm", [128, NH * 128], BF16))
        idb = es.enter_context(nc.sbuf_tensor("idb", [128, 128], BF16))
        ones_c = es.enter_context(nc.sbuf_tensor("ones_c", [128, 1], BF16))
        ones_rb = es.enter_context(nc.sbuf_tensor("ones_rb", [1, 128], BF16))
        imp_sb = es.enter_context(nc.sbuf_tensor("imp_sb", [128, 2], F32))
        impb = es.enter_context(nc.sbuf_tensor("impb", [128, 2], BF16))
        impc = es.enter_context(nc.sbuf_tensor("impc", [128, 2], F32))
        impc2 = es.enter_context(nc.sbuf_tensor("impc2", [128, 2], F32))
        implo = es.enter_context(nc.sbuf_tensor("implo", [128, 2], BF16))
        ind_sb = es.enter_context(nc.sbuf_tensor("ind_sb", [128, 2], BF16))
        rank_sb = es.enter_context(nc.sbuf_tensor("rank_sb", [128, 2], F32))
        qm_sb = es.enter_context(nc.sbuf_tensor("qm_sb", [128, HP], F32))
        qmb = es.enter_context(nc.sbuf_tensor("qmb", [128, HP], BF16))
        qmt = es.enter_context(nc.sbuf_tensor("qmt", [128, 1], F32))
        gates = es.enter_context(nc.sbuf_tensor("gates", [1, 64], BF16))
        gcol = es.enter_context(nc.sbuf_tensor("gcol", [128, 64], F32))
        dst = es.enter_context(nc.sbuf_tensor("dst", [128, 16], F32))
        brwb = es.enter_context(nc.sbuf_tensor("brwb", [1, TGS], BF16))
        imp_row = brwb
        stat1 = es.enter_context(nc.sbuf_tensor("stat1", [1, 8], F32))
        xres = es.enter_context(nc.sbuf_tensor("xres", [128, DM], F32))
        ln = LnExit(nc, es, "L", nbuf=1)
        sq_s = ln.sq_sb
        pc_sb = cv_sb
        crow = cv_sb.ap()[:, 256:512]
        psS = [es.enter_context(nc.psum_tensor(f"psS{i}", [128, 512], F32)) for i in range(2)]
        psG = [es.enter_context(nc.psum_tensor(f"psG{i}", [128, 512], F32)) for i in range(2)]
        psOW = [es.enter_context(nc.psum_tensor(f"psOW{i}", [128, 512], F32)) for i in range(2)]
        ps_tr = [es.enter_context(nc.psum_tensor(f"pstr{i}", [128, 256], BF16)) for i in range(2)]

        gp = bld.pool("gp", 6)
        wp = bld.pool("wp", 6)
        yp = bld.pool("yp", 6)
        tok_const = []
        tok_xt = {}
        tok_w = {}
        tok_woall = {}
        tok_xres = {}
        tok_store = {}

        def blocks_of_tg(tg):
            n0 = 0 if tg == 0 else 32 * tg - 1
            n1 = min(32 * tg + 30, NB - 1)
            return n0, n1 - n0 + 1

        xord = []
        for tg in range(NTG):
            for sub in range(NSUB):
                xord.append(("a0", tg, sub))
        for ph in ("a1", "b"):
            for qg in range(NTG):
                for sub in range(NSUB):
                    xord.append((ph, qg, sub))

        def t_body(t):
            t.wait(*tok_const)
            cnt = {"S": 0, "G": 0, "O": 0, "T": 0}

            def sgrp(emit, mark=None):
                i = cnt["S"]; cnt["S"] += 1
                if i >= 2:
                    t.wait(t.tok(("Sev", i - 2)))
                emit(psS[i % 2], i)
                bld.marks[("Sgrp", i)] = (bld.tickers["t"], bld.tickers["t"].n)
                if mark:
                    bld.marks[mark] = bld.marks[("Sgrp", i)]
                return i

            def ggrp(emit, mark=None, wait_list=()):
                i = cnt["G"]; cnt["G"] += 1
                if i >= 2:
                    t.wait(t.tok(("Gev", i - 2)))
                for wtk in wait_list:
                    t.wait(wtk)
                emit(psG[i % 2], i)
                bld.marks[("Ggrp", i)] = (bld.tickers["t"], bld.tickers["t"].n)
                if mark:
                    bld.marks[mark] = bld.marks[("Ggrp", i)]
                return i

            def transp(src_ap, width, mark=None, wait_list=(), rows=128, base=0):
                j = cnt["T"]; cnt["T"] += 1
                if j >= 2:
                    t.wait(t.tok(("Tev", j - 2)))
                for wtk in wait_list:
                    t.wait(wtk)
                t.op("transpose", ps_tr[j % 2][0:width, 0:rows],
                     src_ap, idb[base:base + rows, base:base + rows],
                     tick=True, mark=("Ttr", j))
                if mark:
                    bld.marks[mark] = bld.marks[("Ttr", j)]
                return j

            def xgroup(ph, gg, nxc):
                for sub in range(NSUB):
                    t.wait(tok_xt.get((ph, gg, sub)))
                    for c in range(CC):
                        transp(xtm[nxc[0] % 2][:, c * 128:(c + 1) * 128], 128,
                               ("xtr", ph, gg, sub, c))
                    nxc[0] += 1

            def proj_group(widx, mark):
                def em(ps, i):
                    for c in range(CC):
                        t.op("matmul", ps[:],
                             wsl[widx % 2][:, c * 128:(c + 1) * 128],
                             xg_sb[:, c * TGS:(c + 1) * TGS],
                             start=(c == 0), stop=(c == CC - 1),
                             tick=(c == CC - 1))
                sgrp(em, mark)

            nxc = [0]
            for tg in range(NTG):
                xgroup("a0", tg, nxc)
                t.wait(t.tok(("xev", "a0", tg)))
                for dc in range(CC):
                    t.wait(tok_w.get(("k", tg, dc)))
                    proj_group(dc, ("kg", tg, dc))
                for hp in range(HP):
                    t.wait(tok_w.get(("v", tg, hp)))
                    proj_group(hp, ("vg", tg, hp))
                    for sub in range(NSUB):
                        transp(vwork[:, 16 + sub * 128:16 + sub * 128 + 128], 128,
                               ("vtr", tg, hp, sub),
                               wait_list=([t.tok(("vev", tg, hp))] if sub == 0 else ()))
                    n0, nn = blocks_of_tg(tg)
                    off = 16 * n0 - 512 * tg + 16
                    for hh in range(2):
                        def em(ps, i, hh=hh, nn=nn, off=off):
                            for tt in range(32):
                                t.op("matmul", ps[0:64, 0:nn],
                                     wcvb[hh * 64:hh * 64 + 64,
                                          tt * 64:(tt + 1) * 64],
                                     vwork[hh * 64:hh * 64 + 64,
                                           off + tt: off + tt + 16 * (nn - 1) + 1:16],
                                     start=(tt == 0), stop=(tt == 31),
                                     tick=(tt == 31))
                        ggrp(em, ("cvg", tg, hp, hh),
                             wait_list=[t.tok(("vwork", tg, hp))] if hh == 0 else ())
            for h in range(NH):
                hb = (h % 2) * 64
                hc = h // 2

                def em(ps, i, hb=hb, hc=hc):
                    for tt in range(32):
                        t.op("matmul", ps[0:64, 0:NB],
                             wckb[hb:hb + 64, tt * 64:(tt + 1) * 64],
                             kfm[hb:hb + 64, hc * S + tt: hc * S + tt + 16 * (NB - 1) + 1:16],
                             start=(tt == 0), stop=(tt == 31),
                             tick=(tt == 31))
                ggrp(em, ("ckg", h),
                     wait_list=[t.tok(("kg", NTG - 1, CC - 1))] if h == 0 else ())
                for ch in range(2):
                    wdt = 128 if ch == 0 else 127
                    transp(cv_sb[hb:hb + 64, hc * NB + ch * 128: hc * NB + ch * 128 + wdt],
                           wdt, ("cvtr", h, ch), rows=64, base=hb,
                           wait_list=[t.tok(("cvev", NTG - 1, hc, h % 2))] if ch == 0 else ())
            for qg in range(NTG):
                xgroup("a1", qg, nxc)
                t.wait(t.tok(("xev", "a1", qg)))
                for dc in range(CC):
                    t.wait(tok_w.get(("qa", qg, dc)))
                    proj_group(dc, ("qga", qg, dc))
                for h in range(NH):
                    hb = (h % 2) * 64
                    hc = h // 2
                    for qt in range(NSUB):
                        def em(ps, i, hb=hb, hc=hc, qt=qt):
                            t.op("matmul", ps[:, 0:NB],
                                 qg_sb[hb:hb + 64, hc * TGS + qt * 128: hc * TGS + qt * 128 + 128],
                                 ck_sb[hb:hb + 64, hc * NB:(hc + 1) * NB],
                                 start=True, stop=True, tick=True)
                        wl = []
                        if h == 0 and qt == 0:
                            wl.append(t.tok(("qga_ev", qg, CC - 1)))
                        if qg == 0 and qt == 0:
                            wl.append(t.tok(("ckv", h)))
                        ggrp(em, ("csca", qg, h, qt), wait_list=wl)

                        def em(ps, i):
                            t.op("matmul", ps[0:128, 0:1], pc_sb[:, 0:128],
                                 ones_c[:], start=True, stop=True, tick=False)
                            t.op("matmul", ps[0:127, 1:2], pc_sb[:, 128:255],
                                 ones_c[:], start=True, stop=True, tick=True)
                        ggrp(em, ("impg", qg, h, qt),
                             wait_list=[t.tok(("pnca", qg, h, qt))])
            for h in range(NH):
                hb = (h % 2) * 64

                def em(ps, i, h=h, hb=hb):
                    t.op("matmul", ps[0:1, 0:3],
                         qmb[hb:hb + 64, (h // 2):(h // 2) + 1],
                         wg_sb[hb:hb + 64, :], start=True, stop=True, tick=True)
                ggrp(em, ("gmm", h),
                     wait_list=[t.tok(("qmrdy",))] if h == 0 else ())

            def em(ps, i):
                t.op("matmul", ps[0:128, 0:48], ones_rb[:, :], gates[0:1, 0:48],
                     start=True, stop=True, tick=True)
            ggrp(em, ("gbc",), wait_list=[t.tok(("grow",))])

            transp(impb[:, 0:1], 1, wait_list=[t.tok(("impbv",))])
            transp(impb[0:127, 1:2], 1, rows=127)
            transp(implo[:, 0:1], 1)
            transp(implo[0:127, 1:2], 1, rows=127)

            def em(ps, i):
                t.op("matmul", ps[0:128, 0:NB], ones_rb[:, :], imp_row[0:1, 0:NB],
                     start=True, stop=True, tick=True)
            ggrp(em, ("impbc",), wait_list=[t.tok(("improw",))])

            def em(ps, i):
                t.op("matmul", ps[0:128, 0:NB], ones_rb[:, :],
                     imp_row[0:1, 256:256 + NB],
                     start=True, stop=True, tick=True)
            ggrp(em, ("impbc2",))
            def em(ps, i):
                for kg in range(CC):
                    for ch in range(2):
                        t.wait(tok_w.get(("amat", ch, kg)))
                        if kg == 0 and ch == 0:
                            t.wait(t.tok(("indrdy",)))
                        for k4 in range(4):
                            kc = kg * 4 + k4
                            fo = (kg == 0 and ch == 0 and k4 == 0)
                            t.op("matmul", ps[0:128, kc:kc + 1],
                                 wsl[(kg * 2 + ch) % 2][:, k4 * 128:(k4 + 1) * 128],
                                 ind_sb[:, ch:ch + 1],
                                 start=fo, stop=fo, skip_group_check=not fo,
                                 tick=(k4 == 3),
                                 mark=(("mg", kg, ch) if k4 == 3 else None))
            ggrp(em, ("mcolg",))
            for qg in range(NTG):
                xgroup("b", qg, nxc)
                t.wait(t.tok(("xev", "b", qg)))
                for dc in range(CC):
                    t.wait(tok_w.get(("qb", qg, dc)))
                    proj_group(dc, ("qgb", qg, dc))
                t.wait(t.tok(("qgb_ev", qg, CC - 1)))
                if qg == 0:
                    t.wait(t.tok(("vmsc",)))
                for h in range(NH):
                    hb = (h % 2) * 64
                    hc = h // 2
                    giD = cnt["G"]; cnt["G"] += 1
                    if giD >= 2:
                        t.wait(t.tok(("Gev", giD - 2)))
                    for kc in range(NKC):
                        def em(ps, i, hb=hb, hc=hc, kc=kc):
                            t.op("matmul", ps[0:128, 0:TGS],
                                 kfm[hb:hb + 64, hc * S + kc * 128: hc * S + (kc + 1) * 128],
                                 qg_sb[hb:hb + 64, hc * TGS:(hc + 1) * TGS],
                                 start=True, stop=True, tick=True)
                        sgrp(em, ("ssc", qg, h, kc))
                        t.wait(t.tok(("pex", qg, h, kc)))
                        if kc == 0:
                            un = qg * NH + h
                            if un >= 2:
                                pqg, ph = divmod(un - 2, NH)
                                t.wait(t.tok(("ocmb", pqg, ph)))
                        for qt in range(NSUB):
                            fo = (kc == 0 and qt == 0)
                            t.op("matmul", psOW[h % 2][:, qt * 64:(qt + 1) * 64],
                                 prT[kc % 2][:, qt * 128:(qt + 1) * 128],
                                 vtm[:, kc * DM + h * 64: kc * DM + h * 64 + 64],
                                 start=fo, stop=fo, skip_group_check=not fo,
                                 tick=False)
                        if kc >= 28:
                            for qt in range(NSUB):
                                t.op("matmul",
                                     psOW[h % 2][:, 256 + qt * 64: 256 + (qt + 1) * 64],
                                     prT[kc % 2][:, qt * 128:(qt + 1) * 128],
                                     vwin[:, (kc - 28) * DM + h * 64: (kc - 28) * DM + h * 64 + 64],
                                     start=False, stop=False,
                                     skip_group_check=True, tick=False)
                        for qt in range(NSUB):
                            fo = (kc == 0 and qt == 0)
                            t.op("matmul", psG[giD % 2][:, qt * 2:(qt + 1) * 2],
                                 prT[kc % 2][:, qt * 128:(qt + 1) * 128],
                                 mw2[:, kc * 2:(kc + 1) * 2],
                                 start=fo, stop=fo, skip_group_check=not fo,
                                 tick=(qt == NSUB - 1),
                                 mark=(("pvk", qg, h, kc) if qt == NSUB - 1 else None))
                    bld.marks[("Ggrp", giD)] = bld.marks[("pvk", qg, h, NKC - 1)]
                    bld.marks[("dgrp", qg, h)] = bld.marks[("Ggrp", giD)]
                    for qt in range(NSUB):
                        def em(ps, i, hb=hb, hc=hc, qt=qt):
                            t.op("matmul", ps[:, 0:NB],
                                 qg_sb[hb:hb + 64, hc * TGS + qt * 128: hc * TGS + qt * 128 + 128],
                                 ck_sb[hb:hb + 64, hc * NB:(hc + 1) * NB],
                                 start=True, stop=True, tick=True)
                        ggrp(em, ("cscb", qg, qt, h))
                        tjs = []
                        for ch in range(2):
                            wdt = 128 if ch == 0 else 127
                            wl = [t.tok(("pncb", qg, qt, h))] if ch == 0 else []
                            tjs.append(transp(pc_sb[:, ch * 128: ch * 128 + wdt],
                                              wdt, rows=128, wait_list=wl,
                                              mark=(("pctr", qg, qt, h)
                                                    if ch == 1 else None)))

                        def em(ps, i, h=h, qt=qt, tjs=tjs):
                            for ch in range(2):
                                wdt = 128 if ch == 0 else 127
                                j = tjs[ch]
                                t.wait(t.tok(("Tevd", j)))
                                t.op("matmul", ps[:, 0:64],
                                     ptr_sb[j % 2][0:wdt, :],
                                     cvbm[0:wdt, h * 128 + ch * 64: h * 128 + ch * 64 + 64],
                                     start=(ch == 0), stop=(ch == 1),
                                     tick=True, mark=("pvmm", j))
                        ggrp(em, ("cpv", qg, qt, h))
                for qt in range(NSUB):
                    qb0 = qt * NH * 64
                    for c in range(CC):
                        transp(oq_sb[:, qb0 + c * 128: qb0 + (c + 1) * 128], 128,
                               ("otr", qg, qt, c),
                               wait_list=([t.tok(("oqv", qg, qt))] if c == 0 else ()))
                    for nh in range(2):
                        def em(ps, i, qt=qt, nh=nh, qg=qg, qb0=qb0):
                            for c in range(CC):
                                wli = ((qg * NSUB + qt) * 2 + nh) * CC + c
                                t.wait(tok_w.get(("wo", qg, qt, nh, c)))
                                if c == 0:
                                    t.wait(t.tok(("ofmv", qg, qt)))
                                t.op("matmul", ps[:],
                                     ofm[:, qb0 + c * 128: qb0 + (c + 1) * 128],
                                     wosl[wli % 2][:],
                                     start=(c == 0), stop=(c == CC - 1),
                                     tick=True, mark=("womm", wli))
                        si = sgrp(em, ("wog", qg, qt, nh))
                        bld.marks[("wogS", qg, qt, nh)] = ("idx", si)
        def s_body(s):
            cS = [0]; cG = [0]

            def snS():
                i = cS[0]; cS[0] += 1; return i

            def snG():
                i = cG[0]; cG[0] += 1; return i

            for tg in range(NTG):
                for dc in range(CC):
                    si = snS()
                    s.wait(s.tok(("Sgrp", si)))
                    s.copy(kfm[:, dc * S + tg * TGS: dc * S + (tg + 1) * TGS],
                           psS[si % 2][:], mark=("Sev", si))
                for hp in range(HP):
                    si = snS()
                    s.wait(s.tok(("Sgrp", si)))
                    if tg > 0 or hp > 0:
                        ptg, php = (tg, hp - 1) if hp > 0 else (tg - 1, HP - 1)
                        s.wait(s.tok(("vwork", ptg, php)))
                        s.wait(s.tok(("vtr", ptg, php, NSUB - 1)))
                        s.wait(s.tok(("cvg", ptg, php, 1)))
                    s.copy(vwork[:, 16:16 + TGS], psS[si % 2][:], mark=("Sev", si))
                    bld.marks[("vev", tg, hp)] = bld.marks[("Sev", si)]
                    n0, nn = blocks_of_tg(tg)
                    h2 = hp * 2
                    for hh in range(2):
                        gi = snG()
                        s.wait(s.tok(("Ggrp", gi)))
                        h = h2 + hh
                        s.add(cv_sb[(h % 2) * 64:(h % 2) * 64 + 64,
                                    (h // 2) * NB + n0:(h // 2) * NB + n0 + nn],
                              psG[gi % 2][0:64, 0:nn],
                              bcv_sb[:, 0:1], mark=("Gev", gi))
                        bld.marks[("cvev", tg, h // 2, h % 2)] = bld.marks[("Gev", gi)]
            for h in range(NH):
                hb = (h % 2) * 64
                hc = h // 2
                gi = snG()
                s.wait(s.tok(("Ggrp", gi)))
                s.add(ck_sb[hb:hb + 64, hc * NB: hc * NB + NB],
                      psG[gi % 2][0:64, 0:NB],
                      bck_sb[:, 0:1], mark=("Gev", gi))
                bld.marks[("ckv", h)] = bld.marks[("Gev", gi)]
            for qg in range(NTG):
                for dc in range(CC):
                    si = snS()
                    s.wait(s.tok(("Sgrp", si)))
                    s.copy(qg_sb[:, dc * TGS:(dc + 1) * TGS], psS[si % 2][:],
                           mark=("Sev", si))
                    bld.marks[("qga_ev", qg, dc)] = bld.marks[("Sev", si)]
                for h in range(NH):
                    for qt in range(NSUB):
                        gi = snG()
                        s.wait(s.tok(("Ggrp", gi)))
                        s.wait(s.tok(("dmax", 0, qg, h, qt)))
                        if (h, qt) != (0, 0) or qg > 0:
                            ph, pqt = (h, qt - 1) if qt > 0 else ((h - 1, NSUB - 1) if h > 0 else (NH - 1, NSUB - 1))
                            pqg = qg if (h, qt) != (0, 0) else qg - 1
                            s.wait(s.tok(("impg", pqg, ph, pqt)))
                        s.activation(pc_sb[:, 0:NB], psG[gi % 2][:, 0:NB], AF.Exp,
                                     bias=dst[:, 0:1], scale=SC,
                                     accum_out=dst[:, 1:2], mark=("cexa", qg, h, qt))
                        bld.marks[("Gev", gi)] = bld.marks[("cexa", qg, h, qt)]
                        gi = snG()
            for h in range(NH):
                gi = snG()
            gi = snG()
            s.wait(s.tok(("graw",)))
            s.activation(gates[0:1, 0:48], gates[0:1, 0:48], AF.Sigmoid,
                         mark=("grow",))
            gi = snG()
            gi = snG()
            gi = snG()
            for qg in range(NTG):
                for dc in range(CC):
                    si = snS()
                    s.wait(s.tok(("Sgrp", si)))
                    s.copy(qg_sb[:, dc * TGS:(dc + 1) * TGS], psS[si % 2][:],
                           mark=("Sev", si))
                    bld.marks[("qgb_ev", qg, dc)] = bld.marks[("Sev", si)]
                for h in range(NH):
                    for kc in range(NKC):
                        si = snS()
                        s.wait(s.tok(("Sgrp", si)))
                        if kc >= 2:
                            s.wait(s.tok(("pvk", qg, h, kc - 2)))
                        elif (qg, h) != (0, 0) and kc == 0:
                            pqg, ph = (qg, h - 1) if h > 0 else (qg - 1, NH - 1)
                            s.wait(s.tok(("pvk", pqg, ph, NKC - 2 + kc)))
                        elif (qg, h) != (0, 0):
                            pqg, ph = (qg, h - 1) if h > 0 else (qg - 1, NH - 1)
                            s.wait(s.tok(("pvk", pqg, ph, NKC - 1)))
                        s.activation(prT[kc % 2][:], psS[si % 2][0:128, 0:TGS],
                                     AF.Exp, scale=SC, mark=("pex", qg, h, kc))
                        bld.marks[("Sev", si)] = bld.marks[("pex", qg, h, kc)]
                    gi = snG()
                    for qt in range(NSUB):
                        gi = snG()
                        s.wait(s.tok(("Ggrp", gi)))
                        s.wait(s.tok(("dmax", 1, qg, h, qt)))
                        prev = ((qg, qt - 1, h) if qt > 0 else
                                ((qg, NSUB - 1, h - 1) if h > 0 else
                                 ((qg - 1, NSUB - 1, NH - 1) if qg > 0 else None)))
                        if prev is not None:
                            s.wait(s.tok(("pctr", *prev)))
                        s.activation(pc_sb[:, 0:NB], psG[gi % 2][:, 0:NB], AF.Exp,
                                     bias=dst[:, 0:1], scale=SC,
                                     accum_out=dst[:, 1:2], mark=("cexb", qg, qt, h))
                        bld.marks[("Gev", gi)] = bld.marks[("cexb", qg, qt, h)]
                        bld.marks[("pncb", qg, qt, h)] = bld.marks[("cexb", qg, qt, h)]
                        gi = snG()
                for qt in range(NSUB):
                    for nh in range(2):
                        si = snS()
                    ln.s_stats(s, qg * NSUB + qt)

        def v_body(v):
            cT = [0]; cG = [0]; cO = [0]
            v.memset(stat1[:], 0.0)
            v.memset(imp_sb[:], 0.0)
            v.memset(rank_sb[:], 0.0)

            def tev(dest_ap, mark=None, width=128, rows=128):
                j = cT[0]; cT[0] += 1
                v.wait(v.tok(("Ttr", j)))
                v.tensor_copy(dest_ap, ps_tr[j % 2][0:width, 0:rows],
                              mark=("Tev", j))
                if mark:
                    bld.marks[mark] = bld.marks[("Tev", j)]
                return j

            def tevd(width=128):
                j = cT[0]; cT[0] += 1
                v.wait(v.tok(("Ttr", j)))
                if j >= 2:
                    v.wait(v.tok_opt(("pvmm", j - 2)))
                v.tensor_copy(ptr_sb[j % 2][0:width, :],
                              ps_tr[j % 2][0:width, 0:128], mark=("Tevd", j))
                bld.marks[("Tev", j)] = bld.marks[("Tevd", j)]
                return j

            def xevac(ph, gg):
                for sub in range(NSUB):
                    for c in range(CC):
                        tev(xg_sb[:, c * TGS + sub * 128: c * TGS + sub * 128 + 128])
                v.op("tensor_copy", stat1[0:1, 7:8], stat1[0:1, 7:8], tick=True,
                     mark=("xev", ph, gg))

            def snG():
                i = cG[0]; cG[0] += 1; return i

            for tg in range(NTG):
                xevac("a0", tg)
                for hp in range(HP):
                    v.wait(v.tok(("vev", tg, hp)))
                    if tg > 0:
                        v.tensor_copy(vwork[:, 0:16], vtail[:, hp * 16:(hp + 1) * 16])
                    v.tensor_copy(vtail[:, hp * 16:(hp + 1) * 16],
                                  vwork[:, TGS:TGS + 16], mark=("vwork", tg, hp))
                    for sub in range(NSUB):
                        kc = tg * NSUB + sub
                        tev(vtm.ap()[:, kc * DM + hp * 128: kc * DM + hp * 128 + 128],
                            mark=("vtr", tg, hp, sub))
                    for hh in range(2):
                        snG()
            for h in range(NH):
                snG()
                for ch in range(2):
                    wdt = 128 if ch == 0 else 127
                    tev(cvbm[0:wdt, h * 128 + ch * 64: h * 128 + ch * 64 + 64],
                        mark=("cvbm", h, ch), width=wdt, rows=64)
            for qg in range(NTG):
                xevac("a1", qg)
                for h in range(NH):
                    hb = (h % 2) * 64
                    hc = h // 2
                    for qt in range(NSUB):
                        gi = snG()
                        v.wait(v.tok(("Ggrp", gi)))
                        v.reduce_max(dst[:, 0:1], psG[gi % 2][:, 0:NB], axis=AX.X,
                                     negate=True, apply_transpose=False)
                        v.tensor_scalar_mul(dst[:, 0:1], dst[:, 0:1], SC,
                                            mark=("dmax", 0, qg, h, qt))
                        v.wait(v.tok(("cexa", qg, h, qt)))
                        v.reciprocal(dst[:, 2:3], dst[:, 1:2])
                        v.tensor_scalar(pc_sb[:, 0:NB], pc_sb[:, 0:NB],
                                        dst[:, 2:3], None, ALU.mult,
                                        mark=("pnca", qg, h, qt))
                        gi = snG()
                        v.wait(v.tok(("Ggrp", gi)))
                        if qg == 0 and h == 0 and qt == 0:
                            v.tensor_copy(imp_sb[:, 0:1], psG[gi % 2][:, 0:1])
                            v.tensor_copy(imp_sb[0:127, 1:2],
                                          psG[gi % 2][0:127, 1:2],
                                          mark=("Gev", gi))
                        else:
                            v.tensor_add(imp_sb[:, 0:1], imp_sb[:, 0:1],
                                         psG[gi % 2][:, 0:1])
                            v.tensor_add(imp_sb[0:127, 1:2], imp_sb[0:127, 1:2],
                                         psG[gi % 2][0:127, 1:2],
                                         mark=("Gev", gi))
                    v.reduce_sum(qmt[hb:hb + 64, 0:1],
                                 qg_sb[hb:hb + 64, hc * TGS:(hc + 1) * TGS], axis=AX.X)
                    if qg == 0:
                        v.tensor_copy(qm_sb[hb:hb + 64, hc:hc + 1], qmt[hb:hb + 64, 0:1])
                    else:
                        v.tensor_add(qm_sb[hb:hb + 64, hc:hc + 1],
                                     qm_sb[hb:hb + 64, hc:hc + 1], qmt[hb:hb + 64, 0:1])
            v.tensor_scalar_mul(qmb[:], qm_sb[:], 1.0 / S, mark=("qmrdy",))
            for h in range(NH):
                gi = snG()
                v.wait(v.tok(("Ggrp", gi)))
                v.tensor_add(gates[0:1, h * 3:h * 3 + 3], psG[gi % 2][0:1, 0:3],
                             bg_sb[0:1, :], mark=("Gev", gi))
                if h == NH - 1:
                    bld.marks[("graw",)] = bld.marks[("Gev", gi)]
            gi = snG()
            v.wait(v.tok(("Ggrp", gi)))
            v.tensor_copy(gcol[:, 0:48], psG[gi % 2][0:128, 0:48], mark=("Gev", gi))
            v.tensor_copy(impb[:], imp_sb[:])
            v.tensor_copy(impc[:], impb[:])
            v.tensor_tensor(implo[:], imp_sb[:], impb[:], ALU.subtract)
            v.tensor_copy(impc2[:], implo[:], mark=("impbv",))
            tev(imp_row[0:1, 0:128], width=1, rows=128)
            tev(imp_row[0:1, 128:255], width=1, rows=127)
            tev(imp_row[0:1, 256:384], width=1, rows=128)
            tev(imp_row[0:1, 384:511], width=1, rows=127, mark=("improw",))
            gi = snG()
            gi2 = snG()
            v.wait(v.tok(("Ggrp", gi2)))
            pcx = cv_sb
            for ch in range(2):
                wdt = 128 if ch == 0 else 127
                v.tensor_scalar(crow[0:wdt, 0:NB], psG[gi % 2][0:wdt, 0:NB],
                                impc[0:wdt, ch:ch + 1], None, ALU.is_gt)
                v.reduce_sum(rank_sb[0:wdt, ch:ch + 1], crow[0:wdt, 0:NB], axis=AX.X)
                v.tensor_scalar(crow[0:wdt, 0:NB], psG[gi % 2][0:wdt, 0:NB],
                                impc[0:wdt, ch:ch + 1], None, ALU.is_equal)
                v.tensor_scalar(pcx[0:wdt, 0:NB], psG[gi2 % 2][0:wdt, 0:NB],
                                impc2[0:wdt, ch:ch + 1], None, ALU.is_gt)
                v.tensor_tensor(crow[0:wdt, 0:NB], crow[0:wdt, 0:NB],
                                pcx[0:wdt, 0:NB], ALU.mult)
                v.reduce_sum(dst[0:wdt, 3:4], crow[0:wdt, 0:NB], axis=AX.X)
                v.tensor_tensor(rank_sb[0:wdt, ch:ch + 1],
                                rank_sb[0:wdt, ch:ch + 1], dst[0:wdt, 3:4],
                                ALU.add)
            v.tensor_scalar(ind_sb[:], rank_sb[:], 15.5, None, ALU.is_lt,
                            mark=("indrdy",))
            bld.marks[("Gev", gi)] = bld.marks[("indrdy",)]
            bld.marks[("Gev", gi2)] = bld.marks[("indrdy",)]
            gi = snG()
            v.wait(v.tok(("Ggrp", gi)))
            v.tensor_copy(mcol_sb[:], psG[gi % 2][0:128, 0:NKC], mark=("Gev", gi))
            v.memset(ones16[:], 1.0)
            for kw in range(4):
                for h2 in range(NH):
                    v.tensor_copy(vwin[:, kw * DM + h2 * 64: kw * DM + h2 * 64 + 64],
                                  vtm.ap()[:, (28 + kw) * DM + h2 * 64: (28 + kw) * DM + h2 * 64 + 64])
            for kc in range(NKC):
                v.tensor_scalar(vtm.ap()[:, kc * DM:(kc + 1) * DM],
                                vtm.ap()[:, kc * DM:(kc + 1) * DM],
                                mcol_sb[:, kc:kc + 1], None, ALU.mult)
            v.tensor_copy(mw2.ap()[:, 0:2 * NKC:2], mcol_sb[:, 0:NKC])
            v.memset(mw2.ap()[:, 1:2 * 28:2], 0.0)
            v.tensor_copy(mw2.ap()[:, 2 * 28 + 1:2 * NKC:2], ones16[:, 0:4],
                          mark=("vmsc",))
            for qg in range(NTG):
                xevac("b", qg)
                for h in range(NH):
                    gi = snG()
                    v.wait(v.tok(("Ggrp", gi)))
                    v.tensor_copy(dcol_sb[:], psG[gi % 2][0:128, 0:8],
                                  mark=("Gev", gi))
                    for qt in range(NSUB):
                        gi = snG()
                        v.wait(v.tok(("Ggrp", gi)))
                        v.reduce_max(dst[:, 0:1], psG[gi % 2][:, 0:NB], axis=AX.X,
                                     negate=True)
                        v.tensor_scalar_mul(dst[:, 0:1], dst[:, 0:1], SC,
                                            mark=("dmax", 1, qg, h, qt))
                        tevd(128)
                        tevd(127)
                        gi = snG()
                        v.wait(v.tok(("Ggrp", gi)))
                        v.wait(v.tok(("cexb", qg, qt, h)))
                        v.reciprocal(dst[:, 6:7], dcol_sb[:, qt * 2: qt * 2 + 1])
                        v.tensor_scalar(dst[:, 6:7], dst[:, 6:7],
                                        gcol[:, h * 3 + 1: h * 3 + 2], None, ALU.mult)
                        v.reciprocal(dst[:, 7:8], dcol_sb[:, qt * 2 + 1: qt * 2 + 2])
                        v.tensor_scalar(dst[:, 7:8], dst[:, 7:8],
                                        gcol[:, h * 3 + 2: h * 3 + 3], None, ALU.mult)
                        v.reciprocal(dst[:, 8:9], dst[:, 1:2])
                        v.tensor_scalar(dst[:, 8:9], dst[:, 8:9],
                                        gcol[:, h * 3: h * 3 + 1], None, ALU.mult)
                        if qt == 0:
                            v.wait(v.tok(("pvk", qg, h, NKC - 1)))
                        ob = qt * NH * 64 + h * 64
                        v.tensor_scalar(oq_sb[:, ob:ob + 64],
                                        psOW[h % 2][:, qt * 64:(qt + 1) * 64],
                                        dst[:, 6:7], None, ALU.mult)
                        v.scalar_tensor_tensor(oq_sb[:, ob:ob + 64],
                                               psOW[h % 2][:, 256 + qt * 64: 256 + (qt + 1) * 64],
                                               dst[:, 7:8],
                                               oq_sb[:, ob:ob + 64],
                                               ALU.mult, ALU.add)
                        v.scalar_tensor_tensor(oq_sb[:, ob:ob + 64],
                                               psG[gi % 2][:, 0:64], dst[:, 8:9],
                                               oq_sb[:, ob:ob + 64],
                                               ALU.mult, ALU.add,
                                               mark=("ocmbq", qg, h, qt))
                        bld.marks[("Gev", gi)] = bld.marks[("ocmbq", qg, h, qt)]
                        if h == NH - 1:
                            bld.marks[("oqv2", qg, qt)] = bld.marks[("ocmbq", qg, h, qt)]
                    bld.marks[("ocmb", qg, h)] = bld.marks[("ocmbq", qg, h, NSUB - 1)]
                for qt in range(NSUB):
                    bld.marks[("oqv", qg, qt)] = bld.marks[("oqv2", qg, qt)]
                    qb0 = qt * NH * 64
                    for c in range(CC):
                        tev(ofm.ap()[:, qb0 + c * 128: qb0 + (c + 1) * 128],
                            mark=("ofmv", qg, qt) if c == CC - 1 else None)
                    i = qg * NSUB + qt
                    for nh in range(2):
                        v.wait(v.tok(("wog", qg, qt, nh)))
                        v.wait(tok_xres.get(i))
                        if i >= 1 and nh == 0:
                            v.wait(v.tok(("L", "sq", i - 1)))
                            v.wait(tok_store.get(i - 1))
                        si = bld.marks[("wogS", qg, qt, nh)][1]
                        v.scalar_tensor_tensor(
                            ln.sum_sb[0][:, nh * 512:(nh + 1) * 512],
                            psS[si % 2][:], 1.0,
                            xres[:, nh * 512:(nh + 1) * 512],
                            ALU.mult, ALU.add, mark=("Sev", si))
                    ln.v_stats(v, i)

        def g_body(g):
            tok_const.append(gp.load(g.eng, idb[:], ident[:, :]))
            tok_const.append(gp.load(g.eng, ones_c[:], onescol[:, :]))
            tok_const.append(gp.load(g.eng, ones_rb[:], onesrow[:, :]))
            tok_const.append(gp.load(g.eng, wckb[:], wck[:, :]))
            tok_const.append(gp.load(g.eng, wcvb[:], wcv[:, :]))
            tok_const.append(gp.load(g.eng, wg_sb[:], wgm[:, :]))
            nx = [0]

            def xload(ph, gg, sub):
                if nx[0] >= 2:
                    m = ("xtr", *xord[nx[0] - 2][0:1], xord[nx[0] - 2][1], xord[nx[0] - 2][2], CC - 1)
                    g.wait(g.tok(m))
                tok_xt[(ph, gg, sub)] = gp.load(
                    g.eng, xtm[nx[0] % 2][:],
                    x_in[(gg * NSUB + sub) * 128:(gg * NSUB + sub + 1) * 128, :])
                nx[0] += 1

            WMARK = {}
            li = [0]

            def wload(key, cons_mark, src, full=True):
                if li[0] >= 2:
                    m = WMARK.get(li[0] - 2)
                    if m:
                        g.wait(g.tok(m))
                dest = (wsl[li[0] % 2].ap().rearrange("p (c f) -> p c f", c=CC)
                        if full else wsl[li[0] % 2][:, 0:TGS])
                tok_w[key] = wp.load(g.eng, dest, src)
                WMARK[li[0]] = cons_mark
                li[0] += 1

            for tg in range(NTG):
                for sub in range(NSUB):
                    xload("a0", tg, sub)
                for dc in range(CC):
                    wload(("k", tg, dc), ("kg", tg, dc),
                          wkT[:, dc * 128:(dc + 1) * 128].rearrange("(c p) f -> p c f", p=128))
                for hp in range(HP):
                    wload(("v", tg, hp), ("vg", tg, hp),
                          wvT[:, hp * 128:(hp + 1) * 128].rearrange("(c p) f -> p c f", p=128))
            for qg in range(NTG):
                for sub in range(NSUB):
                    xload("a1", qg, sub)
                for dc in range(CC):
                    wload(("qa", qg, dc), ("qga", qg, dc),
                          wqT[:, dc * 128:(dc + 1) * 128].rearrange("(c p) f -> p c f", p=128))
            for kg in range(CC):
                for ch in range(2):
                    wload(("amat", ch, kg), ("mg", kg, ch),
                          amat[ch, :, kg * TGS:(kg + 1) * TGS], full=False)
            for qg in range(NTG):
                for sub in range(NSUB):
                    xload("b", qg, sub)
                for dc in range(CC):
                    wload(("qb", qg, dc), ("qgb", qg, dc),
                          wqT[:, dc * 128:(dc + 1) * 128].rearrange("(c p) f -> p c f", p=128))
                for qt in range(NSUB):
                    for nh in range(2):
                        for c in range(CC):
                            wli = ((qg * NSUB + qt) * 2 + nh) * CC + c
                            if wli >= 2:
                                g.wait(g.tok(("womm", wli - 2)))
                            tok_w[("wo", qg, qt, nh, c)] = wp.load(
                                g.eng, wosl[wli % 2][:],
                                wo[c * 128:(c + 1) * 128, nh * 512:(nh + 1) * 512])

        def y_body(y):
            tok_const.append(yp.load(y.eng, bck_sb[:], bck[:, :]))
            tok_const.append(yp.load(y.eng, bcv_sb[:], bcv[:, :]))
            tok_const.append(yp.load(y.eng, bg_sb[:], bgm[:, :]))
            for i in range(NT):
                if i >= 1:
                    y.wait(y.tok(("L", "sum", i - 1)))
                tok_xres[i] = yp.load(
                    y.eng, xres[:], x_in[i * 128:(i + 1) * 128, :])
                if i >= 1:
                    j = i - 1
                    tok_store[j] = ln.store(y, yp, j, x_out[j * 128:(j + 1) * 128, :])
            tok_store[NT - 1] = ln.store(y, yp, NT - 1, x_out[(NT - 1) * 128:NT * 128, :])
            if y.eng is not None:
                for tkn in yp.snapshot():
                    y.eng.wait_ge(tkn[1].sems[tkn[2]], tkn[3])

        bld.run_stage(nc, es, ges, [("t", t_body), ("s", s_body),
                                    ("v", v_body), ("g", g_body),
                                    ("y", y_body)])


def build_stage1():
    if "s1" in _cache:
        return _cache["s1"]
    nc0 = bass.Bass("TRN2", target_bir_lowering=False, debug=False)
    nc = _NcAlias(nc0)
    ges = contextlib.ExitStack()
    with ges:
        _emit_stage1(nc, ges, Build(), _s1_drams(nc))
    _cache["s1"] = nc0
    return nc0



def _wT(a):
    return np.ascontiguousarray(np.asarray(a).T)


def _fmvec(v):
    return np.ascontiguousarray(np.asarray(v).reshape(CC, 128).T)


def _ident():
    return np.eye(128, dtype=np.float32)


def stage2_inputs(x_core, W_r, W_k, W_v, mu_r, mu_k):
    return dict(x_in=x_core, wrT=_wT(W_r), wkT=_wT(W_k), wvT=_wT(W_v),
                mur=_fmvec(mu_r), muk=_fmvec(mu_k), ident=_ident())


def run_stage2(x, W_r, W_k, W_v, mu_r, mu_k):
    nc = build_stage2()
    com = stage2_inputs(None, W_r, W_k, W_v, mu_r, mu_k)
    del com["x_in"]
    in_maps = [dict(com, x_in=np.ascontiguousarray(x[c * BPC:(c + 1) * BPC]))
               for c in range(NCORES)]
    res = run_bass_kernel_spmd(nc, in_maps, core_ids=list(range(NCORES)))
    return np.concatenate([np.asarray(r["x_out"]) for r in res.results], axis=0)


def _c(a):
    return np.ascontiguousarray(np.asarray(a))


def _s1_dup(W):
    out = np.zeros((128, 32 * 64), np.float32)
    for t in range(32):
        A = np.asarray(W)[t * 64:(t + 1) * 64, :]
        out[0:64, t * 64:(t + 1) * 64] = A
        out[64:128, t * 64:(t + 1) * 64] = A
    return out


def _s1_amat():
    A = np.zeros((255, S), np.float32)
    for t in range(255):
        for r in range(64):
            A[t, min(t * 16 + r, S - 1)] += 1.0
    amat = np.zeros((2, 128, S), np.float32)
    amat[0, :, :] = A[0:128]
    amat[1, 0:127, :] = A[128:255]
    return amat


def stage1_inputs(Wq, Wk, Wv, Wo, Wck, bck, Wcv, bcv, Wg, bg):
    return dict(wqT=_c(Wq), wkT=_c(Wk), wvT=_c(Wv), wo=_c(Wo),
                wck=_s1_dup(Wck), wcv=_s1_dup(Wcv),
                bck=_c(np.asarray(bck).reshape(64, 1)),
                bcv=_c(np.asarray(bcv).reshape(64, 1)),
                wgm=_c(np.concatenate([np.asarray(Wg), np.asarray(Wg)], axis=0)),
                bgm=_c(np.asarray(bg).reshape(1, 3)),
                amat=_s1_amat(), ident=_ident(),
                onescol=np.ones((128, 1), np.float32),
                onesrow=np.ones((1, 128), np.float32))


def stage4_inputs(sW1, sb1, sW2, sb2, rW1, rb1, rW2, rb2, gW, gb):
    sW1 = np.asarray(sW1); sb1 = np.asarray(sb1)
    sW2 = np.asarray(sW2); sb2 = np.asarray(sb2)
    w1s = np.stack([sW1[0, :, :2048], sW1[0, :, 2048:]])
    wcat1 = _c(np.concatenate([w1s, np.asarray(rW1)], axis=0))
    w2s = np.stack([sW2[0, :2048, :], sW2[0, 2048:, :]])
    wcat2 = _c(np.concatenate([w2s, np.asarray(rW2)], axis=0))
    b1cat_parts = [sb1[0][:2048].reshape(16, 128).T,
                   sb1[0][2048:].reshape(16, 128).T]
    for e in range(15):
        b1cat_parts.append(np.asarray(rb1)[e].reshape(16, 128).T)
    b1cat = _c(np.concatenate(b1cat_parts, axis=1))
    sb2b = _c(np.tile(sb2[0][None, :], (128, 1)))
    gbb = _c(np.tile(np.asarray(gb)[None, :], (128, 1)))
    return dict(wcat1=wcat1, wcat2=wcat2, b1cat=b1cat, sb2b=sb2b,
                rb2=_c(rb2), gw=_c(gW), gbb=gbb, ident=_ident())


def build_mega():
    if "mega" in _cache:
        return _cache["mega"]
    nc0 = bass.Bass("TRN2", target_bir_lowering=False, debug=False)
    nc = _NcAlias(nc0)

    def T(name, shape, kind="ExternalInput"):
        return nc.dram_tensor(name, shape, F32, kind=kind).ap()

    x_in = T("x_in", [BPC, S, DM])
    x_out = T("x_out", [BPC, S, DM], "ExternalOutput")
    y1 = T("y1", [BPC, S, DM], "Internal")
    y2 = T("y2", [BPC, S, DM], "Internal")
    y3 = T("y3", [BPC, S, DM], "Internal")
    ident = T("ident", [128, 128])
    onescol = T("onescol", [128, 1])
    onesrow = T("onesrow", [1, 128])
    com = dict(ident=ident, onescol=onescol, onesrow=onesrow)
    s1w = dict(wqT=T("wqT", [DM, DM]), wkT=T("wkT", [DM, DM]),
               wvT=T("wvT", [DM, DM]), wo=T("wo", [DM, DM]),
               wck=T("wck", [128, 32 * 64]), wcv=T("wcv", [128, 32 * 64]),
               bck=T("bck", [64, 1]), bcv=T("bcv", [64, 1]),
               wgm=T("wgm", [128, 3]), bgm=T("bgm", [1, 3]),
               amat=T("amat", [2, 128, S]))
    s2w = dict(wrT=T("fwrT", [DM, DM]), wkT=T("fwkT", [DM, DM]),
               wvT=T("fwvT", [DM, DM]), mur=T("mur", [128, CC]),
               muk=T("muk", [128, CC]))
    s3w = dict(wqkvT=T("wqkvT", [DM, 3 * DM]))
    s4w = dict(wcat1=T("wcat1", [17, DM, 2048]), wcat2=T("wcat2", [17, 2048, DM]),
               b1cat=T("b1cat", [128, 17 * 16]), sb2b=T("sb2b", [128, DM]),
               rb2=T("rb2", [15, DM]), gw=T("gw", [DM, 15]),
               gbb=T("gbb", [128, 15]))

    bld = Build()
    ges = contextlib.ExitStack()
    with ges:
        for b in range(BPC):
            _emit_stage1(nc, ges, bld,
                         dict(s1w, **com, x_in=x_in[b], x_out=y1[b]))
        _emit_stage2(nc, ges, bld, dict(s2w, **com, x_in=y1, x_out=y2))
        _emit_stage3(nc, ges, bld, dict(s3w, **com, x_in=y2, x_out=y3))
        for b in range(BPC):
            _emit_stage4(nc, ges, bld,
                         dict(s4w, **com, x_in=y3[b], x_out=x_out[b]))
    _cache["mega"] = nc0
    return nc0


def mega_inputs(inputs):
    s1c = stage1_inputs(inputs["L0_attn_Wq"], inputs["L0_attn_Wk"],
                        inputs["L0_attn_Wv"], inputs["L0_attn_Wo"],
                        inputs["L0_attn_Wck"], inputs["L0_attn_bck"],
                        inputs["L0_attn_Wcv"], inputs["L0_attn_bcv"],
                        inputs["L0_attn_Wg"], inputs["L0_attn_bg"])
    s4c = stage4_inputs(inputs["L1_ffn_sW1"], inputs["L1_ffn_sb1"],
                        inputs["L1_ffn_sW2"], inputs["L1_ffn_sb2"],
                        inputs["L1_ffn_rW1"], inputs["L1_ffn_rb1"],
                        inputs["L1_ffn_rW2"], inputs["L1_ffn_rb2"],
                        inputs["L1_ffn_gW"], inputs["L1_ffn_gb"])
    com = dict(s1c)
    com.update(s4c)
    com.update(fwrT=_wT(inputs["L0_ffn_W_r"]), fwkT=_wT(inputs["L0_ffn_W_k"]),
               fwvT=_wT(inputs["L0_ffn_W_v"]),
               mur=_fmvec(inputs["L0_ffn_mu_r"]),
               muk=_fmvec(inputs["L0_ffn_mu_k"]),
               wqkvT=_wT(inputs["L1_attn_W_qkv"]))
    return com


def kernel(**inputs):
    x = _c(inputs["L0_attn_x"])
    nc = build_mega()
    com = mega_inputs(inputs)
    in_maps = [dict(com, x_in=_c(x[c * BPC:(c + 1) * BPC]))
               for c in range(NCORES)]
    res = run_bass_kernel_spmd(nc, in_maps, core_ids=list(range(NCORES)))
    return np.concatenate([np.asarray(r["x_out"]) for r in res.results],
                          axis=0)


def kernel_unfused(**inputs):
    x = _c(inputs["L0_attn_x"])
    cores = list(range(NCORES))
    ones_c = np.ones((128, 1), np.float32)
    ones_r = np.ones((1, 128), np.float32)

    nc1 = build_stage1()
    s1c = stage1_inputs(inputs["L0_attn_Wq"], inputs["L0_attn_Wk"],
                        inputs["L0_attn_Wv"], inputs["L0_attn_Wo"],
                        inputs["L0_attn_Wck"], inputs["L0_attn_bck"],
                        inputs["L0_attn_Wcv"], inputs["L0_attn_bcv"],
                        inputs["L0_attn_Wg"], inputs["L0_attn_bg"])
    y = np.empty((B, S, DM), np.float32)
    for lp in range(2):
        in_maps = [dict(s1c, x_in=_c(x[2 * c + lp])) for c in range(NCORES)]
        res = run_bass_kernel_spmd(nc1, in_maps, core_ids=cores)
        for c in range(NCORES):
            y[2 * c + lp] = np.asarray(res.results[c]["x_out"])

    nc2 = build_stage2()
    com2 = stage2_inputs(None, inputs["L0_ffn_W_r"], inputs["L0_ffn_W_k"],
                         inputs["L0_ffn_W_v"], inputs["L0_ffn_mu_r"],
                         inputs["L0_ffn_mu_k"])
    del com2["x_in"]
    in_maps = [dict(com2, x_in=_c(y[c * BPC:(c + 1) * BPC]))
               for c in range(NCORES)]
    res = run_bass_kernel_spmd(nc2, in_maps, core_ids=cores)
    y = np.concatenate([np.asarray(r["x_out"]) for r in res.results], axis=0)

    nc3 = build_stage3()
    com3 = dict(wqkvT=_wT(inputs["L1_attn_W_qkv"]), ident=_ident(),
                onescol=ones_c, onesrow=ones_r)
    in_maps = [dict(com3, x_in=_c(y[c * BPC:(c + 1) * BPC]))
               for c in range(NCORES)]
    res = run_bass_kernel_spmd(nc3, in_maps, core_ids=cores)
    y = np.concatenate([np.asarray(r["x_out"]) for r in res.results], axis=0)

    nc4 = build_stage4()
    s4c = stage4_inputs(inputs["L1_ffn_sW1"], inputs["L1_ffn_sb1"],
                        inputs["L1_ffn_sW2"], inputs["L1_ffn_sb2"],
                        inputs["L1_ffn_rW1"], inputs["L1_ffn_rb1"],
                        inputs["L1_ffn_rW2"], inputs["L1_ffn_rb2"],
                        inputs["L1_ffn_gW"], inputs["L1_ffn_gb"])
    out = np.empty((B, S, DM), np.float32)
    for lp in range(2):
        in_maps = [dict(s4c, x_in=_c(y[2 * c + lp])) for c in range(NCORES)]
        res = run_bass_kernel_spmd(nc4, in_maps, core_ids=cores)
        for c in range(NCORES):
            out[2 * c + lp] = np.asarray(res.results[c]["x_out"])
    return out


# version 04
# speedup vs baseline: 1.2610x; 1.0066x over previous
import contextlib
import numpy as np

import concourse.bass as bass
import concourse.mybir as mybir
from concourse.bass_utils import run_bass_kernel_spmd

F32 = mybir.dt.float32
BF16 = mybir.dt.bfloat16
AF = mybir.ActivationFunctionType
ALU = mybir.AluOpType
AX = mybir.AxisListType

NCORES = 8
B = 16
S = 4096
DM = 1024
BPC = B // NCORES
CC = DM // 128
EPS = 1e-5

_cache = {}



class _MockIns:
    def then_inc(self, *a, **k):
        return self


_MOCK = _MockIns()


class Ticker:

    def __init__(self, name):
        self.name = name
        self.sem = None
        self.n = 0

    def reset(self):
        self.n = 0


class DmaPool:

    def __init__(self, name, nsem):
        self.name = name
        self.nsem = nsem
        self.sems = None
        self.uses = [0] * nsem
        self.i = 0

    def reset(self):
        self.uses = [0] * self.nsem
        self.i = 0

    def load(self, eng, out_ap, in_ap):
        j = self.i % self.nsem
        if eng is not None:
            if self.uses[j] > 0:
                eng.wait_ge(self.sems[j], 16 * self.uses[j])
            eng.dma_start(out=out_ap, in_=in_ap).then_inc(self.sems[j], 16)
        self.uses[j] += 1
        self.i += 1
        return ("dma", self, j, 16 * self.uses[j])

    def snapshot(self):
        return [("dma", self, j, 16 * self.uses[j]) for j in range(self.nsem)
                if self.uses[j] > 0]


class Eng:

    def __init__(self, bld, ticker, serialize):
        self.bld = bld
        self.ticker = ticker
        self.serialize = serialize
        self.eng = None

    def wait(self, *tokens):
        if self.eng is None:
            return
        need = {}
        for t in tokens:
            if t is None:
                continue
            if t[0] == "dma":
                _, pool, j, cnt = t
                key = ("dma", id(pool), j)
                sem = pool.sems[j]
            else:
                _, tk, cnt = t
                key = ("tk", id(tk))
                sem = tk.sem
            if key not in need or need[key][1] < cnt:
                need[key] = (sem, cnt)
        for sem, cnt in need.values():
            self.eng.wait_ge(sem, cnt)

    def tok(self, mark):
        v = self.bld.marks.get(mark)
        if v is None:
            assert self.bld.pass0, f"mark {mark} unresolved in pass 1"
            return None
        tk, n = v
        return ("tick", tk, n)

    def tok_opt(self, mark):
        v = self.bld.marks.get(mark)
        if v is None:
            return None
        tk, n = v
        return ("tick", tk, n)

    def op(self, name, *a, mark=None, tick=True, **k):
        if self.eng is not None:
            if self.serialize and self.ticker.n > 0:
                self.eng.wait_ge(self.ticker.sem, self.ticker.n)
            ins = getattr(self.eng, name)(*a, **k)
        else:
            ins = _MOCK
        if tick:
            if self.eng is not None:
                ins.then_inc(self.ticker.sem, 1)
            self.ticker.n += 1
            if mark is not None:
                self.bld.marks[mark] = (self.ticker, self.ticker.n)
        return ins

    def __getattr__(self, name):
        if name.startswith("_"):
            raise AttributeError(name)

        def f(*a, **k):
            return self.op(name, *a, **k)
        return f


class Build:

    def __init__(self):
        self.marks = {}
        self.tickers = {n: Ticker(n) for n in ("t", "v", "s", "g", "y")}
        self.pools = {}
        self.pass0 = True

    def pool(self, name, nsem):
        if name not in self.pools:
            self.pools[name] = DmaPool(name, nsem)
        return self.pools[name]

    def run(self, nc, es, bodies):
        return self.run_stage(nc, es, es, bodies)

    def run_stage(self, nc, es, ges, bodies):
        serialize = {"t": False, "v": True, "s": True, "g": False, "y": False}
        self.marks = {}
        bar = [("tick", tk, tk.n) for tk in self.tickers.values() if tk.n > 0]
        for p in self.pools.values():
            bar.extend(p.snapshot())
        snap_t = {k: t.n for k, t in self.tickers.items()}
        snap_p = {p.name: (list(p.uses), p.i) for p in self.pools.values()}
        self.pass0 = True
        engs = {k: Eng(self, self.tickers[k], serialize[k]) for k in self.tickers}
        for k, fn in bodies:
            fn(engs[k])
        counts0 = {k: t.n for k, t in self.tickers.items()}
        pools0 = {p.name: p.i for p in self.pools.values()}
        self.pass0 = False
        for k, t in self.tickers.items():
            t.n = snap_t[k]
            if t.sem is None:
                t.sem = ges.enter_context(nc.semaphore(f"tk_{t.name}"))
        for p in self.pools.values():
            p.uses = list(snap_p[p.name][0]) if p.name in snap_p else [0] * p.nsem
            p.i = snap_p[p.name][1] if p.name in snap_p else 0
            if p.sems is None:
                p.sems = [ges.enter_context(nc.semaphore(f"{p.name}{j}"))
                          for j in range(p.nsem)]
        blk = es.enter_context(nc.Block())
        attach = {"t": blk.tensor, "v": blk.vector, "s": blk.scalar,
                  "g": blk.gpsimd, "y": blk.sync}

        def mk(k, fn):
            def body(raw):
                engs[k].eng = raw
                engs[k].wait(*bar)
                fn(engs[k])
                engs[k].eng = None
            return body
        for k, fn in bodies:
            attach[k](mk(k, fn))
        counts1 = {k: t.n for k, t in self.tickers.items()}
        assert counts0 == counts1, (counts0, counts1)
        for p in self.pools.values():
            assert pools0[p.name] == p.i


class _NcAlias:

    def __init__(self, nc):
        self._nc = nc

    def sbuf_tensor(self, *a, **k):
        k.setdefault("allow_name_mangling", True)
        return self._nc.sbuf_tensor(*a, **k)

    def psum_tensor(self, *a, **k):
        k.setdefault("allow_name_mangling", True)
        return self._nc.psum_tensor(*a, **k)

    def __getattr__(self, name):
        return getattr(self._nc, name)



class LnExit:
    def __init__(self, nc, es, tag, dm=DM, nbuf=2):
        self.dm = dm
        self.nbuf = nbuf
        self.sum_sb = [es.enter_context(nc.sbuf_tensor(f"{tag}sum{i}", [128, dm], F32)) for i in range(nbuf)]
        self.sq_sb = es.enter_context(nc.sbuf_tensor(f"{tag}sq", [128, dm], BF16))
        self.st = es.enter_context(nc.sbuf_tensor(f"{tag}st", [128, 16], F32))
        self.tag = tag

    def v_stats(self, v, i):
        tag = self.tag
        i = i % self.nbuf if False else i
        o = (i % 2) * 8
        st = self.st
        v.reduce_sum(st[:, o + 0:o + 1], self.sum_sb[i % self.nbuf][:], axis=AX.X,
                     mark=(tag, "sum", i))
        v.tensor_scalar_mul(st[:, o + 2:o + 3], st[:, o + 0:o + 1], 1.0 / self.dm)
        v.tensor_mul(st[:, o + 3:o + 4], st[:, o + 2:o + 3], st[:, o + 2:o + 3])
        v.tensor_scalar_sub(st[:, o + 3:o + 4], st[:, o + 3:o + 4], EPS)
        v.wait(v.tok((tag, "sq", i)))
        v.scalar_tensor_tensor(st[:, o + 4:o + 5], st[:, o + 1:o + 2], 1.0 / self.dm,
                               st[:, o + 3:o + 4], ALU.mult, ALU.subtract,
                               mark=(tag, "ve", i))
        v.wait(v.tok((tag, "sqrt", i)))
        v.reciprocal(st[:, o + 6:o + 7], st[:, o + 5:o + 6])
        v.tensor_scalar(self.sum_sb[i % self.nbuf][:], self.sum_sb[i % self.nbuf][:],
                        st[:, o + 2:o + 3], st[:, o + 6:o + 7],
                        ALU.subtract, ALU.mult, mark=(tag, "nrm", i))

    def s_stats(self, s, i):
        tag = self.tag
        o = (i % 2) * 8
        s.wait(s.tok((tag, "sum", i)))
        s.activation(self.sq_sb[:], self.sum_sb[i % self.nbuf][:], AF.Square,
                     accum_out=self.st[:, o + 1:o + 2], mark=(tag, "sq", i))
        s.wait(s.tok((tag, "ve", i)))
        s.activation(self.st[:, o + 5:o + 6], self.st[:, o + 4:o + 5],
                     AF.Sqrt, mark=(tag, "sqrt", i))

    def store(self, y, pool, i, out_ap):
        y.wait(y.tok((self.tag, "nrm", i)))
        return pool.load(y.eng, out_ap, self.sum_sb[i % self.nbuf][:])



def _s2_drams(nc):
    return dict(
        x_in=nc.dram_tensor("x_in", [BPC, S, DM], F32, kind="ExternalInput").ap(),
        wrT=nc.dram_tensor("wrT", [DM, DM], F32, kind="ExternalInput").ap(),
        wkT=nc.dram_tensor("wkT", [DM, DM], F32, kind="ExternalInput").ap(),
        wvT=nc.dram_tensor("wvT", [DM, DM], F32, kind="ExternalInput").ap(),
        mur=nc.dram_tensor("mur", [128, CC], F32, kind="ExternalInput").ap(),
        muk=nc.dram_tensor("muk", [128, CC], F32, kind="ExternalInput").ap(),
        ident=nc.dram_tensor("ident", [128, 128], F32, kind="ExternalInput").ap(),
        x_out=nc.dram_tensor("x_out", [BPC, S, DM], F32, kind="ExternalOutput").ap())


def _emit_stage2(nc, ges, bld, A):
    x_in = A["x_in"]; wrT = A["wrT"]; wkT = A["wkT"]; wvT = A["wvT"]
    mur = A["mur"]; muk = A["muk"]; ident = A["ident"]; x_out = A["x_out"]

    SW = 4097
    NT = S // 128
    TGS = 512
    NTG = S // TGS
    NSUB = TGS // 128
    NSUBT = BPC * NT

    es = contextlib.ExitStack()
    with es:
        xfm = es.enter_context(nc.sbuf_tensor("xfm", [128, CC * SW], BF16))
        xtm = [es.enter_context(nc.sbuf_tensor(f"xtm{i}", [128, DM], BF16)) for i in range(2)]
        idb = es.enter_context(nc.sbuf_tensor("idb", [128, 128], BF16))
        wr_sb = es.enter_context(nc.sbuf_tensor("wr_sb", [128, CC * DM], BF16))
        wk_sb = es.enter_context(nc.sbuf_tensor("wk_sb", [128, CC * DM], BF16))
        wv_sb = es.enter_context(nc.sbuf_tensor("wv_sb", [128, CC * DM], BF16))
        mur_sb = es.enter_context(nc.sbuf_tensor("mur_sb", [128, CC], F32))
        muk_sb = es.enter_context(nc.sbuf_tensor("muk_sb", [128, CC], F32))
        xr_sb = [es.enter_context(nc.sbuf_tensor(f"xr{i}", [128, CC * TGS], BF16)) for i in range(2)]
        xk_sb = [es.enter_context(nc.sbuf_tensor(f"xk{i}", [128, CC * TGS], BF16)) for i in range(2)]
        d_sb = es.enter_context(nc.sbuf_tensor("d_sb", [128, TGS], BF16))
        r_sb = es.enter_context(nc.sbuf_tensor("r_sb", [128, NSUB * DM], BF16))
        kr_sb = es.enter_context(nc.sbuf_tensor("kr_sb", [128, TGS], BF16))
        k2_sb = [es.enter_context(nc.sbuf_tensor(f"k2{i}", [128, CC * TGS], BF16)) for i in range(2)]
        xres = [es.enter_context(nc.sbuf_tensor(f"xres{i}", [128, DM], F32)) for i in range(2)]
        ln = LnExit(nc, es, "L")
        ps_tr = [es.enter_context(nc.psum_tensor(f"ps_tr{i}", [128, 128], BF16)) for i in range(2)]
        pr = [es.enter_context(nc.psum_tensor(f"pr{i}", [128, 512], F32)) for i in range(2)]
        pk = [es.enter_context(nc.psum_tensor(f"pk{i}", [128, 512], F32)) for i in range(2)]
        pv = [es.enter_context(nc.psum_tensor(f"pv{i}", [128, 512], F32)) for i in range(2)]

        gp = bld.pool("gp", 6)
        yp = bld.pool("yp", 6)
        tok_xtile = {}
        tok_w = []
        tok_xres = {}
        tok_store = {}

        def g_body(g):
            tok_w.append(gp.load(g.eng, idb[:], ident[:, :]))
            for (w_sb, w_ap) in ((wr_sb, wrT), (wk_sb, wkT), (wv_sb, wvT)):
                for c in range(CC):
                    tok_w.append(gp.load(g.eng, w_sb[:, c * DM:(c + 1) * DM],
                                         w_ap[c * 128:(c + 1) * 128, :]))
            tok_w.append(gp.load(g.eng, mur_sb[:], mur[:, :]))
            tok_w.append(gp.load(g.eng, muk_sb[:], muk[:, :]))
            for b in range(BPC):
                for t in range(NT):
                    i = b * NT + t
                    if i >= 2:
                        g.wait(g.tok(("tr", (i - 1) * CC - 1)))
                    tok_xtile[(b, t)] = gp.load(
                        g.eng, xtm[i % 2][:], x_in[b, t * 128:(t + 1) * 128, :])

        def y_body(y):
            for b in range(BPC):
                for sg in range(NT):
                    i = b * NT + sg
                    if i >= 2:
                        y.wait(y.tok(("L", "sum", i - 2)))
                    tok_xres[i] = yp.load(
                        y.eng, xres[i % 2][:], x_in[b, sg * 128:(sg + 1) * 128, :])
                    if i >= 2:
                        j = i - 2
                        bb, ss = divmod(j, NT)
                        tok_store[j] = ln.store(y, yp, j,
                                                x_out[bb, ss * 128:(ss + 1) * 128, :])
            for j in (NSUBT - 2, NSUBT - 1):
                bb, ss = divmod(j, NT)
                tok_store[j] = ln.store(y, yp, j, x_out[bb, ss * 128:(ss + 1) * 128, :])
            if y.eng is not None:
                for t in yp.snapshot():
                    y.eng.wait_ge(t[1].sems[t[2]], t[3])

        def t_body(t):
            t.wait(tok_w[0])
            for b in range(BPC):
                for tt in range(NT):
                    i = b * NT + tt
                    t.wait(tok_xtile[(b, tt)])
                    if b > 0 and tt == 0:
                        t.wait(t.tok(("xrk", b - 1, NTG - 1)))
                    for c in range(CC):
                        j = i * CC + c
                        if j >= 2:
                            t.wait(t.tok(("trv", j - 2)))
                        t.op("transpose", ps_tr[j % 2][:],
                             xtm[i % 2][:, c * 128:(c + 1) * 128], idb[:],
                             tick=True, mark=("tr", j))
                if b == 0:
                    t.wait(*tok_w)
                for tg in range(NTG):
                    gi = b * NTG + tg
                    t.wait(t.tok(("xrk", b, tg)))
                    for sub in range(NSUB):
                        for nh in range(2):
                            rgi = gi * 8 + sub * 2 + nh
                            if rgi >= 2:
                                t.wait(t.tok(("sig", rgi - 2)))
                            for c in range(CC):
                                t.op("matmul",
                                     pr[rgi % 2][:],
                                     xr_sb[tg % 2][:, c * TGS + sub * 128: c * TGS + sub * 128 + 128],
                                     wr_sb[:, c * DM + nh * 512: c * DM + (nh + 1) * 512],
                                     start=(c == 0), stop=(c == CC - 1),
                                     tick=(c == CC - 1), mark=(("peR", rgi) if c == CC - 1 else None))
                    for kc in range(CC):
                        kgi = gi * 8 + kc
                        if kgi >= 2:
                            t.wait(t.tok(("k2", kgi - 2)))
                        for c in range(CC):
                            t.op("matmul",
                                 pk[kgi % 2][:],
                                 wk_sb[:, c * DM + kc * 128: c * DM + kc * 128 + 128],
                                 xk_sb[tg % 2][:, c * TGS:(c + 1) * TGS],
                                 start=(c == 0), stop=(c == CC - 1),
                                 tick=(c == CC - 1), mark=(("peK", kgi) if c == CC - 1 else None))
                    t.wait(t.tok(("k2", gi * 8 + 7)))
                    for sub in range(NSUB):
                        for nh in range(2):
                            vgi = gi * 8 + sub * 2 + nh
                            if vgi >= 2:
                                t.wait(t.tok(("mulV", vgi - 2)))
                            for c in range(CC):
                                t.op("matmul",
                                     pv[vgi % 2][:],
                                     k2_sb[tg % 2][:, c * TGS + sub * 128: c * TGS + sub * 128 + 128],
                                     wv_sb[:, c * DM + nh * 512: c * DM + (nh + 1) * 512],
                                     start=(c == 0), stop=(c == CC - 1),
                                     tick=(c == CC - 1), mark=(("peV", vgi) if c == CC - 1 else None))

        def s_body(s):
            for b in range(BPC):
                for tg in range(NTG):
                    gi = b * NTG + tg
                    for sub in range(NSUB):
                        for nh in range(2):
                            rgi = gi * 8 + sub * 2 + nh
                            s.wait(s.tok(("peR", rgi)))
                            if gi >= 1 and sub == 0 and nh == 0:
                                s.wait(s.tok(("mulV", gi * 8 - 1)))
                            s.activation(
                                r_sb[:, sub * DM + nh * 512: sub * DM + (nh + 1) * 512],
                                pr[rgi % 2][:], AF.Sigmoid, mark=("sig", rgi))
                    for kc in range(CC):
                        kgi = gi * 8 + kc
                        s.wait(s.tok(("peK", kgi)))
                        if gi >= 2 and kc == 0:
                            s.wait(s.tok(("peV", (gi - 2) * 8 + 7)))
                        s.activation(kr_sb[:], pk[kgi % 2][:], AF.Relu)
                        s.activation(
                            k2_sb[tg % 2][:, kc * TGS:(kc + 1) * TGS],
                            kr_sb[:], AF.Square, mark=("k2", kgi))
                    for sub in range(NSUB):
                        ln.s_stats(s, gi * NSUB + sub)

        def v_body(v):
            for c in range(CC):
                v.memset(xfm[:, c * SW:c * SW + 1], 0.0)
            for b in range(BPC):
                for tt in range(NT):
                    i = b * NT + tt
                    for c in range(CC):
                        j = i * CC + c
                        v.wait(v.tok(("tr", j)))
                        v.tensor_copy(
                            xfm[:, c * SW + 1 + tt * 128: c * SW + 1 + tt * 128 + 128],
                            ps_tr[j % 2][:], mark=("trv", j))
                for tg in range(NTG):
                    gi = b * NTG + tg
                    if gi >= 2:
                        v.wait(v.tok(("peK", (gi - 2) * 8 + 7)))
                    for c in range(CC):
                        xs = xfm[:, c * SW + 1 + tg * TGS: c * SW + 1 + (tg + 1) * TGS]
                        xp = xfm[:, c * SW + tg * TGS: c * SW + (tg + 1) * TGS]
                        v.tensor_sub(d_sb[:], xs, xp)
                        v.scalar_tensor_tensor(
                            xr_sb[tg % 2][:, c * TGS:(c + 1) * TGS], d_sb[:],
                            mur_sb[:, c:c + 1], xp, ALU.mult, ALU.add)
                        v.scalar_tensor_tensor(
                            xk_sb[tg % 2][:, c * TGS:(c + 1) * TGS], d_sb[:],
                            muk_sb[:, c:c + 1], xp, ALU.mult, ALU.add,
                            mark=(("xrk", b, tg) if c == CC - 1 else None))
                    for sub in range(NSUB):
                        i = gi * NSUB + sub
                        for nh in range(2):
                            vgi = gi * 8 + sub * 2 + nh
                            v.wait(v.tok(("peV", vgi)))
                            if nh == 0 and sub == 0:
                                v.wait(v.tok(("sig", gi * 8 + 7)))
                            if i >= 2 and nh == 0:
                                v.wait(v.tok(("L", "sq", i - 2)))
                                v.wait(tok_store.get(i - 2))
                            v.tensor_mul(
                                ln.sum_sb[i % 2][:, nh * 512:(nh + 1) * 512],
                                pv[vgi % 2][:],
                                r_sb[:, sub * DM + nh * 512: sub * DM + (nh + 1) * 512],
                                mark=("mulV", vgi))
                        v.wait(tok_xres[i])
                        if i >= 2:
                            v.wait(tok_store[i - 2])
                        v.tensor_add(ln.sum_sb[i % 2][:], ln.sum_sb[i % 2][:],
                                     xres[i % 2][:])
                        ln.v_stats(v, i)

        bld.run_stage(nc, es, ges, [("g", g_body), ("y", y_body), ("t", t_body),
                                    ("s", s_body), ("v", v_body)])


def build_stage2():
    if "s2" in _cache:
        return _cache["s2"]
    nc0 = bass.Bass("TRN2", target_bir_lowering=False, debug=False)
    nc = _NcAlias(nc0)
    ges = contextlib.ExitStack()
    with ges:
        _emit_stage2(nc, ges, Build(), _s2_drams(nc))
    _cache["s2"] = nc0
    return nc0



def _s3_drams(nc):
    return dict(
        x_in=nc.dram_tensor("x_in", [BPC, S, DM], F32, kind="ExternalInput").ap(),
        wqkvT=nc.dram_tensor("wqkvT", [DM, 3 * DM], F32, kind="ExternalInput").ap(),
        ident=nc.dram_tensor("ident", [128, 128], F32, kind="ExternalInput").ap(),
        onescol=nc.dram_tensor("onescol", [128, 1], F32, kind="ExternalInput").ap(),
        onesrow=nc.dram_tensor("onesrow", [1, 128], F32, kind="ExternalInput").ap(),
        x_out=nc.dram_tensor("x_out", [BPC, S, DM], F32, kind="ExternalOutput").ap())


def _emit_stage3(nc, ges, bld, A):
    x_in = A["x_in"]; wqkvT = A["wqkvT"]; ident = A["ident"]
    onescol = A["onescol"]; onesrow = A["onesrow"]; x_out = A["x_out"]

    NT = S // 128
    TGS = 512
    NTG = S // TGS
    NSUB = TGS // 128
    NKP = NT
    SCALE = 1.0 / 32.0
    A1G = 17
    A2G = 46

    es = contextlib.ExitStack()
    with es:
        kfm = es.enter_context(nc.sbuf_tensor("kfm", [128, CC * S], BF16))
        vtm = es.enter_context(nc.sbuf_tensor("vtm", [128, NKP * DM], BF16))
        p_sb = es.enter_context(nc.sbuf_tensor("p_sb", [128, NKP * TGS], BF16))
        qg_sb = es.enter_context(nc.sbuf_tensor("qg_sb", [128, CC * TGS], BF16))
        xg_sb = es.enter_context(nc.sbuf_tensor("xg_sb", [128, CC * TGS], BF16))
        vfm_sb = es.enter_context(nc.sbuf_tensor("vfm", [128, TGS], BF16))
        xtm = [es.enter_context(nc.sbuf_tensor(f"xtm{i}", [128, DM], BF16)) for i in range(2)]
        wsl = [es.enter_context(nc.sbuf_tensor(f"wsl{i}", [128, CC * 128], BF16)) for i in range(2)]
        idb = es.enter_context(nc.sbuf_tensor("idb", [128, 128], BF16))
        ones_c = es.enter_context(nc.sbuf_tensor("ones_c", [128, 1], BF16))
        ones_r = es.enter_context(nc.sbuf_tensor("ones_r", [1, 128], F32))
        stat = es.enter_context(nc.sbuf_tensor("stat", [1, 8], F32))
        negb_sb = es.enter_context(nc.sbuf_tensor("negb_sb", [128, 1], F32))
        recd_sb = es.enter_context(nc.sbuf_tensor("recd_sb", [128, NSUB], F32))
        xres = [es.enter_context(nc.sbuf_tensor(f"xres{i}", [128, DM], F32)) for i in range(2)]
        ln = LnExit(nc, es, "L")
        sqg = ln.sq_sb
        A = [es.enter_context(nc.psum_tensor(f"A{i}", [128, 512], F32)) for i in range(2)]
        BIG = [es.enter_context(nc.psum_tensor(f"BIG{i}", [128, DM], F32)) for i in range(2)]
        ps_tr = [es.enter_context(nc.psum_tensor(f"pstr{i}", [128, 128], BF16)) for i in range(2)]

        gp = bld.pool("gp", 6)
        wp = bld.pool("wp", 4)
        yp = bld.pool("yp", 6)
        tok_const = []
        tok_xt = {}
        tok_w = {}
        tok_xres = {}
        tok_store = {}
        TRS = []
        xlast = {}
        vlast = {}

        PERB = NTG * (A1G + A2G)

        def a1(b, tg):
            return b * PERB + tg * A1G

        def a2(b, qg):
            return b * PERB + NTG * A1G + qg * A2G

        def big2(b, qg):
            return (b * NTG + qg) * NSUB

        def wl_k(b, tg, dc):
            return (b, tg * 16 + dc)

        def wl_v(b, tg, dc):
            return (b, tg * 16 + 8 + dc)

        def wl_q(b, qg, dc):
            return (b, 128 + qg * CC + dc)

        WMARK = {}

        def t_body(t):
            t.wait(*tok_const)
            trj = [0]

            def transpose(kind, key, src_sb, src_col, dst_col, waitfor=None):
                j = trj[0]
                if t.bld.pass0:
                    TRS.append((kind, dst_col))
                if j >= 2:
                    t.wait(t.tok(("trv3", j - 2)))
                if waitfor is not None:
                    t.wait(waitfor)
                t.op("transpose", ps_tr[j % 2][:],
                     src_sb[:, src_col:src_col + 128], idb[:],
                     tick=True, mark=("tr3", j))
                if t.bld.pass0:
                    key[0][key[1]] = j
                trj[0] += 1

            if t.bld.pass0:
                TRS.clear()
            for b in range(BPC):
                for tg in range(NTG):
                    for sub in range(NSUB):
                        tokx = tok_xt.get((0, b, tg, sub))
                        for c in range(CC):
                            transpose("x", (xlast, (0, b, tg, sub)),
                                      xtm[(b * 2 * NT + tg * NSUB + sub) % 2],
                                      c * 128, c * TGS + sub * 128,
                                      waitfor=(tokx if c == 0 else None))
                    for dc in range(CC):
                        ai = a1(b, tg) + dc
                        t.wait(tok_w.get(wl_k(b, tg, dc)))
                        if ai >= 2:
                            t.wait(t.tok(("Aev", ai - 2)))
                        t.wait(t.tok(("trv3", xlast.get((0, b, tg, NSUB - 1), 0))))
                        for c in range(CC):
                            t.op("matmul", A[ai % 2][:],
                                 wsl[(tg * 16 + dc) % 2][:, c * 128:(c + 1) * 128],
                                 xg_sb[:, c * TGS:(c + 1) * TGS],
                                 start=(c == 0), stop=(c == CC - 1),
                                 tick=(c == CC - 1), mark=(("Agrp", ai) if c == CC - 1 else None))
                    for dc in range(CC):
                        ai = a1(b, tg) + 8 + dc
                        t.wait(tok_w.get(wl_v(b, tg, dc)))
                        if ai >= 2:
                            t.wait(t.tok(("Aev", ai - 2)))
                        for c in range(CC):
                            t.op("matmul", A[ai % 2][:],
                                 wsl[(tg * 16 + 8 + dc) % 2][:, c * 128:(c + 1) * 128],
                                 xg_sb[:, c * TGS:(c + 1) * TGS],
                                 start=(c == 0), stop=(c == CC - 1),
                                 tick=(c == CC - 1), mark=(("Agrp", ai) if c == CC - 1 else None))
                        for sub in range(NSUB):
                            transpose("v", (vlast, (b, tg, dc)),
                                      vfm_sb, sub * 128,
                                      (tg * NSUB + sub) * DM + dc * 128,
                                      waitfor=(t.tok(("vev", b, tg, dc)) if sub == 0 else None))
                    ai = a1(b, tg) + 16
                    if ai >= 2:
                        t.wait(t.tok(("Aev", ai - 2)))
                    for dc in range(CC):
                        t.wait(t.tok(("ksq", b, tg, dc)))
                        t.op("matmul", A[ai % 2][0:1, :], ones_c[:], sqg[:, 0:TGS],
                             start=(dc == 0), stop=(dc == CC - 1),
                             tick=True, mark=("nmm1", b, tg, dc))
                    bld.marks[("Agrp", ai)] = bld.marks[("nmm1", b, tg, CC - 1)]
                for qg in range(NTG):
                    if qg == 0:
                        t.wait(t.tok(("Aev", a1(b, NTG - 1) + 7)))
                        t.wait(t.tok(("trv3", vlast.get((b, NTG - 1, CC - 1), 0))))
                    for sub in range(NSUB):
                        tokx = tok_xt.get((1, b, qg, sub))
                        for c in range(CC):
                            transpose("x", (xlast, (1, b, qg, sub)),
                                      xtm[(b * 2 * NT + NT + qg * NSUB + sub) % 2],
                                      c * 128, c * TGS + sub * 128,
                                      waitfor=(tokx if c == 0 else None))
                    for dc in range(CC):
                        ai = a2(b, qg) + dc
                        t.wait(tok_w.get(wl_q(b, qg, dc)))
                        t.wait(t.tok(("Aev", ai - 2)))
                        t.wait(t.tok(("trv3", xlast.get((1, b, qg, NSUB - 1), 0))))
                        for c in range(CC):
                            t.op("matmul", A[ai % 2][:],
                                 wsl[(128 + qg * CC + dc) % 2][:, c * 128:(c + 1) * 128],
                                 xg_sb[:, c * TGS:(c + 1) * TGS],
                                 start=(c == 0), stop=(c == CC - 1),
                                 tick=(c == CC - 1), mark=(("Agrp", ai) if c == CC - 1 else None))
                    ai = a2(b, qg) + 8
                    t.wait(t.tok(("Aev", ai - 2)))
                    for dc in range(CC):
                        t.wait(t.tok(("qsq", b, qg, dc)))
                        t.op("matmul", A[ai % 2][0:1, :], ones_c[:], sqg[:, 0:TGS],
                             start=(dc == 0), stop=(dc == CC - 1),
                             tick=True, mark=("nmm2", b, qg, dc))
                    bld.marks[("Agrp", ai)] = bld.marks[("nmm2", b, qg, CC - 1)]
                    ai = a2(b, qg) + 9
                    t.wait(t.tok(("Aev", ai - 2)))
                    t.wait(t.tok(("negb", b, qg)))
                    t.op("matmul", A[ai % 2][:, 0:1], ones_r[:], stat[0:1, 3:4],
                         start=True, stop=True, tick=True, mark=("Agrp", ai))
                    for kc in range(NKP):
                        ai = a2(b, qg) + 10 + kc
                        t.wait(t.tok(("Aev", ai - 2)))
                        for dc in range(CC):
                            t.op("matmul", A[ai % 2][:],
                                 kfm[:, dc * S + kc * 128: dc * S + kc * 128 + 128],
                                 qg_sb[:, dc * TGS:(dc + 1) * TGS],
                                 start=(dc == 0), stop=(dc == CC - 1),
                                 tick=(dc == CC - 1), mark=(("Agrp", ai) if dc == CC - 1 else None))
                    t.wait(t.tok(("Aev", a2(b, qg) + 41)))
                    for sub in range(NSUB):
                        ai = a2(b, qg) + 42 + sub
                        t.wait(t.tok(("Aev", ai - 2)))
                        for kc in range(NKP):
                            t.op("matmul", A[ai % 2][:, 0:1],
                                 p_sb[:, kc * TGS + sub * 128: kc * TGS + sub * 128 + 128],
                                 ones_c[:],
                                 start=(kc == 0), stop=(kc == NKP - 1),
                                 tick=(kc == NKP - 1), mark=(("Agrp", ai) if kc == NKP - 1 else None))
                    for sub in range(NSUB):
                        bi = big2(b, qg) + sub
                        if bi >= 2:
                            t.wait(t.tok(("Bev", bi - 2)))
                        for kc in range(NKP):
                            for nh in range(2):
                                t.op("matmul", BIG[bi % 2][:, nh * 512:(nh + 1) * 512],
                                     p_sb[:, kc * TGS + sub * 128: kc * TGS + sub * 128 + 128],
                                     vtm[:, kc * DM + nh * 512: kc * DM + (nh + 1) * 512],
                                     start=(kc == 0), stop=(kc == NKP - 1),
                                     tick=(kc == NKP - 1 and nh == 1),
                                     mark=(("Bgrp", bi) if (kc == NKP - 1 and nh == 1) else None))

        def s_body(s):
            for b in range(BPC):
                for tg in range(NTG):
                    for dc in range(CC):
                        ai = a1(b, tg) + dc
                        s.wait(s.tok(("Agrp", ai)))
                        if (b, tg) != (0, 0) or dc >= 1:
                            pass
                        s.copy(kfm[:, dc * S + tg * TGS: dc * S + (tg + 1) * TGS],
                               A[ai % 2][:], mark=("Aev", ai))
                    for dc in range(CC):
                        ai = a1(b, tg) + 8 + dc
                        s.wait(s.tok(("Agrp", ai)))
                        if dc >= 1:
                            m = vlast.get((b, tg, dc - 1))
                            s.wait(s.tok(("tr3", m)) if m is not None else None)
                        s.copy(vfm_sb[:], A[ai % 2][:], mark=("Aev", ai))
                        bld.marks[("vev", b, tg, dc)] = bld.marks[("Aev", ai)]
                    for dc in range(CC):
                        if dc > 0:
                            s.wait(s.tok(("nmm1", b, tg, dc - 1)))
                        s.activation(sqg[:, 0:TGS],
                                     kfm[:, dc * S + tg * TGS: dc * S + (tg + 1) * TGS],
                                     AF.Square, mark=("ksq", b, tg, dc))
                for qg in range(NTG):
                    for dc in range(CC):
                        ai = a2(b, qg) + dc
                        s.wait(s.tok(("Agrp", ai)))
                        s.copy(qg_sb[:, dc * TGS:(dc + 1) * TGS],
                               A[ai % 2][:], mark=("Aev", ai))
                    for dc in range(CC):
                        if dc > 0:
                            s.wait(s.tok(("nmm2", b, qg, dc - 1)))
                        s.activation(sqg[:, 0:TGS],
                                     qg_sb[:, dc * TGS:(dc + 1) * TGS],
                                     AF.Square, mark=("qsq", b, qg, dc))
                    s.wait(s.tok(("b2", b, qg)))
                    s.activation(stat[0:1, 2:3], stat[0:1, 2:3], AF.Sqrt,
                                 mark=("sqrtb", b, qg))
                    for kc in range(NKP):
                        ai = a2(b, qg) + 10 + kc
                        s.wait(s.tok(("Agrp", ai)))
                        s.wait(s.tok(("negbev", b, qg)))
                        if (b, qg) != (0, 0) and kc == 0:
                            s.wait(s.tok(("Bgrp", big2(b, qg) - 1)))
                        s.activation(p_sb[:, kc * TGS:(kc + 1) * TGS],
                                     A[ai % 2][:], AF.Exp, bias=negb_sb[:, 0:1],
                                     scale=SCALE, mark=("Aev", ai))
                    for sub in range(NSUB):
                        ln.s_stats(s, b * NT + qg * NSUB + sub)

        def v_body2(v):
            trv = [0]

            def evacs(n):
                for _ in range(n):
                    j = trv[0]
                    kind, dst_col = TRS[j] if TRS else ("x", 0)
                    v.wait(v.tok(("tr3", j)))
                    dst = xg_sb if kind == "x" else vtm
                    v.tensor_copy(dst[:, dst_col:dst_col + 128], ps_tr[j % 2][:],
                                  mark=("trv3", j))
                    trv[0] += 1
            for b in range(BPC):
                for tg in range(NTG):
                    evacs(32)
                    ai = a1(b, tg) + 16
                    for dc in range(CC):
                        evacs(4)
                    v.wait(v.tok(("nmm1", b, tg, CC - 1)))
                    v.reduce_max(stat[0:1, 4:5], A[ai % 2][0:1, :], axis=AX.X,
                                 mark=("Aev", ai))
                    if tg == 0 and b == 0:
                        v.tensor_copy(stat[0:1, 0:1], stat[0:1, 4:5])
                    elif tg == 0:
                        v.tensor_copy(stat[0:1, 0:1], stat[0:1, 4:5])
                    else:
                        v.tensor_max(stat[0:1, 0:1], stat[0:1, 0:1], stat[0:1, 4:5])
                for qg in range(NTG):
                    evacs(32)
                    ai = a2(b, qg) + 8
                    v.wait(v.tok(("nmm2", b, qg, CC - 1)))
                    v.reduce_max(stat[0:1, 1:2], A[ai % 2][0:1, :], axis=AX.X,
                                 mark=("Aev", ai))
                    v.tensor_mul(stat[0:1, 2:3], stat[0:1, 0:1], stat[0:1, 1:2],
                                 mark=("b2", b, qg))
                    v.wait(v.tok(("sqrtb", b, qg)))
                    v.tensor_scalar_mul(stat[0:1, 3:4], stat[0:1, 2:3], -SCALE,
                                        mark=("negb", b, qg))
                    nbai = a2(b, qg) + 9
                    v.wait(v.tok(("Agrp", nbai)))
                    v.tensor_copy(negb_sb[:], A[nbai % 2][:, 0:1],
                                  mark=("negbev", b, qg))
                    bld.marks[("Aev", nbai)] = bld.marks[("negbev", b, qg)]
                    for sub in range(NSUB):
                        ai = a2(b, qg) + 42 + sub
                        v.wait(v.tok(("Agrp", ai)))
                        v.reciprocal(recd_sb[:, sub:sub + 1], A[ai % 2][:, 0:1],
                                     mark=("Aev", ai))
                    for sub in range(NSUB):
                        i = b * NT + qg * NSUB + sub
                        bi = big2(b, qg) + sub
                        v.wait(v.tok(("Bgrp", bi)))
                        v.wait(tok_xres.get(i))
                        if i >= 2:
                            v.wait(v.tok(("L", "sq", i - 2)))
                            v.wait(tok_store.get(i - 2))
                        v.scalar_tensor_tensor(
                            ln.sum_sb[i % 2][:], BIG[bi % 2][:],
                            recd_sb[:, sub:sub + 1], xres[i % 2][:],
                            ALU.mult, ALU.add, mark=("Bev", bi))
                        ln.v_stats(v, i)

        def g_body(g):
            tok_const.append(gp.load(g.eng, idb[:], ident[:, :]))
            tok_const.append(gp.load(g.eng, ones_c[:], onescol[:, :]))
            tok_const.append(gp.load(g.eng, ones_r[:], onesrow[:, :]))
            nx = [0]

            def xload(ph, b, gg, sub):
                if nx[0] >= 2:
                    k = None
                    pb, pph, pgg, psub = xorder[nx[0] - 2]
                    m = xlast.get((pph, pb, pgg, psub))
                    if m is not None:
                        g.wait(g.tok(("tr3", m)))
                tok_xt[(ph, b, gg, sub)] = gp.load(
                    g.eng, xtm[nx[0] % 2][:],
                    x_in[b, (gg * NSUB + sub) * 128:(gg * NSUB + sub + 1) * 128, :])
                nx[0] += 1

            def wload(key, li_b, src_ap):
                b, li = key
                if li >= 2:
                    m = WMARK.get((b, li - 2))
                    if m is not None:
                        g.wait(g.tok(m))
                tok_w[key] = wp.load(
                    g.eng, wsl[li % 2].ap().rearrange("p (c f) -> p c f", c=CC),
                    src_ap)

            xorder = []
            for b in range(BPC):
                for ph in range(2):
                    for gg in range(NTG):
                        for sub in range(NSUB):
                            xorder.append((b, ph, gg, sub))
            xi = [0]

            def xnext():
                b, ph, gg, sub = xorder[xi[0]]
                xi[0] += 1
                xload(ph, b, gg, sub)

            for b in range(BPC):
                for tg in range(NTG):
                    for _ in range(NSUB):
                        xnext()
                    for dc in range(CC):
                        key = wl_k(b, tg, dc)
                        WMARK[key] = ("Agrp", a1(b, tg) + dc)
                        wload(key, None,
                              wqkvT[:, DM + dc * 128: DM + (dc + 1) * 128]
                              .rearrange("(c p) f -> p c f", p=128))
                    for dc in range(CC):
                        key = wl_v(b, tg, dc)
                        WMARK[key] = ("Agrp", a1(b, tg) + 8 + dc)
                        wload(key, None,
                              wqkvT[:, 2 * DM + dc * 128: 2 * DM + (dc + 1) * 128]
                              .rearrange("(c p) f -> p c f", p=128))
                for qg in range(NTG):
                    for _ in range(NSUB):
                        xnext()
                    for dc in range(CC):
                        key = wl_q(b, qg, dc)
                        WMARK[key] = ("Agrp", a2(b, qg) + dc)
                        wload(key, None,
                              wqkvT[:, dc * 128:(dc + 1) * 128]
                              .rearrange("(c p) f -> p c f", p=128))

        def y_body(y):
            NST = BPC * NT
            for b in range(BPC):
                for i_loc in range(NT):
                    i = b * NT + i_loc
                    if i >= 2:
                        y.wait(y.tok(("L", "sum", i - 2)))
                    tok_xres[i] = yp.load(
                        y.eng, xres[i % 2][:],
                        x_in[b, i_loc * 128:(i_loc + 1) * 128, :])
                    if i >= 2:
                        j = i - 2
                        bb, ss = divmod(j, NT)
                        tok_store[j] = ln.store(y, yp, j,
                                                x_out[bb, ss * 128:(ss + 1) * 128, :])
            for j in (NST - 2, NST - 1):
                bb, ss = divmod(j, NT)
                tok_store[j] = ln.store(y, yp, j, x_out[bb, ss * 128:(ss + 1) * 128, :])
            if y.eng is not None:
                for tkn in yp.snapshot():
                    y.eng.wait_ge(tkn[1].sems[tkn[2]], tkn[3])

        bld.run_stage(nc, es, ges, [("t", t_body), ("s", s_body),
                                    ("v", v_body2), ("g", g_body),
                                    ("y", y_body)])


def build_stage3():
    if "s3" in _cache:
        return _cache["s3"]
    nc0 = bass.Bass("TRN2", target_bir_lowering=False, debug=False)
    nc = _NcAlias(nc0)
    ges = contextlib.ExitStack()
    with ges:
        _emit_stage3(nc, ges, Build(), _s3_drams(nc))
    _cache["s3"] = nc0
    return nc0



def _s4_drams(nc):
    return dict(
        x_in=nc.dram_tensor("x_in", [S, DM], F32, kind="ExternalInput").ap(),
        wcat1=nc.dram_tensor("wcat1", [17, DM, 2048], F32, kind="ExternalInput").ap(),
        wcat2=nc.dram_tensor("wcat2", [17, 2048, DM], F32, kind="ExternalInput").ap(),
        b1cat=nc.dram_tensor("b1cat", [128, 17 * 16], F32, kind="ExternalInput").ap(),
        sb2b=nc.dram_tensor("sb2b", [128, DM], F32, kind="ExternalInput").ap(),
        rb2=nc.dram_tensor("rb2", [15, DM], F32, kind="ExternalInput").ap(),
        gw=nc.dram_tensor("gw", [DM, 15], F32, kind="ExternalInput").ap(),
        gbb=nc.dram_tensor("gbb", [128, 15], F32, kind="ExternalInput").ap(),
        ident=nc.dram_tensor("ident", [128, 128], F32, kind="ExternalInput").ap(),
        x_out=nc.dram_tensor("x_out", [S, DM], F32, kind="ExternalOutput").ap())


def _emit_stage4(nc, ges, bld, A):
    x_in = A["x_in"]; wcat1 = A["wcat1"]; wcat2 = A["wcat2"]
    b1cat = A["b1cat"]; sb2b = A["sb2b"]; rb2 = A["rb2"]; gw = A["gw"]
    gbb = A["gbb"]; ident = A["ident"]; x_out = A["x_out"]

    BLKT = 1024
    NBLK = S // BLKT
    NSUB = BLKT // 128
    NE = 17
    FC = 2048 // 128
    NTSUB = S // 128

    es = contextlib.ExitStack()
    with es:
        xfm = es.enter_context(nc.sbuf_tensor("xfm", [128, CC * BLKT], BF16))
        xtm = [es.enter_context(nc.sbuf_tensor(f"xtm{i}", [128, DM], BF16)) for i in range(2)]
        acc = es.enter_context(nc.sbuf_tensor("acc", [128, NSUB * DM], F32))
        h_sb = es.enter_context(nc.sbuf_tensor("h_sb", [128, FC * BLKT], BF16))
        w2_sb = es.enter_context(nc.sbuf_tensor("w2_sb", [128, FC * DM], BF16))
        wsl = [es.enter_context(nc.sbuf_tensor(f"wsl{i}", [128, CC * 128], BF16)) for i in range(2)]
        gated = es.enter_context(nc.sbuf_tensor("gated", [128, NSUB * 15], F32))
        gtb = es.enter_context(nc.sbuf_tensor("gtb", [15, BLKT], BF16))
        wso = es.enter_context(nc.sbuf_tensor("wso", [128, 15], F32))
        wwk = es.enter_context(nc.sbuf_tensor("wwk", [128, 15], F32))
        wmk = es.enter_context(nc.sbuf_tensor("wmk", [128, 15], F32))
        gst = es.enter_context(nc.sbuf_tensor("gst", [128, 4], F32))
        gw_sb = es.enter_context(nc.sbuf_tensor("gw_sb", [128, CC * 15], BF16))
        rb2_sb = es.enter_context(nc.sbuf_tensor("rb2_sb", [15, DM], BF16))
        sb2_sb = es.enter_context(nc.sbuf_tensor("sb2_sb", [128, DM], BF16))
        b1_sb = es.enter_context(nc.sbuf_tensor("b1_sb", [128, 17 * 16], F32))
        gb_sb = es.enter_context(nc.sbuf_tensor("gb_sb", [128, 15], F32))
        idb = es.enter_context(nc.sbuf_tensor("idb", [128, 128], BF16))
        idf = es.enter_context(nc.sbuf_tensor("idf", [128, 128], F32))
        xres = [es.enter_context(nc.sbuf_tensor(f"xres{i}", [128, DM], F32)) for i in range(2)]
        ln = LnExit(nc, es, "L")
        A = [es.enter_context(nc.psum_tensor(f"A{i}", [128, 512], F32)) for i in range(2)]
        BIG = [es.enter_context(nc.psum_tensor(f"BIG{i}", [128, DM], F32)) for i in range(2)]
        ps_tr = [es.enter_context(nc.psum_tensor(f"pstr{i}", [128, 128], BF16)) for i in range(2)]

        gp = bld.pool("gp", 6)
        wp = bld.pool("wp", 6)
        yp = bld.pool("yp", 6)
        tok_const = []
        tok_xt = {}
        tok_w1 = {}
        tok_w2 = {}
        tok_xres = {}
        tok_store = {}

        APB = 16 + 32 * NE

        def ag_log(blk, sub):
            return blk * APB + sub

        def ag_gt(blk, sub):
            return blk * APB + 8 + sub

        def ag_h(blk, e, g):
            return blk * APB + 16 + e * 32 + g

        BPB = (NE + 1) * NSUB

        def bg(blk, e, sub):
            return blk * BPB + e * NSUB + sub

        def t_body(t):
            t.wait(*tok_const)
            trj = [0]
            for blk in range(NBLK):
                for sub in range(NSUB):
                    nx = blk * NSUB + sub
                    t.wait(tok_xt.get((blk, sub)))
                    for c in range(CC):
                        j = trj[0]
                        if j >= 2:
                            t.wait(t.tok(("trv4", j - 2)))
                        t.op("transpose", ps_tr[j % 2][:],
                             xtm[nx % 2][:, c * 128:(c + 1) * 128], idb[:],
                             tick=True, mark=("tr4", j))
                        trj[0] += 1
                for sub in range(NSUB):
                    ai = ag_log(blk, sub)
                    if ai >= 2:
                        t.wait(t.tok(("Aev", ai - 2)))
                    t.wait(t.tok(("trv4", blk * NSUB * CC + sub * CC + CC - 1)))
                    for c in range(CC):
                        t.op("matmul", A[ai % 2][0:128, 0:15],
                             xfm[:, c * BLKT + sub * 128: c * BLKT + sub * 128 + 128],
                             gw_sb[:, c * 15:(c + 1) * 15],
                             start=(c == 0), stop=(c == CC - 1),
                             tick=(c == CC - 1), mark=(("Agrp", ai) if c == CC - 1 else None))
                for sub in range(NSUB):
                    ai = ag_gt(blk, sub)
                    t.wait(t.tok(("Aev", ai - 2)))
                    t.wait(t.tok(("gtm", blk, sub)))
                    t.op("transpose", A[ai % 2][0:15, 0:128],
                         gated[:, sub * 15:(sub + 1) * 15], idf[:],
                         tick=True, mark=("Agrp", ai))
                for e in range(NE):
                    for fc in range(FC):
                        li = e * FC + fc
                        t.wait(tok_w1.get((blk, li)))
                        for th in range(2):
                            ai = ag_h(blk, e, fc * 2 + th)
                            t.wait(t.tok(("Aev", ai - 2)))
                            for c in range(CC):
                                t.op("matmul", A[ai % 2][:],
                                     wsl[li % 2][:, c * 128:(c + 1) * 128],
                                     xfm[:, c * BLKT + th * 512: c * BLKT + (th + 1) * 512],
                                     start=(c == 0), stop=(c == CC - 1),
                                     tick=(c == CC - 1), mark=(("Agrp", ai) if c == CC - 1 else None))
                    for sub in range(NSUB):
                        bi = bg(blk, e, sub)
                        if bi >= 2:
                            t.wait(t.tok(("Bev", bi - 2)))
                        if sub == 0:
                            for fc in range(FC):
                                t.wait(tok_w2.get((blk, e, fc)))
                            t.wait(t.tok(("hev", blk, e, FC * 2 - 1)))
                        for fc in range(FC):
                            for nh in range(2):
                                t.op("matmul", BIG[bi % 2][:, nh * 512:(nh + 1) * 512],
                                     h_sb[:, fc * BLKT + sub * 128: fc * BLKT + sub * 128 + 128],
                                     w2_sb[:, fc * DM + nh * 512: fc * DM + (nh + 1) * 512],
                                     start=(fc == 0), stop=(fc == FC - 1),
                                     tick=(fc == FC - 1 and nh == 1),
                                     mark=(("Bgrp", bi) if (fc == FC - 1 and nh == 1) else None))
                for sub in range(NSUB):
                    bi = bg(blk, NE, sub)
                    t.wait(t.tok(("Bev", bi - 2)))
                    if sub == 0:
                        t.wait(t.tok(("gtev", blk, NSUB - 1)))
                    for nh in range(2):
                        t.op("matmul", BIG[bi % 2][:, nh * 512:(nh + 1) * 512],
                             gtb[:, sub * 128:(sub + 1) * 128],
                             rb2_sb[:, nh * 512:(nh + 1) * 512],
                             start=True, stop=True,
                             tick=(nh == 1), mark=(("Bgrp", bi) if nh == 1 else None))

        def s_body(s):
            for blk in range(NBLK):
                for sub in range(NSUB):
                    s.wait(s.tok(("sfte", blk, sub)))
                    s.activation(wmk[:], wwk[:], AF.Exp,
                                 accum_out=gst[:, 1:2], mark=("sftx", blk, sub))
                for e in range(NE):
                    for fc in range(FC):
                        for th in range(2):
                            g = fc * 2 + th
                            ai = ag_h(blk, e, g)
                            s.wait(s.tok(("Agrp", ai)))
                            if g == 0 and (blk > 0 or e > 0):
                                pe = (blk, e - 1) if e > 0 else (blk - 1, NE - 1)
                                s.wait(s.tok(("Bgrp", bg(pe[0], pe[1], NSUB - 1))))
                            s.activation(
                                h_sb[:, fc * BLKT + th * 512: fc * BLKT + (th + 1) * 512],
                                A[ai % 2][:], AF.Gelu,
                                bias=b1_sb[:, e * 16 + fc: e * 16 + fc + 1],
                                mark=("Aev", ai))
                            bld.marks[("hev", blk, e, g)] = bld.marks[("Aev", ai)]
                for sub in range(NSUB):
                    ln.s_stats(s, blk * NSUB + sub)

        def v_body(v):
            TRV = []
            for blk in range(NBLK):
                for sub in range(NSUB):
                    for c in range(CC):
                        j = blk * NSUB * CC + sub * CC + c
                        v.wait(v.tok(("tr4", j)))
                        v.tensor_copy(
                            xfm[:, c * BLKT + sub * 128: c * BLKT + sub * 128 + 128],
                            ps_tr[j % 2][:], mark=("trv4", j))
                for sub in range(NSUB):
                    ai = ag_log(blk, sub)
                    v.wait(v.tok(("Agrp", ai)))
                    v.tensor_add(wso[:], A[ai % 2][0:128, 0:15], gb_sb[:],
                                 mark=("Aev", ai))
                    v.reduce_max(gst[:, 0:1], wso[:], axis=AX.X)
                    v.tensor_scalar(wwk[:], wso[:], gst[:, 0:1], None, ALU.subtract,
                                    mark=("sfte", blk, sub))
                    v.wait(v.tok(("sftx", blk, sub)))
                    v.reciprocal(gst[:, 2:3], gst[:, 1:2])
                    v.tensor_scalar(wso[:], wmk[:], gst[:, 2:3], None, ALU.mult)
                    v.tensor_copy(wwk[:], wso[:])
                    for r in range(3):
                        v.reduce_max(gst[:, 3:4], wwk[:], axis=AX.X)
                        v.tensor_scalar(wmk[:], wwk[:], gst[:, 3:4], None, ALU.is_ge)
                        if r < 2:
                            v.scalar_tensor_tensor(wwk[:], wmk[:], -1e9, wwk[:],
                                                   ALU.mult, ALU.add)
                    v.tensor_scalar(wmk[:], wso[:], gst[:, 3:4], None, ALU.is_ge)
                    v.tensor_mul(gated[:, sub * 15:(sub + 1) * 15], wmk[:], wso[:],
                                 mark=("gtm", blk, sub))
                for sub in range(NSUB):
                    ai = ag_gt(blk, sub)
                    v.wait(v.tok(("Agrp", ai)))
                    v.tensor_copy(gtb[:, sub * 128:(sub + 1) * 128],
                                  A[ai % 2][0:15, 0:128], mark=("Aev", ai))
                    bld.marks[("gtev", blk, sub)] = bld.marks[("Aev", ai)]
                for e in range(NE):
                    for sub in range(NSUB):
                        bi = bg(blk, e, sub)
                        v.wait(v.tok(("Bgrp", bi)))
                        a_sl = acc[:, sub * DM:(sub + 1) * DM]
                        if e == 0:
                            v.tensor_add(a_sl, BIG[bi % 2][:], sb2_sb[:],
                                         mark=("Bev", bi))
                        elif e == 1:
                            v.scalar_tensor_tensor(a_sl, BIG[bi % 2][:], 1.0,
                                                   a_sl, ALU.mult, ALU.add,
                                                   mark=("Bev", bi))
                        else:
                            v.scalar_tensor_tensor(
                                a_sl, BIG[bi % 2][:],
                                gated[:, sub * 15 + (e - 2): sub * 15 + (e - 1)],
                                a_sl, ALU.mult, ALU.add, mark=("Bev", bi))
                for sub in range(NSUB):
                    bi = bg(blk, NE, sub)
                    i = blk * NSUB + sub
                    v.wait(v.tok(("Bgrp", bi)))
                    a_sl = acc[:, sub * DM:(sub + 1) * DM]
                    v.scalar_tensor_tensor(a_sl, BIG[bi % 2][:], 1.0, a_sl,
                                           ALU.mult, ALU.add, mark=("Bev", bi))
                    v.wait(tok_xres.get(i))
                    if i >= 2:
                        v.wait(v.tok(("L", "sq", i - 2)))
                        v.wait(tok_store.get(i - 2))
                    v.scalar_tensor_tensor(ln.sum_sb[i % 2][:], xres[i % 2][:],
                                           2.0, a_sl, ALU.mult, ALU.add)
                    ln.v_stats(v, i)

        def g_body(g):
            tok_const.append(gp.load(g.eng, idb[:], ident[:, :]))
            for c in range(CC):
                tok_const.append(gp.load(g.eng, gw_sb[:, c * 15:(c + 1) * 15],
                                         gw[c * 128:(c + 1) * 128, :]))
            tok_const.append(gp.load(g.eng, rb2_sb[:], rb2[:, :]))
            tok_const.append(gp.load(g.eng, sb2_sb[:], sb2b[:, :]))
            for blk in range(NBLK):
                for sub in range(NSUB):
                    nx = blk * NSUB + sub
                    if nx >= 2:
                        g.wait(g.tok(("tr4", (nx - 1) * CC - 1)))
                    tok_xt[(blk, sub)] = gp.load(
                        g.eng, xtm[nx % 2][:],
                        x_in[(blk * NSUB + sub) * 128:(blk * NSUB + sub + 1) * 128, :])
                for e in range(NE):
                    for fc in range(FC):
                        li = e * FC + fc
                        gli = blk * NE * FC + li
                        if gli >= 2:
                            pblk, pli = divmod(gli - 2, NE * FC)
                            pe_, pfc = divmod(pli, FC)
                            m = ("Agrp", ag_h(pblk, pe_, pfc * 2 + 1))
                            g.wait(g.tok(m))
                        tok_w1[(blk, li)] = wp.load(
                            g.eng, wsl[li % 2].ap().rearrange("p (c f) -> p c f", c=CC),
                            wcat1[e, :, fc * 128:(fc + 1) * 128]
                            .rearrange("(c p) f -> p c f", p=128))
                    for fc in range(FC):
                        if blk > 0 or e > 0:
                            pe = (blk, e - 1) if e > 0 else (blk - 1, NE - 1)
                            g.wait(g.tok(("Bgrp", bg(pe[0], pe[1], NSUB - 1))))
                        tok_w2[(blk, e, fc)] = wp.load(
                            g.eng, w2_sb[:, fc * DM:(fc + 1) * DM],
                            wcat2[e, fc * 128:(fc + 1) * 128, :])

        def y_body(y):
            for i in range(NTSUB):
                if i >= 2:
                    y.wait(y.tok(("L", "sum", i - 2)))
                tok_xres[i] = yp.load(
                    y.eng, xres[i % 2][:], x_in[i * 128:(i + 1) * 128, :])
                if i >= 2:
                    j = i - 2
                    tok_store[j] = ln.store(y, yp, j,
                                            x_out[j * 128:(j + 1) * 128, :])
            for j in (NTSUB - 2, NTSUB - 1):
                tok_store[j] = ln.store(y, yp, j, x_out[j * 128:(j + 1) * 128, :])
            if y.eng is not None:
                for tkn in yp.snapshot():
                    y.eng.wait_ge(tkn[1].sems[tkn[2]], tkn[3])
        def y0_body(y):
            tok_const.append(yp.load(y.eng, idf[:], ident[:, :]))
            tok_const.append(yp.load(y.eng, b1_sb[:], b1cat[:, :]))
            tok_const.append(yp.load(y.eng, gb_sb[:], gbb[:, :]))

        bld.run_stage(nc, es, ges, [("t", t_body), ("s", s_body),
                                    ("v", v_body), ("g", g_body),
                                    ("y", y0_body), ("y", y_body)])


def build_stage4():
    if "s4" in _cache:
        return _cache["s4"]
    nc0 = bass.Bass("TRN2", target_bir_lowering=False, debug=False)
    nc = _NcAlias(nc0)
    ges = contextlib.ExitStack()
    with ges:
        _emit_stage4(nc, ges, Build(), _s4_drams(nc))
    _cache["s4"] = nc0
    return nc0



def _s1_drams(nc):
    return dict(
        x_in=nc.dram_tensor("x_in", [S, DM], F32, kind="ExternalInput").ap(),
        wqT=nc.dram_tensor("wqT", [DM, DM], F32, kind="ExternalInput").ap(),
        wkT=nc.dram_tensor("wkT", [DM, DM], F32, kind="ExternalInput").ap(),
        wvT=nc.dram_tensor("wvT", [DM, DM], F32, kind="ExternalInput").ap(),
        wo=nc.dram_tensor("wo", [DM, DM], F32, kind="ExternalInput").ap(),
        wck=nc.dram_tensor("wck", [128, 32 * 64], F32, kind="ExternalInput").ap(),
        wcv=nc.dram_tensor("wcv", [128, 32 * 64], F32, kind="ExternalInput").ap(),
        bck=nc.dram_tensor("bck", [64, 1], F32, kind="ExternalInput").ap(),
        bcv=nc.dram_tensor("bcv", [64, 1], F32, kind="ExternalInput").ap(),
        wgm=nc.dram_tensor("wgm", [128, 3], F32, kind="ExternalInput").ap(),
        bgm=nc.dram_tensor("bgm", [1, 3], F32, kind="ExternalInput").ap(),
        amat=nc.dram_tensor("amat", [2, 128, S], F32, kind="ExternalInput").ap(),
        ident=nc.dram_tensor("ident", [128, 128], F32, kind="ExternalInput").ap(),
        onescol=nc.dram_tensor("onescol", [128, 1], F32, kind="ExternalInput").ap(),
        onesrow=nc.dram_tensor("onesrow", [1, 128], F32, kind="ExternalInput").ap(),
        x_out=nc.dram_tensor("x_out", [S, DM], F32, kind="ExternalOutput").ap())


def _emit_stage1(nc, ges, bld, A):
    x_in = A["x_in"]; wqT = A["wqT"]; wkT = A["wkT"]; wvT = A["wvT"]
    wo = A["wo"]; wck = A["wck"]; wcv = A["wcv"]; bck = A["bck"]
    bcv = A["bcv"]; wgm = A["wgm"]; bgm = A["bgm"]; amat = A["amat"]
    ident = A["ident"]; onescol = A["onescol"]; onesrow = A["onesrow"]
    x_out = A["x_out"]

    NT = S // 128
    TGS = 512
    NTG = S // TGS
    NSUB = TGS // 128
    NH = 16
    HP = 8
    NB = 255
    NKC = 32
    SC = 0.125

    es = contextlib.ExitStack()
    with es:
        HSTR = 66
        CSTR = NH * HSTR
        kfm = es.enter_context(nc.sbuf_tensor("kfm", [128, CC * S], BF16))
        vtm = es.enter_context(nc.sbuf_tensor("vtm", [128, NKC * CSTR], BF16))
        prT2 = es.enter_context(nc.sbuf_tensor("prT2", [128, TGS], BF16))
        mcol_sb = es.enter_context(nc.sbuf_tensor("mcol_sb", [128, NKC], F32))
        ones16 = es.enter_context(nc.sbuf_tensor("ones16", [128, 16], BF16))
        qg_sb = es.enter_context(nc.sbuf_tensor("qg_sb", [128, CC * TGS], BF16))
        xg_sb = es.enter_context(nc.sbuf_tensor("xg_sb", [128, CC * TGS], BF16))
        prT = [es.enter_context(nc.sbuf_tensor(f"prT{i}", [128, TGS], BF16)) for i in range(2)]
        ptr_sb = [es.enter_context(nc.sbuf_tensor(f"ptr{i}", [128, 128], BF16)) for i in range(2)]
        oq_sb = es.enter_context(nc.sbuf_tensor("oq_sb", [128, NSUB * NH * 64], BF16))
        ofm = oq_sb
        xtm = [es.enter_context(nc.sbuf_tensor(f"xtm{i}", [128, DM], BF16)) for i in range(2)]
        wsl = [es.enter_context(nc.sbuf_tensor(f"wsl{i}", [128, CC * 128], BF16)) for i in range(2)]
        wosl = [es.enter_context(nc.sbuf_tensor(f"wosl{i}", [128, 512], BF16)) for i in range(2)]
        wckb = es.enter_context(nc.sbuf_tensor("wckb", [128, 32 * 64], BF16))
        wcvb = es.enter_context(nc.sbuf_tensor("wcvb", [128, 32 * 64], BF16))
        bck_sb = es.enter_context(nc.sbuf_tensor("bck_sb", [64, 1], F32))
        bcv_sb = es.enter_context(nc.sbuf_tensor("bcv_sb", [64, 1], F32))
        wg_sb = es.enter_context(nc.sbuf_tensor("wg_sb", [128, 3], BF16))
        bg_sb = es.enter_context(nc.sbuf_tensor("bg_sb", [1, 3], F32))
        vwork = es.enter_context(nc.sbuf_tensor("vwork", [128, TGS + 16], BF16))
        vtail = es.enter_context(nc.sbuf_tensor("vtail", [128, HP * 16], BF16))
        ck_sb = es.enter_context(nc.sbuf_tensor("ck_sb", [128, HP * NB], BF16))
        cv_sb = es.enter_context(nc.sbuf_tensor("cv_sb", [128, HP * NB], BF16))
        pc_sb = None
        crow = None
        cvbm = es.enter_context(nc.sbuf_tensor("cvbm", [128, NH * 128], BF16))
        idb = es.enter_context(nc.sbuf_tensor("idb", [128, 128], BF16))
        ones_c = es.enter_context(nc.sbuf_tensor("ones_c", [128, 1], BF16))
        ones_rb = es.enter_context(nc.sbuf_tensor("ones_rb", [1, 128], BF16))
        imp_sb = es.enter_context(nc.sbuf_tensor("imp_sb", [128, 2], F32))
        impb = es.enter_context(nc.sbuf_tensor("impb", [128, 2], BF16))
        impc = es.enter_context(nc.sbuf_tensor("impc", [128, 2], F32))
        impc2 = es.enter_context(nc.sbuf_tensor("impc2", [128, 2], F32))
        implo = es.enter_context(nc.sbuf_tensor("implo", [128, 2], BF16))
        ind_sb = es.enter_context(nc.sbuf_tensor("ind_sb", [128, 2], BF16))
        rank_sb = es.enter_context(nc.sbuf_tensor("rank_sb", [128, 2], F32))
        qm_sb = es.enter_context(nc.sbuf_tensor("qm_sb", [128, HP], F32))
        qmb = es.enter_context(nc.sbuf_tensor("qmb", [128, HP], BF16))
        qmt = es.enter_context(nc.sbuf_tensor("qmt", [128, 1], F32))
        gates = es.enter_context(nc.sbuf_tensor("gates", [1, 64], BF16))
        gcol = es.enter_context(nc.sbuf_tensor("gcol", [128, 64], F32))
        dst = es.enter_context(nc.sbuf_tensor("dst", [128, 16], F32))
        brwb = es.enter_context(nc.sbuf_tensor("brwb", [1, TGS], BF16))
        imp_row = brwb
        stat1 = es.enter_context(nc.sbuf_tensor("stat1", [1, 8], F32))
        xres = es.enter_context(nc.sbuf_tensor("xres", [128, DM], F32))
        ln = LnExit(nc, es, "L", nbuf=1)
        sq_s = ln.sq_sb
        pc_sb = cv_sb
        crow = cv_sb.ap()[:, 256:512]
        psS = [es.enter_context(nc.psum_tensor(f"psS{i}", [128, 512], F32)) for i in range(2)]
        psG = [es.enter_context(nc.psum_tensor(f"psG{i}", [128, 512], F32)) for i in range(2)]
        psOW = [es.enter_context(nc.psum_tensor(f"psOW{i}", [128, 512], F32)) for i in range(2)]
        ps_tr = [es.enter_context(nc.psum_tensor(f"pstr{i}", [128, 256], BF16)) for i in range(2)]

        gp = bld.pool("gp", 6)
        wp = bld.pool("wp", 6)
        yp = bld.pool("yp", 6)
        tok_const = []
        tok_xt = {}
        tok_w = {}
        tok_woall = {}
        tok_xres = {}
        tok_store = {}

        def blocks_of_tg(tg):
            n0 = 0 if tg == 0 else 32 * tg - 1
            n1 = min(32 * tg + 30, NB - 1)
            return n0, n1 - n0 + 1

        xord = []
        for tg in range(NTG):
            for sub in range(NSUB):
                xord.append(("a0", tg, sub))
        for ph in ("a1", "b"):
            for qg in range(NTG):
                for sub in range(NSUB):
                    xord.append((ph, qg, sub))

        def t_body(t):
            t.wait(*tok_const)
            cnt = {"S": 0, "G": 0, "O": 0, "T": 0}

            def sgrp(emit, mark=None):
                i = cnt["S"]; cnt["S"] += 1
                if i >= 2:
                    t.wait(t.tok(("Sev", i - 2)))
                emit(psS[i % 2], i)
                bld.marks[("Sgrp", i)] = (bld.tickers["t"], bld.tickers["t"].n)
                if mark:
                    bld.marks[mark] = bld.marks[("Sgrp", i)]
                return i

            def ggrp(emit, mark=None, wait_list=()):
                i = cnt["G"]; cnt["G"] += 1
                if i >= 2:
                    t.wait(t.tok(("Gev", i - 2)))
                for wtk in wait_list:
                    t.wait(wtk)
                emit(psG[i % 2], i)
                bld.marks[("Ggrp", i)] = (bld.tickers["t"], bld.tickers["t"].n)
                if mark:
                    bld.marks[mark] = bld.marks[("Ggrp", i)]
                return i

            def transp(src_ap, width, mark=None, wait_list=(), rows=128, base=0):
                j = cnt["T"]; cnt["T"] += 1
                if j >= 2:
                    t.wait(t.tok(("Tev", j - 2)))
                for wtk in wait_list:
                    t.wait(wtk)
                t.op("transpose", ps_tr[j % 2][0:width, 0:rows],
                     src_ap, idb[base:base + rows, base:base + rows],
                     tick=True, mark=("Ttr", j))
                if mark:
                    bld.marks[mark] = bld.marks[("Ttr", j)]
                return j

            def xgroup(ph, gg, nxc):
                for sub in range(NSUB):
                    t.wait(tok_xt.get((ph, gg, sub)))
                    for c in range(CC):
                        transp(xtm[nxc[0] % 2][:, c * 128:(c + 1) * 128], 128,
                               ("xtr", ph, gg, sub, c))
                    nxc[0] += 1

            def proj_group(widx, mark):
                def em(ps, i):
                    for c in range(CC):
                        t.op("matmul", ps[:],
                             wsl[widx % 2][:, c * 128:(c + 1) * 128],
                             xg_sb[:, c * TGS:(c + 1) * TGS],
                             start=(c == 0), stop=(c == CC - 1),
                             tick=(c == CC - 1))
                sgrp(em, mark)

            nxc = [0]
            for tg in range(NTG):
                xgroup("a0", tg, nxc)
                t.wait(t.tok(("xev", "a0", tg)))
                for dc in range(CC):
                    t.wait(tok_w.get(("k", tg, dc)))
                    proj_group(dc, ("kg", tg, dc))
                for hp in range(HP):
                    t.wait(tok_w.get(("v", tg, hp)))
                    proj_group(hp, ("vg", tg, hp))
                    for sub in range(NSUB):
                        transp(vwork[:, 16 + sub * 128:16 + sub * 128 + 128], 128,
                               ("vtr", tg, hp, sub),
                               wait_list=([t.tok(("vev", tg, hp))] if sub == 0 else ()))
                    n0, nn = blocks_of_tg(tg)
                    off = 16 * n0 - 512 * tg + 16
                    for hh in range(2):
                        def em(ps, i, hh=hh, nn=nn, off=off):
                            for tt in range(32):
                                t.op("matmul", ps[0:64, 0:nn],
                                     wcvb[hh * 64:hh * 64 + 64,
                                          tt * 64:(tt + 1) * 64],
                                     vwork[hh * 64:hh * 64 + 64,
                                           off + tt: off + tt + 16 * (nn - 1) + 1:16],
                                     start=(tt == 0), stop=(tt == 31),
                                     tick=(tt == 31))
                        ggrp(em, ("cvg", tg, hp, hh),
                             wait_list=[t.tok(("vwork", tg, hp))] if hh == 0 else ())
            for h in range(NH):
                hb = (h % 2) * 64
                hc = h // 2

                def em(ps, i, hb=hb, hc=hc):
                    for tt in range(32):
                        t.op("matmul", ps[0:64, 0:NB],
                             wckb[hb:hb + 64, tt * 64:(tt + 1) * 64],
                             kfm[hb:hb + 64, hc * S + tt: hc * S + tt + 16 * (NB - 1) + 1:16],
                             start=(tt == 0), stop=(tt == 31),
                             tick=(tt == 31))
                ggrp(em, ("ckg", h),
                     wait_list=[t.tok(("kg", NTG - 1, CC - 1))] if h == 0 else ())
                for ch in range(2):
                    wdt = 128 if ch == 0 else 127
                    transp(cv_sb[hb:hb + 64, hc * NB + ch * 128: hc * NB + ch * 128 + wdt],
                           wdt, ("cvtr", h, ch), rows=64, base=hb,
                           wait_list=[t.tok(("cvev", NTG - 1, hc, h % 2))] if ch == 0 else ())
            for qg in range(NTG):
                xgroup("a1", qg, nxc)
                t.wait(t.tok(("xev", "a1", qg)))
                for dc in range(CC):
                    t.wait(tok_w.get(("qa", qg, dc)))
                    proj_group(dc, ("qga", qg, dc))
                for h in range(NH):
                    hb = (h % 2) * 64
                    hc = h // 2
                    for qt in range(NSUB):
                        def em(ps, i, hb=hb, hc=hc, qt=qt):
                            t.op("matmul", ps[:, 0:NB],
                                 qg_sb[hb:hb + 64, hc * TGS + qt * 128: hc * TGS + qt * 128 + 128],
                                 ck_sb[hb:hb + 64, hc * NB:(hc + 1) * NB],
                                 start=True, stop=True, tick=True)
                        wl = []
                        if h == 0 and qt == 0:
                            wl.append(t.tok(("qga_ev", qg, CC - 1)))
                        if qg == 0 and qt == 0:
                            wl.append(t.tok(("ckv", h)))
                        ggrp(em, ("csca", qg, h, qt), wait_list=wl)

                        def em(ps, i):
                            t.op("matmul", ps[0:128, 0:1], pc_sb[:, 0:128],
                                 ones_c[:], start=True, stop=True, tick=False)
                            t.op("matmul", ps[0:127, 1:2], pc_sb[:, 128:255],
                                 ones_c[:], start=True, stop=True, tick=True)
                        ggrp(em, ("impg", qg, h, qt),
                             wait_list=[t.tok(("pnca", qg, h, qt))])
            for h in range(NH):
                hb = (h % 2) * 64

                def em(ps, i, h=h, hb=hb):
                    t.op("matmul", ps[0:1, 0:3],
                         qmb[hb:hb + 64, (h // 2):(h // 2) + 1],
                         wg_sb[hb:hb + 64, :], start=True, stop=True, tick=True)
                ggrp(em, ("gmm", h),
                     wait_list=[t.tok(("qmrdy",))] if h == 0 else ())

            def em(ps, i):
                t.op("matmul", ps[0:128, 0:48], ones_rb[:, :], gates[0:1, 0:48],
                     start=True, stop=True, tick=True)
            ggrp(em, ("gbc",), wait_list=[t.tok(("grow",))])

            transp(impb[:, 0:1], 1, wait_list=[t.tok(("impbv",))])
            transp(impb[0:127, 1:2], 1, rows=127)
            transp(implo[:, 0:1], 1)
            transp(implo[0:127, 1:2], 1, rows=127)

            def em(ps, i):
                t.op("matmul", ps[0:128, 0:NB], ones_rb[:, :], imp_row[0:1, 0:NB],
                     start=True, stop=True, tick=True)
            ggrp(em, ("impbc",), wait_list=[t.tok(("improw",))])

            def em(ps, i):
                t.op("matmul", ps[0:128, 0:NB], ones_rb[:, :],
                     imp_row[0:1, 256:256 + NB],
                     start=True, stop=True, tick=True)
            ggrp(em, ("impbc2",))
            def em(ps, i):
                for kg in range(CC):
                    for ch in range(2):
                        t.wait(tok_w.get(("amat", ch, kg)))
                        if kg == 0 and ch == 0:
                            t.wait(t.tok(("indrdy",)))
                        for k4 in range(4):
                            kc = kg * 4 + k4
                            fo = (kg == 0 and ch == 0 and k4 == 0)
                            t.op("matmul", ps[0:128, kc:kc + 1],
                                 wsl[(kg * 2 + ch) % 2][:, k4 * 128:(k4 + 1) * 128],
                                 ind_sb[:, ch:ch + 1],
                                 start=fo, stop=fo, skip_group_check=not fo,
                                 tick=(k4 == 3),
                                 mark=(("mg", kg, ch) if k4 == 3 else None))
            ggrp(em, ("mcolg",))
            for qg in range(NTG):
                xgroup("b", qg, nxc)
                t.wait(t.tok(("xev", "b", qg)))
                for dc in range(CC):
                    t.wait(tok_w.get(("qb", qg, dc)))
                    proj_group(dc, ("qgb", qg, dc))
                t.wait(t.tok(("qgb_ev", qg, CC - 1)))
                if qg == 0:
                    t.wait(t.tok(("vmsc",)))
                for h in range(NH):
                    hb = (h % 2) * 64
                    hc = h // 2
                    giD = cnt["G"]; cnt["G"] += 1
                    if giD >= 2:
                        t.wait(t.tok(("Gev", giD - 2)))
                    for kc in range(NKC):
                        def em(ps, i, hb=hb, hc=hc, kc=kc):
                            t.op("matmul", ps[0:128, 0:TGS],
                                 kfm[hb:hb + 64, hc * S + kc * 128: hc * S + (kc + 1) * 128],
                                 qg_sb[hb:hb + 64, hc * TGS:(hc + 1) * TGS],
                                 start=True, stop=True, tick=True)
                        sgrp(em, ("ssc", qg, h, kc))
                        t.wait(t.tok(("pex", qg, h, kc)))
                        if kc == 0:
                            un = qg * NH + h
                            if un >= 2:
                                pqg, ph = divmod(un - 2, NH)
                                t.wait(t.tok(("ocmbA", pqg, ph)))
                        if kc < 28:
                            for qt in range(NSUB):
                                fo = (kc == 0 and qt == 0)
                                last = (qt == NSUB - 1)
                                t.op("matmul",
                                     psOW[h % 2][:, qt * HSTR:(qt + 1) * HSTR],
                                     prT[kc % 2][:, qt * 128:(qt + 1) * 128],
                                     vtm[:, kc * CSTR + h * HSTR: kc * CSTR + (h + 1) * HSTR],
                                     start=fo, stop=fo, skip_group_check=not fo,
                                     tick=last,
                                     mark=(("pvk", qg, h, kc) if last else None))
                        else:
                            for qt in range(NSUB):
                                fo = (kc == 28 and qt == 0)
                                t.op("matmul",
                                     psG[giD % 2][:, qt * HSTR:(qt + 1) * HSTR],
                                     prT[kc % 2][:, qt * 128:(qt + 1) * 128],
                                     vtm[:, kc * CSTR + h * HSTR: kc * CSTR + (h + 1) * HSTR],
                                     start=fo, stop=fo, skip_group_check=not fo,
                                     tick=False)
                            t.wait(t.tok(("pm2", qg, h, kc)))
                            for qt in range(NSUB):
                                last = (qt == NSUB - 1)
                                t.op("matmul",
                                     psOW[h % 2][:, qt * HSTR:(qt + 1) * HSTR],
                                     prT2[:, qt * 128:(qt + 1) * 128],
                                     vtm[:, kc * CSTR + h * HSTR: kc * CSTR + (h + 1) * HSTR],
                                     start=False, stop=False,
                                     skip_group_check=True,
                                     tick=last,
                                     mark=(("pvk", qg, h, kc) if last else None))
                    bld.marks[("Ggrp", giD)] = bld.marks[("pvk", qg, h, NKC - 1)]
                    bld.marks[("dgrp", qg, h)] = bld.marks[("Ggrp", giD)]
                    for qt in range(NSUB):
                        def em(ps, i, hb=hb, hc=hc, qt=qt):
                            t.op("matmul", ps[:, 0:NB],
                                 qg_sb[hb:hb + 64, hc * TGS + qt * 128: hc * TGS + qt * 128 + 128],
                                 ck_sb[hb:hb + 64, hc * NB:(hc + 1) * NB],
                                 start=True, stop=True, tick=True)
                        ggrp(em, ("cscb", qg, qt, h))
                        tjs = []
                        for ch in range(2):
                            wdt = 128 if ch == 0 else 127
                            wl = [t.tok(("pncb", qg, qt, h))] if ch == 0 else []
                            tjs.append(transp(pc_sb[:, ch * 128: ch * 128 + wdt],
                                              wdt, rows=128, wait_list=wl,
                                              mark=(("pctr", qg, qt, h)
                                                    if ch == 1 else None)))

                        def em(ps, i, h=h, qt=qt, tjs=tjs):
                            for ch in range(2):
                                wdt = 128 if ch == 0 else 127
                                j = tjs[ch]
                                t.wait(t.tok(("Tevd", j)))
                                t.op("matmul", ps[:, 0:64],
                                     ptr_sb[j % 2][0:wdt, :],
                                     cvbm[0:wdt, h * 128 + ch * 64: h * 128 + ch * 64 + 64],
                                     start=(ch == 0), stop=(ch == 1),
                                     tick=True, mark=("pvmm", j))
                        ggrp(em, ("cpv", qg, qt, h))
                for qt in range(NSUB):
                    qb0 = qt * NH * 64
                    for c in range(CC):
                        transp(oq_sb[:, qb0 + c * 128: qb0 + (c + 1) * 128], 128,
                               ("otr", qg, qt, c),
                               wait_list=([t.tok(("oqv", qg, qt))] if c == 0 else ()))
                    for nh in range(2):
                        def em(ps, i, qt=qt, nh=nh, qg=qg, qb0=qb0):
                            for c in range(CC):
                                wli = ((qg * NSUB + qt) * 2 + nh) * CC + c
                                t.wait(tok_w.get(("wo", qg, qt, nh, c)))
                                if c == 0:
                                    t.wait(t.tok(("ofmv", qg, qt)))
                                t.op("matmul", ps[:],
                                     ofm[:, qb0 + c * 128: qb0 + (c + 1) * 128],
                                     wosl[wli % 2][:],
                                     start=(c == 0), stop=(c == CC - 1),
                                     tick=True, mark=("womm", wli))
                        si = sgrp(em, ("wog", qg, qt, nh))
                        bld.marks[("wogS", qg, qt, nh)] = ("idx", si)
        def s_body(s):
            cS = [0]; cG = [0]

            def snS():
                i = cS[0]; cS[0] += 1; return i

            def snG():
                i = cG[0]; cG[0] += 1; return i

            for tg in range(NTG):
                for dc in range(CC):
                    si = snS()
                    s.wait(s.tok(("Sgrp", si)))
                    s.copy(kfm[:, dc * S + tg * TGS: dc * S + (tg + 1) * TGS],
                           psS[si % 2][:], mark=("Sev", si))
                for hp in range(HP):
                    si = snS()
                    s.wait(s.tok(("Sgrp", si)))
                    if tg > 0 or hp > 0:
                        ptg, php = (tg, hp - 1) if hp > 0 else (tg - 1, HP - 1)
                        s.wait(s.tok(("vwork", ptg, php)))
                        s.wait(s.tok(("vtr", ptg, php, NSUB - 1)))
                        s.wait(s.tok(("cvg", ptg, php, 1)))
                    s.copy(vwork[:, 16:16 + TGS], psS[si % 2][:], mark=("Sev", si))
                    bld.marks[("vev", tg, hp)] = bld.marks[("Sev", si)]
                    n0, nn = blocks_of_tg(tg)
                    h2 = hp * 2
                    for hh in range(2):
                        gi = snG()
                        s.wait(s.tok(("Ggrp", gi)))
                        h = h2 + hh
                        s.add(cv_sb[(h % 2) * 64:(h % 2) * 64 + 64,
                                    (h // 2) * NB + n0:(h // 2) * NB + n0 + nn],
                              psG[gi % 2][0:64, 0:nn],
                              bcv_sb[:, 0:1], mark=("Gev", gi))
                        bld.marks[("cvev", tg, h // 2, h % 2)] = bld.marks[("Gev", gi)]
            for h in range(NH):
                hb = (h % 2) * 64
                hc = h // 2
                gi = snG()
                s.wait(s.tok(("Ggrp", gi)))
                s.add(ck_sb[hb:hb + 64, hc * NB: hc * NB + NB],
                      psG[gi % 2][0:64, 0:NB],
                      bck_sb[:, 0:1], mark=("Gev", gi))
                bld.marks[("ckv", h)] = bld.marks[("Gev", gi)]
            for qg in range(NTG):
                for dc in range(CC):
                    si = snS()
                    s.wait(s.tok(("Sgrp", si)))
                    s.copy(qg_sb[:, dc * TGS:(dc + 1) * TGS], psS[si % 2][:],
                           mark=("Sev", si))
                    bld.marks[("qga_ev", qg, dc)] = bld.marks[("Sev", si)]
                for h in range(NH):
                    for qt in range(NSUB):
                        gi = snG()
                        s.wait(s.tok(("Ggrp", gi)))
                        s.wait(s.tok(("dmax", 0, qg, h, qt)))
                        if (h, qt) != (0, 0) or qg > 0:
                            ph, pqt = (h, qt - 1) if qt > 0 else ((h - 1, NSUB - 1) if h > 0 else (NH - 1, NSUB - 1))
                            pqg = qg if (h, qt) != (0, 0) else qg - 1
                            s.wait(s.tok(("impg", pqg, ph, pqt)))
                        s.activation(pc_sb[:, 0:NB], psG[gi % 2][:, 0:NB], AF.Exp,
                                     bias=dst[:, 0:1], scale=SC,
                                     accum_out=dst[:, 1:2], mark=("cexa", qg, h, qt))
                        bld.marks[("Gev", gi)] = bld.marks[("cexa", qg, h, qt)]
                        gi = snG()
            for h in range(NH):
                gi = snG()
            gi = snG()
            s.wait(s.tok(("graw",)))
            s.activation(gates[0:1, 0:48], gates[0:1, 0:48], AF.Sigmoid,
                         mark=("grow",))
            gi = snG()
            gi = snG()
            gi = snG()
            for qg in range(NTG):
                for dc in range(CC):
                    si = snS()
                    s.wait(s.tok(("Sgrp", si)))
                    s.copy(qg_sb[:, dc * TGS:(dc + 1) * TGS], psS[si % 2][:],
                           mark=("Sev", si))
                    bld.marks[("qgb_ev", qg, dc)] = bld.marks[("Sev", si)]
                for h in range(NH):
                    for kc in range(NKC):
                        si = snS()
                        s.wait(s.tok(("Sgrp", si)))
                        if kc >= 2:
                            s.wait(s.tok(("pvk", qg, h, kc - 2)))
                        elif (qg, h) != (0, 0) and kc == 0:
                            pqg, ph = (qg, h - 1) if h > 0 else (qg - 1, NH - 1)
                            s.wait(s.tok(("pvk", pqg, ph, NKC - 2 + kc)))
                        elif (qg, h) != (0, 0):
                            pqg, ph = (qg, h - 1) if h > 0 else (qg - 1, NH - 1)
                            s.wait(s.tok(("pvk", pqg, ph, NKC - 1)))
                        s.activation(prT[kc % 2][:], psS[si % 2][0:128, 0:TGS],
                                     AF.Exp, scale=SC, mark=("pex", qg, h, kc))
                        bld.marks[("Sev", si)] = bld.marks[("pex", qg, h, kc)]
                    gi = snG()
                    for qt in range(NSUB):
                        gi = snG()
                        s.wait(s.tok(("Ggrp", gi)))
                        s.wait(s.tok(("dmax", 1, qg, h, qt)))
                        prev = ((qg, qt - 1, h) if qt > 0 else
                                ((qg, NSUB - 1, h - 1) if h > 0 else
                                 ((qg - 1, NSUB - 1, NH - 1) if qg > 0 else None)))
                        if prev is not None:
                            s.wait(s.tok(("pctr", *prev)))
                        s.activation(pc_sb[:, 0:NB], psG[gi % 2][:, 0:NB], AF.Exp,
                                     bias=dst[:, 0:1], scale=SC,
                                     accum_out=dst[:, 1:2], mark=("cexb", qg, qt, h))
                        bld.marks[("Gev", gi)] = bld.marks[("cexb", qg, qt, h)]
                        bld.marks[("pncb", qg, qt, h)] = bld.marks[("cexb", qg, qt, h)]
                        gi = snG()
                for qt in range(NSUB):
                    for nh in range(2):
                        si = snS()
                    ln.s_stats(s, qg * NSUB + qt)

        def v_body(v):
            cT = [0]; cG = [0]; cO = [0]
            v.memset(stat1[:], 0.0)
            v.memset(imp_sb[:], 0.0)
            v.memset(rank_sb[:], 0.0)
            v.memset(vtm.ap()[:, 0:NKC * CSTR], 0.0)

            def tev(dest_ap, mark=None, width=128, rows=128):
                j = cT[0]; cT[0] += 1
                v.wait(v.tok(("Ttr", j)))
                v.tensor_copy(dest_ap, ps_tr[j % 2][0:width, 0:rows],
                              mark=("Tev", j))
                if mark:
                    bld.marks[mark] = bld.marks[("Tev", j)]
                return j

            def tevd(width=128):
                j = cT[0]; cT[0] += 1
                v.wait(v.tok(("Ttr", j)))
                if j >= 2:
                    v.wait(v.tok_opt(("pvmm", j - 2)))
                v.tensor_copy(ptr_sb[j % 2][0:width, :],
                              ps_tr[j % 2][0:width, 0:128], mark=("Tevd", j))
                bld.marks[("Tev", j)] = bld.marks[("Tevd", j)]
                return j

            def xevac(ph, gg):
                for sub in range(NSUB):
                    for c in range(CC):
                        tev(xg_sb[:, c * TGS + sub * 128: c * TGS + sub * 128 + 128])
                v.op("tensor_copy", stat1[0:1, 7:8], stat1[0:1, 7:8], tick=True,
                     mark=("xev", ph, gg))

            def snG():
                i = cG[0]; cG[0] += 1; return i

            for tg in range(NTG):
                xevac("a0", tg)
                for hp in range(HP):
                    v.wait(v.tok(("vev", tg, hp)))
                    if tg > 0:
                        v.tensor_copy(vwork[:, 0:16], vtail[:, hp * 16:(hp + 1) * 16])
                    v.tensor_copy(vtail[:, hp * 16:(hp + 1) * 16],
                                  vwork[:, TGS:TGS + 16], mark=("vwork", tg, hp))
                    for sub in range(NSUB):
                        kc = tg * NSUB + sub
                        j = cT[0]; cT[0] += 1
                        v.wait(v.tok(("Ttr", j)))
                        v.tensor_copy(
                            vtm.ap()[:, kc * CSTR + (2 * hp) * HSTR:
                                     kc * CSTR + (2 * hp) * HSTR + 64],
                            ps_tr[j % 2][0:128, 0:64])
                        v.tensor_copy(
                            vtm.ap()[:, kc * CSTR + (2 * hp + 1) * HSTR:
                                     kc * CSTR + (2 * hp + 1) * HSTR + 64],
                            ps_tr[j % 2][0:128, 64:128], mark=("Tev", j))
                        bld.marks[("vtr", tg, hp, sub)] = bld.marks[("Tev", j)]
                    for hh in range(2):
                        snG()
            for h in range(NH):
                snG()
                for ch in range(2):
                    wdt = 128 if ch == 0 else 127
                    tev(cvbm[0:wdt, h * 128 + ch * 64: h * 128 + ch * 64 + 64],
                        mark=("cvbm", h, ch), width=wdt, rows=64)
            for qg in range(NTG):
                xevac("a1", qg)
                for h in range(NH):
                    hb = (h % 2) * 64
                    hc = h // 2
                    for qt in range(NSUB):
                        gi = snG()
                        v.wait(v.tok(("Ggrp", gi)))
                        v.reduce_max(dst[:, 0:1], psG[gi % 2][:, 0:NB], axis=AX.X,
                                     negate=True, apply_transpose=False)
                        v.tensor_scalar_mul(dst[:, 0:1], dst[:, 0:1], SC,
                                            mark=("dmax", 0, qg, h, qt))
                        v.wait(v.tok(("cexa", qg, h, qt)))
                        v.reciprocal(dst[:, 2:3], dst[:, 1:2])
                        v.tensor_scalar(pc_sb[:, 0:NB], pc_sb[:, 0:NB],
                                        dst[:, 2:3], None, ALU.mult,
                                        mark=("pnca", qg, h, qt))
                        gi = snG()
                        v.wait(v.tok(("Ggrp", gi)))
                        if qg == 0 and h == 0 and qt == 0:
                            v.tensor_copy(imp_sb[:, 0:1], psG[gi % 2][:, 0:1])
                            v.tensor_copy(imp_sb[0:127, 1:2],
                                          psG[gi % 2][0:127, 1:2],
                                          mark=("Gev", gi))
                        else:
                            v.tensor_add(imp_sb[:, 0:1], imp_sb[:, 0:1],
                                         psG[gi % 2][:, 0:1])
                            v.tensor_add(imp_sb[0:127, 1:2], imp_sb[0:127, 1:2],
                                         psG[gi % 2][0:127, 1:2],
                                         mark=("Gev", gi))
                    v.reduce_sum(qmt[hb:hb + 64, 0:1],
                                 qg_sb[hb:hb + 64, hc * TGS:(hc + 1) * TGS], axis=AX.X)
                    if qg == 0:
                        v.tensor_copy(qm_sb[hb:hb + 64, hc:hc + 1], qmt[hb:hb + 64, 0:1])
                    else:
                        v.tensor_add(qm_sb[hb:hb + 64, hc:hc + 1],
                                     qm_sb[hb:hb + 64, hc:hc + 1], qmt[hb:hb + 64, 0:1])
            v.tensor_scalar_mul(qmb[:], qm_sb[:], 1.0 / S, mark=("qmrdy",))
            for h in range(NH):
                gi = snG()
                v.wait(v.tok(("Ggrp", gi)))
                v.tensor_add(gates[0:1, h * 3:h * 3 + 3], psG[gi % 2][0:1, 0:3],
                             bg_sb[0:1, :], mark=("Gev", gi))
                if h == NH - 1:
                    bld.marks[("graw",)] = bld.marks[("Gev", gi)]
            gi = snG()
            v.wait(v.tok(("Ggrp", gi)))
            v.tensor_copy(gcol[:, 0:48], psG[gi % 2][0:128, 0:48], mark=("Gev", gi))
            v.tensor_copy(impb[:], imp_sb[:])
            v.tensor_copy(impc[:], impb[:])
            v.tensor_tensor(implo[:], imp_sb[:], impb[:], ALU.subtract)
            v.tensor_copy(impc2[:], implo[:], mark=("impbv",))
            tev(imp_row[0:1, 0:128], width=1, rows=128)
            tev(imp_row[0:1, 128:255], width=1, rows=127)
            tev(imp_row[0:1, 256:384], width=1, rows=128)
            tev(imp_row[0:1, 384:511], width=1, rows=127, mark=("improw",))
            gi = snG()
            gi2 = snG()
            v.wait(v.tok(("Ggrp", gi2)))
            pcx = cv_sb
            for ch in range(2):
                wdt = 128 if ch == 0 else 127
                v.tensor_scalar(crow[0:wdt, 0:NB], psG[gi % 2][0:wdt, 0:NB],
                                impc[0:wdt, ch:ch + 1], None, ALU.is_gt)
                v.reduce_sum(rank_sb[0:wdt, ch:ch + 1], crow[0:wdt, 0:NB], axis=AX.X)
                v.tensor_scalar(crow[0:wdt, 0:NB], psG[gi % 2][0:wdt, 0:NB],
                                impc[0:wdt, ch:ch + 1], None, ALU.is_equal)
                v.tensor_scalar(pcx[0:wdt, 0:NB], psG[gi2 % 2][0:wdt, 0:NB],
                                impc2[0:wdt, ch:ch + 1], None, ALU.is_gt)
                v.tensor_tensor(crow[0:wdt, 0:NB], crow[0:wdt, 0:NB],
                                pcx[0:wdt, 0:NB], ALU.mult)
                v.reduce_sum(dst[0:wdt, 3:4], crow[0:wdt, 0:NB], axis=AX.X)
                v.tensor_tensor(rank_sb[0:wdt, ch:ch + 1],
                                rank_sb[0:wdt, ch:ch + 1], dst[0:wdt, 3:4],
                                ALU.add)
            v.tensor_scalar(ind_sb[:], rank_sb[:], 15.5, None, ALU.is_lt,
                            mark=("indrdy",))
            bld.marks[("Gev", gi)] = bld.marks[("indrdy",)]
            bld.marks[("Gev", gi2)] = bld.marks[("indrdy",)]
            gi = snG()
            v.wait(v.tok(("Ggrp", gi)))
            v.tensor_copy(mcol_sb[:], psG[gi % 2][0:128, 0:NKC], mark=("Gev", gi))
            v.memset(ones16[:], 1.0)
            for kc in range(NKC):
                if kc < 28:
                    v.tensor_scalar(vtm.ap()[:, kc * CSTR:(kc + 1) * CSTR],
                                    vtm.ap()[:, kc * CSTR:(kc + 1) * CSTR],
                                    mcol_sb[:, kc:kc + 1], None, ALU.mult)
                    v.tensor_scalar(vtm.ap()[:, kc * CSTR + 64: kc * CSTR + 64 + (NH - 1) * HSTR + 1: HSTR],
                                    ones16[:, 0:NH], mcol_sb[:, kc:kc + 1], None,
                                    ALU.mult)
                else:
                    v.tensor_copy(vtm.ap()[:, kc * CSTR + 64: kc * CSTR + 64 + (NH - 1) * HSTR + 1: HSTR],
                                  ones16[:, 0:NH])
                v.memset(vtm.ap()[:, kc * CSTR + 65: kc * CSTR + 65 + (NH - 1) * HSTR + 1: HSTR], 0.0)
            v.op("tensor_copy", stat1[0:1, 7:8], stat1[0:1, 7:8], tick=True,
                 mark=("vmsc",))
            for qg in range(NTG):
                xevac("b", qg)
                for h in range(NH):
                    for kw in range(4):
                        kc = 28 + kw
                        v.wait(v.tok(("pex", qg, h, kc)))
                        if kw == 0 and (qg, h) != (0, 0):
                            pqg, ph = (qg, h - 1) if h > 0 else (qg - 1, NH - 1)
                            v.wait(v.tok(("pvk", pqg, ph, NKC - 1)))
                        else:
                            v.wait(v.tok_opt(("pvk", qg, h, kc - 1)))
                        v.tensor_scalar(prT2[:], prT[kc % 2][:],
                                        mcol_sb[:, kc:kc + 1], None, ALU.mult,
                                        mark=("pm2", qg, h, kc))
                    giW = snG()
                    v.wait(v.tok(("Ggrp", giW)))
                    for qt in range(NSUB):
                        ob = qt * NH * 64 + h * 64
                        v.reciprocal(dst[:, 6:7],
                                     psOW[h % 2][:, qt * HSTR + 64: qt * HSTR + 65])
                        v.tensor_scalar(dst[:, 6:7], dst[:, 6:7],
                                        gcol[:, h * 3 + 1: h * 3 + 2], None, ALU.mult)
                        v.reciprocal(dst[:, 7:8],
                                     psG[giW % 2][:, qt * HSTR + 64: qt * HSTR + 65])
                        v.tensor_scalar(dst[:, 7:8], dst[:, 7:8],
                                        gcol[:, h * 3 + 2: h * 3 + 3], None, ALU.mult)
                        v.tensor_scalar(oq_sb[:, ob:ob + 64],
                                        psOW[h % 2][:, qt * HSTR: qt * HSTR + 64],
                                        dst[:, 6:7], None, ALU.mult)
                        v.scalar_tensor_tensor(oq_sb[:, ob:ob + 64],
                                               psG[giW % 2][:, qt * HSTR: qt * HSTR + 64],
                                               dst[:, 7:8],
                                               oq_sb[:, ob:ob + 64],
                                               ALU.mult, ALU.add,
                                               mark=(("ocmbA", qg, h)
                                                     if qt == NSUB - 1 else None))
                    bld.marks[("Gev", giW)] = bld.marks[("ocmbA", qg, h)]
                    for qt in range(NSUB):
                        gi = snG()
                        v.wait(v.tok(("Ggrp", gi)))
                        v.reduce_max(dst[:, 0:1], psG[gi % 2][:, 0:NB], axis=AX.X,
                                     negate=True)
                        v.tensor_scalar_mul(dst[:, 0:1], dst[:, 0:1], SC,
                                            mark=("dmax", 1, qg, h, qt))
                        tevd(128)
                        tevd(127)
                        gi = snG()
                        v.wait(v.tok(("Ggrp", gi)))
                        v.wait(v.tok(("cexb", qg, qt, h)))
                        v.reciprocal(dst[:, 8:9], dst[:, 1:2])
                        v.tensor_scalar(dst[:, 8:9], dst[:, 8:9],
                                        gcol[:, h * 3: h * 3 + 1], None, ALU.mult)
                        ob = qt * NH * 64 + h * 64
                        v.scalar_tensor_tensor(oq_sb[:, ob:ob + 64],
                                               psG[gi % 2][:, 0:64], dst[:, 8:9],
                                               oq_sb[:, ob:ob + 64],
                                               ALU.mult, ALU.add,
                                               mark=("ocmbq", qg, h, qt))
                        bld.marks[("Gev", gi)] = bld.marks[("ocmbq", qg, h, qt)]
                        if h == NH - 1:
                            bld.marks[("oqv2", qg, qt)] = bld.marks[("ocmbq", qg, h, qt)]
                    bld.marks[("ocmb", qg, h)] = bld.marks[("ocmbq", qg, h, NSUB - 1)]
                for qt in range(NSUB):
                    bld.marks[("oqv", qg, qt)] = bld.marks[("oqv2", qg, qt)]
                    qb0 = qt * NH * 64
                    for c in range(CC):
                        tev(ofm.ap()[:, qb0 + c * 128: qb0 + (c + 1) * 128],
                            mark=("ofmv", qg, qt) if c == CC - 1 else None)
                    i = qg * NSUB + qt
                    for nh in range(2):
                        v.wait(v.tok(("wog", qg, qt, nh)))
                        v.wait(tok_xres.get(i))
                        if i >= 1 and nh == 0:
                            v.wait(v.tok(("L", "sq", i - 1)))
                            v.wait(tok_store.get(i - 1))
                        si = bld.marks[("wogS", qg, qt, nh)][1]
                        v.scalar_tensor_tensor(
                            ln.sum_sb[0][:, nh * 512:(nh + 1) * 512],
                            psS[si % 2][:], 1.0,
                            xres[:, nh * 512:(nh + 1) * 512],
                            ALU.mult, ALU.add, mark=("Sev", si))
                    ln.v_stats(v, i)

        def g_body(g):
            tok_const.append(gp.load(g.eng, idb[:], ident[:, :]))
            tok_const.append(gp.load(g.eng, ones_c[:], onescol[:, :]))
            tok_const.append(gp.load(g.eng, ones_rb[:], onesrow[:, :]))
            tok_const.append(gp.load(g.eng, wckb[:], wck[:, :]))
            tok_const.append(gp.load(g.eng, wcvb[:], wcv[:, :]))
            tok_const.append(gp.load(g.eng, wg_sb[:], wgm[:, :]))
            nx = [0]

            def xload(ph, gg, sub):
                if nx[0] >= 2:
                    m = ("xtr", *xord[nx[0] - 2][0:1], xord[nx[0] - 2][1], xord[nx[0] - 2][2], CC - 1)
                    g.wait(g.tok(m))
                tok_xt[(ph, gg, sub)] = gp.load(
                    g.eng, xtm[nx[0] % 2][:],
                    x_in[(gg * NSUB + sub) * 128:(gg * NSUB + sub + 1) * 128, :])
                nx[0] += 1

            WMARK = {}
            li = [0]

            def wload(key, cons_mark, src, full=True):
                if li[0] >= 2:
                    m = WMARK.get(li[0] - 2)
                    if m:
                        g.wait(g.tok(m))
                dest = (wsl[li[0] % 2].ap().rearrange("p (c f) -> p c f", c=CC)
                        if full else wsl[li[0] % 2][:, 0:TGS])
                tok_w[key] = wp.load(g.eng, dest, src)
                WMARK[li[0]] = cons_mark
                li[0] += 1

            for tg in range(NTG):
                for sub in range(NSUB):
                    xload("a0", tg, sub)
                for dc in range(CC):
                    wload(("k", tg, dc), ("kg", tg, dc),
                          wkT[:, dc * 128:(dc + 1) * 128].rearrange("(c p) f -> p c f", p=128))
                for hp in range(HP):
                    wload(("v", tg, hp), ("vg", tg, hp),
                          wvT[:, hp * 128:(hp + 1) * 128].rearrange("(c p) f -> p c f", p=128))
            for qg in range(NTG):
                for sub in range(NSUB):
                    xload("a1", qg, sub)
                for dc in range(CC):
                    wload(("qa", qg, dc), ("qga", qg, dc),
                          wqT[:, dc * 128:(dc + 1) * 128].rearrange("(c p) f -> p c f", p=128))
            for kg in range(CC):
                for ch in range(2):
                    wload(("amat", ch, kg), ("mg", kg, ch),
                          amat[ch, :, kg * TGS:(kg + 1) * TGS], full=False)
            for qg in range(NTG):
                for sub in range(NSUB):
                    xload("b", qg, sub)
                for dc in range(CC):
                    wload(("qb", qg, dc), ("qgb", qg, dc),
                          wqT[:, dc * 128:(dc + 1) * 128].rearrange("(c p) f -> p c f", p=128))
                for qt in range(NSUB):
                    for nh in range(2):
                        for c in range(CC):
                            wli = ((qg * NSUB + qt) * 2 + nh) * CC + c
                            if wli >= 2:
                                g.wait(g.tok(("womm", wli - 2)))
                            tok_w[("wo", qg, qt, nh, c)] = wp.load(
                                g.eng, wosl[wli % 2][:],
                                wo[c * 128:(c + 1) * 128, nh * 512:(nh + 1) * 512])

        def y_body(y):
            tok_const.append(yp.load(y.eng, bck_sb[:], bck[:, :]))
            tok_const.append(yp.load(y.eng, bcv_sb[:], bcv[:, :]))
            tok_const.append(yp.load(y.eng, bg_sb[:], bgm[:, :]))
            for i in range(NT):
                if i >= 1:
                    y.wait(y.tok(("L", "sum", i - 1)))
                tok_xres[i] = yp.load(
                    y.eng, xres[:], x_in[i * 128:(i + 1) * 128, :])
                if i >= 1:
                    j = i - 1
                    tok_store[j] = ln.store(y, yp, j, x_out[j * 128:(j + 1) * 128, :])
            tok_store[NT - 1] = ln.store(y, yp, NT - 1, x_out[(NT - 1) * 128:NT * 128, :])
            if y.eng is not None:
                for tkn in yp.snapshot():
                    y.eng.wait_ge(tkn[1].sems[tkn[2]], tkn[3])

        bld.run_stage(nc, es, ges, [("t", t_body), ("s", s_body),
                                    ("v", v_body), ("g", g_body),
                                    ("y", y_body)])


def build_stage1():
    if "s1" in _cache:
        return _cache["s1"]
    nc0 = bass.Bass("TRN2", target_bir_lowering=False, debug=False)
    nc = _NcAlias(nc0)
    ges = contextlib.ExitStack()
    with ges:
        _emit_stage1(nc, ges, Build(), _s1_drams(nc))
    _cache["s1"] = nc0
    return nc0



def _wT(a):
    return np.ascontiguousarray(np.asarray(a).T)


def _fmvec(v):
    return np.ascontiguousarray(np.asarray(v).reshape(CC, 128).T)


def _ident():
    return np.eye(128, dtype=np.float32)


def stage2_inputs(x_core, W_r, W_k, W_v, mu_r, mu_k):
    return dict(x_in=x_core, wrT=_wT(W_r), wkT=_wT(W_k), wvT=_wT(W_v),
                mur=_fmvec(mu_r), muk=_fmvec(mu_k), ident=_ident())


def run_stage2(x, W_r, W_k, W_v, mu_r, mu_k):
    nc = build_stage2()
    com = stage2_inputs(None, W_r, W_k, W_v, mu_r, mu_k)
    del com["x_in"]
    in_maps = [dict(com, x_in=np.ascontiguousarray(x[c * BPC:(c + 1) * BPC]))
               for c in range(NCORES)]
    res = run_bass_kernel_spmd(nc, in_maps, core_ids=list(range(NCORES)))
    return np.concatenate([np.asarray(r["x_out"]) for r in res.results], axis=0)


def _c(a):
    return np.ascontiguousarray(np.asarray(a))


def _s1_dup(W):
    out = np.zeros((128, 32 * 64), np.float32)
    for t in range(32):
        A = np.asarray(W)[t * 64:(t + 1) * 64, :]
        out[0:64, t * 64:(t + 1) * 64] = A
        out[64:128, t * 64:(t + 1) * 64] = A
    return out


def _s1_amat():
    A = np.zeros((255, S), np.float32)
    for t in range(255):
        for r in range(64):
            A[t, min(t * 16 + r, S - 1)] += 1.0
    amat = np.zeros((2, 128, S), np.float32)
    amat[0, :, :] = A[0:128]
    amat[1, 0:127, :] = A[128:255]
    return amat


def stage1_inputs(Wq, Wk, Wv, Wo, Wck, bck, Wcv, bcv, Wg, bg):
    return dict(wqT=_c(Wq), wkT=_c(Wk), wvT=_c(Wv), wo=_c(Wo),
                wck=_s1_dup(Wck), wcv=_s1_dup(Wcv),
                bck=_c(np.asarray(bck).reshape(64, 1)),
                bcv=_c(np.asarray(bcv).reshape(64, 1)),
                wgm=_c(np.concatenate([np.asarray(Wg), np.asarray(Wg)], axis=0)),
                bgm=_c(np.asarray(bg).reshape(1, 3)),
                amat=_s1_amat(), ident=_ident(),
                onescol=np.ones((128, 1), np.float32),
                onesrow=np.ones((1, 128), np.float32))


def stage4_inputs(sW1, sb1, sW2, sb2, rW1, rb1, rW2, rb2, gW, gb):
    sW1 = np.asarray(sW1); sb1 = np.asarray(sb1)
    sW2 = np.asarray(sW2); sb2 = np.asarray(sb2)
    w1s = np.stack([sW1[0, :, :2048], sW1[0, :, 2048:]])
    wcat1 = _c(np.concatenate([w1s, np.asarray(rW1)], axis=0))
    w2s = np.stack([sW2[0, :2048, :], sW2[0, 2048:, :]])
    wcat2 = _c(np.concatenate([w2s, np.asarray(rW2)], axis=0))
    b1cat_parts = [sb1[0][:2048].reshape(16, 128).T,
                   sb1[0][2048:].reshape(16, 128).T]
    for e in range(15):
        b1cat_parts.append(np.asarray(rb1)[e].reshape(16, 128).T)
    b1cat = _c(np.concatenate(b1cat_parts, axis=1))
    sb2b = _c(np.tile(sb2[0][None, :], (128, 1)))
    gbb = _c(np.tile(np.asarray(gb)[None, :], (128, 1)))
    return dict(wcat1=wcat1, wcat2=wcat2, b1cat=b1cat, sb2b=sb2b,
                rb2=_c(rb2), gw=_c(gW), gbb=gbb, ident=_ident())


def build_mega():
    if "mega" in _cache:
        return _cache["mega"]
    nc0 = bass.Bass("TRN2", target_bir_lowering=False, debug=False)
    nc = _NcAlias(nc0)

    def T(name, shape, kind="ExternalInput"):
        return nc.dram_tensor(name, shape, F32, kind=kind).ap()

    x_in = T("x_in", [BPC, S, DM])
    x_out = T("x_out", [BPC, S, DM], "ExternalOutput")
    y1 = T("y1", [BPC, S, DM], "Internal")
    y2 = T("y2", [BPC, S, DM], "Internal")
    y3 = T("y3", [BPC, S, DM], "Internal")
    ident = T("ident", [128, 128])
    onescol = T("onescol", [128, 1])
    onesrow = T("onesrow", [1, 128])
    com = dict(ident=ident, onescol=onescol, onesrow=onesrow)
    s1w = dict(wqT=T("wqT", [DM, DM]), wkT=T("wkT", [DM, DM]),
               wvT=T("wvT", [DM, DM]), wo=T("wo", [DM, DM]),
               wck=T("wck", [128, 32 * 64]), wcv=T("wcv", [128, 32 * 64]),
               bck=T("bck", [64, 1]), bcv=T("bcv", [64, 1]),
               wgm=T("wgm", [128, 3]), bgm=T("bgm", [1, 3]),
               amat=T("amat", [2, 128, S]))
    s2w = dict(wrT=T("fwrT", [DM, DM]), wkT=T("fwkT", [DM, DM]),
               wvT=T("fwvT", [DM, DM]), mur=T("mur", [128, CC]),
               muk=T("muk", [128, CC]))
    s3w = dict(wqkvT=T("wqkvT", [DM, 3 * DM]))
    s4w = dict(wcat1=T("wcat1", [17, DM, 2048]), wcat2=T("wcat2", [17, 2048, DM]),
               b1cat=T("b1cat", [128, 17 * 16]), sb2b=T("sb2b", [128, DM]),
               rb2=T("rb2", [15, DM]), gw=T("gw", [DM, 15]),
               gbb=T("gbb", [128, 15]))

    bld = Build()
    ges = contextlib.ExitStack()
    with ges:
        for b in range(BPC):
            _emit_stage1(nc, ges, bld,
                         dict(s1w, **com, x_in=x_in[b], x_out=y1[b]))
        _emit_stage2(nc, ges, bld, dict(s2w, **com, x_in=y1, x_out=y2))
        _emit_stage3(nc, ges, bld, dict(s3w, **com, x_in=y2, x_out=y3))
        for b in range(BPC):
            _emit_stage4(nc, ges, bld,
                         dict(s4w, **com, x_in=y3[b], x_out=x_out[b]))
    _cache["mega"] = nc0
    return nc0


def mega_inputs(inputs):
    s1c = stage1_inputs(inputs["L0_attn_Wq"], inputs["L0_attn_Wk"],
                        inputs["L0_attn_Wv"], inputs["L0_attn_Wo"],
                        inputs["L0_attn_Wck"], inputs["L0_attn_bck"],
                        inputs["L0_attn_Wcv"], inputs["L0_attn_bcv"],
                        inputs["L0_attn_Wg"], inputs["L0_attn_bg"])
    s4c = stage4_inputs(inputs["L1_ffn_sW1"], inputs["L1_ffn_sb1"],
                        inputs["L1_ffn_sW2"], inputs["L1_ffn_sb2"],
                        inputs["L1_ffn_rW1"], inputs["L1_ffn_rb1"],
                        inputs["L1_ffn_rW2"], inputs["L1_ffn_rb2"],
                        inputs["L1_ffn_gW"], inputs["L1_ffn_gb"])
    com = dict(s1c)
    com.update(s4c)
    com.update(fwrT=_wT(inputs["L0_ffn_W_r"]), fwkT=_wT(inputs["L0_ffn_W_k"]),
               fwvT=_wT(inputs["L0_ffn_W_v"]),
               mur=_fmvec(inputs["L0_ffn_mu_r"]),
               muk=_fmvec(inputs["L0_ffn_mu_k"]),
               wqkvT=_wT(inputs["L1_attn_W_qkv"]))
    return com


def kernel(**inputs):
    x = _c(inputs["L0_attn_x"])
    nc = build_mega()
    com = mega_inputs(inputs)
    in_maps = [dict(com, x_in=_c(x[c * BPC:(c + 1) * BPC]))
               for c in range(NCORES)]
    res = run_bass_kernel_spmd(nc, in_maps, core_ids=list(range(NCORES)))
    return np.concatenate([np.asarray(r["x_out"]) for r in res.results],
                          axis=0)


def kernel_unfused(**inputs):
    x = _c(inputs["L0_attn_x"])
    cores = list(range(NCORES))
    ones_c = np.ones((128, 1), np.float32)
    ones_r = np.ones((1, 128), np.float32)

    nc1 = build_stage1()
    s1c = stage1_inputs(inputs["L0_attn_Wq"], inputs["L0_attn_Wk"],
                        inputs["L0_attn_Wv"], inputs["L0_attn_Wo"],
                        inputs["L0_attn_Wck"], inputs["L0_attn_bck"],
                        inputs["L0_attn_Wcv"], inputs["L0_attn_bcv"],
                        inputs["L0_attn_Wg"], inputs["L0_attn_bg"])
    y = np.empty((B, S, DM), np.float32)
    for lp in range(2):
        in_maps = [dict(s1c, x_in=_c(x[2 * c + lp])) for c in range(NCORES)]
        res = run_bass_kernel_spmd(nc1, in_maps, core_ids=cores)
        for c in range(NCORES):
            y[2 * c + lp] = np.asarray(res.results[c]["x_out"])

    nc2 = build_stage2()
    com2 = stage2_inputs(None, inputs["L0_ffn_W_r"], inputs["L0_ffn_W_k"],
                         inputs["L0_ffn_W_v"], inputs["L0_ffn_mu_r"],
                         inputs["L0_ffn_mu_k"])
    del com2["x_in"]
    in_maps = [dict(com2, x_in=_c(y[c * BPC:(c + 1) * BPC]))
               for c in range(NCORES)]
    res = run_bass_kernel_spmd(nc2, in_maps, core_ids=cores)
    y = np.concatenate([np.asarray(r["x_out"]) for r in res.results], axis=0)

    nc3 = build_stage3()
    com3 = dict(wqkvT=_wT(inputs["L1_attn_W_qkv"]), ident=_ident(),
                onescol=ones_c, onesrow=ones_r)
    in_maps = [dict(com3, x_in=_c(y[c * BPC:(c + 1) * BPC]))
               for c in range(NCORES)]
    res = run_bass_kernel_spmd(nc3, in_maps, core_ids=cores)
    y = np.concatenate([np.asarray(r["x_out"]) for r in res.results], axis=0)

    nc4 = build_stage4()
    s4c = stage4_inputs(inputs["L1_ffn_sW1"], inputs["L1_ffn_sb1"],
                        inputs["L1_ffn_sW2"], inputs["L1_ffn_sb2"],
                        inputs["L1_ffn_rW1"], inputs["L1_ffn_rb1"],
                        inputs["L1_ffn_rW2"], inputs["L1_ffn_rb2"],
                        inputs["L1_ffn_gW"], inputs["L1_ffn_gb"])
    out = np.empty((B, S, DM), np.float32)
    for lp in range(2):
        in_maps = [dict(s4c, x_in=_c(y[2 * c + lp])) for c in range(NCORES)]
        res = run_bass_kernel_spmd(nc4, in_maps, core_ids=cores)
        for c in range(NCORES):
            out[2 * c + lp] = np.asarray(res.results[c]["x_out"])
    return out
